# version 13
# baseline (speedup 1.0000x reference)
import numpy as np
import ml_dtypes
import concourse.bass as bass
import concourse.mybir as mybir
from concourse.bass_utils import run_bass_kernel_spmd
from contextlib import ExitStack

F32 = mybir.dt.float32
BF16 = mybir.dt.bfloat16
AF = mybir.ActivationFunctionType
ALU = mybir.AluOpType
AX_ = mybir.AxisListType.X
BFNP = ml_dtypes.bfloat16

D = 2048
DG = 16
FFDIM = 4096
FG = 32
NH = 8
RC = float(1.5 * 2 ** 23)
EPS_LN = 1e-5
EPS_Q = 1e-5
QB = 127.0
SQ_D = float(np.sqrt(2048.0))

ENGS = ("sync", "scalar", "vector", "tensor", "gpsimd")
ESEM = {"scalar": "S", "vector": "V", "tensor": "T", "gpsimd": "G"}


class Em:

    def __init__(self, nc):
        self.nc = nc
        self.sems = {}
        self.cnt = {}
        self.active = None
        self.engines = {"sync": nc.sync, "scalar": nc.scalar,
                        "vector": nc.vector, "tensor": nc.tensor,
                        "gpsimd": nc.gpsimd}

    def sem(self, name, ctx):
        self.sems[name] = ctx.enter_context(self.nc.semaphore(name))
        self.cnt[name] = 0

    def e(self, eng, f, extra_sem=None):
        s = ESEM[eng]
        self.cnt[s] += 1
        if extra_sem is not None:
            self.cnt[extra_sem] += 1
        if self.active == eng:
            inst = f(self.engines[eng])
            inst.then_inc(self.sems[s], 1)
            if extra_sem is not None:
                inst.then_inc(self.sems[extra_sem], 1)
        return self.cnt[s]

    def dma(self, eng, out, in_, sem):
        self.cnt[sem] += 16
        if self.active == eng:
            self.engines[eng].dma_start(out=out, in_=in_).then_inc(
                self.sems[sem], 16)
        return self.cnt[sem]

    def w(self, eng, sem, at=None):
        tgt = self.cnt[sem] if at is None else at
        if tgt <= 0:
            return
        if self.active == eng:
            self.engines[eng].wait_ge(self.sems[sem], tgt)

    def mark(self, sem):
        return self.cnt[sem]

    def barrier(self, sems=None):
        if sems is None:
            sems = list(self.sems.keys())
        for e in ENGS:
            for s in sems:
                self.w(e, s)

    def run(self, block, fn):
        for name in ENGS:
            for k in self.cnt:
                self.cnt[k] = 0
            self.active = name
            deco = getattr(block, name)

            def body(eng, _n=name):
                fn(self)
            deco(body)


def build_program(L_all, L_own, TCA, TCB, TCC, n_half=None, debug=()):
    nc = bass.Bass("TRN2", target_bir_lowering=False)
    nch_a = L_own // TCA
    nch_b = L_all // TCB
    nch_c = L_own // TCC
    multi = n_half is None

    def din(name, shape, dt=F32):
        return nc.dram_tensor(name, shape, dt, kind="ExternalInput").ap()

    x_fm = din("x_fm", [D, L_all])
    xprojT = din("xprojT", [D, 144])
    dtprojT = din("dtprojT", [128, D])
    dtbias_c = din("dtbias_c", [128, DG])
    A_cols = din("A_cols", [128, DG * NH])
    D_cols = din("D_cols", [128, DG])
    WoutT = din("WoutT", [D, D])
    obias_c = din("obias_c", [128, DG])
    wupT = din("wupT", [D, FFDIM])
    wdownT = din("wdownT", [FFDIM, D])
    H128 = din("H128", [128, 128])
    ident = din("ident", [128, 128])
    ones_r = din("ones_r", [1, 128])

    out_fm = nc.dram_tensor("out_fm", [D, L_own], F32,
                            kind="ExternalOutput").ap()

    def scratch(name, shape, dt=BF16, dbg_key=None, **kw):
        kind = "ExternalOutput" if (dbg_key and dbg_key in debug) \
            else "Internal"
        return nc.dram_tensor(name, shape, dt, kind=kind, **kw).ap()

    tupQ = scratch("tupQ", [128, FG * D], dbg_key="tupq")
    tdownQ = scratch("tdownQ", [128, DG * FFDIM], dbg_key="tdnq")
    WoutTb = scratch("WoutTb", [128, DG * D])
    g_all = scratch("g_all", [D, L_own], dbg_key="gall")
    xnb_own = scratch("xnb_own", [D, L_own], dbg_key="xnball")
    y_all = scratch("y_all", [D, L_all], dbg_key="yall")
    dsink = nc.dram_tensor("dsink", [D, 512], BF16, kind="Internal").ap()
    bcscr = nc.dram_tensor("bcscr", [1, 8192], BF16, kind="Internal").ap()
    if multi:
        ysh = nc.dram_tensor("ysh", [D, 2 * L_own], BF16, kind="Internal",
                             addr_space="Shared").ap()
        bar_i = nc.dram_tensor("bar_i", [1, 16], F32, kind="Internal").ap()
        bar_o = nc.dram_tensor("bar_o", [1, 16], F32, kind="Internal",
                               addr_space="Local").ap()
    else:
        yprt = din("yprt", [D, L_own], BF16)

    dbg_specs = {
        "xn": ([D, TCA], F32), "y2": ([D, TCA], F32), "q1": ([D, TCA], BF16),
        "r2": ([FFDIM, TCA], F32), "q2": ([FFDIM, TCA], BF16),
        "delta": ([D, TCB], BF16), "u": ([D, TCB], BF16),
        "bc": ([16, TCB], BF16), "s2": ([D, TCC], BF16),
    }
    dbg = {k: nc.dram_tensor("dbg_" + k, *dbg_specs[k],
                             kind="ExternalOutput").ap()
           for k in debug if k in dbg_specs}

    ctx = ExitStack()
    with ctx:
        em = Em(nc)
        for s in ["V", "S", "T", "G",
                  "ldx", "ldx2", "ldc", "lwa", "lwb", "lwc", "lwd",
                  "stg", "stw", "stx", "sty", "sty2", "styp", "sto", "std",
                  "stb", "ldb"]:
            em.sem(s, ctx)

        def sb(name, shape, dt=F32):
            return ctx.enter_context(nc.sbuf_tensor(name, shape, dt))

        def psum(name, shape):
            return ctx.enter_context(nc.psum_tensor(name, shape, F32))

        xprojTb = sb("xprojTb", [128, DG * 144], BF16)
        dtprojTb = sb("dtprojTb", [128, D], BF16)
        dtbias_s = sb("dtbias_s", [128, DG])
        A_s = sb("A_s", [128, DG * NH])
        D_s = sb("D_s", [128, DG])
        obias_s = sb("obias_s", [128, DG])
        H128f = sb("H128f", [128, 128])
        identf = sb("identf", [128, 128])
        onesf_r = sb("onesf_r", [1, 128])
        onesb_r = sb("onesb_r", [1, 128], BF16)
        identb = sb("identb", [128, 128], BF16)
        onesf_c = sb("onesf_c", [128, 1])
        onesb_c_ = sb("onesb_c", [128, 1], BF16)
        supc = sb("supc", [128, 1])
        sdnc = sb("sdnc", [128, 1])
        cst = sb("cst", [128, 4])
        am1 = sb("am1", [128, 512])
        am2 = sb("am2", [128, 512])
        am3 = sb("am3", [128, 512])
        mxa = sb("mxa", [128, 512])
        mxb = sb("mxb", [128, 512])
        mca = sb("mca", [128, 16])
        acol1 = sb("acol1", [128, 8])
        bcol1 = sb("bcol1", [128, 8])
        rows_q = sb("rows_q", [1, 1024])
        rows_ln = sb("rows_ln", [1, 1024])
        rowtmp = sb("rowtmp", [1, 64])
        coltmp = sb("coltmp", [128, 4])
        statec = sb("statec", [128, 128])
        bcs16 = sb("bcs16", [16, 512], BF16)

        SLA = sb("SLA", [128, 8192])
        SLB = sb("SLB", [128, 8192])
        SLR = sb("SLR", [128, 16384])
        SLQ = sb("SLQ", [128, 4096])
        SLG = sb("SLG", [128, 2048])
        SLW = sb("SLW", [128, 6144])

        ps_mm = psum("ps_mm", [128, 512])
        ps_mm2 = psum("ps_mm2", [128, 512])
        ps_bc = psum("ps_bc", [128, 1024])
        ps_t = psum("ps_t", [128, 512])
        ps_y = psum("ps_y", [128, 512])
        ps_y2 = psum("ps_y2", [128, 512])
        ps_x = psum("ps_x", [128, 512])
        ps_r = ps_bc

        def fm_store(dst_ap, arena, ngrp, T, sem, eng="sync"):
            return em.dma(
                eng, dst_ap.rearrange("(g p) t -> p g t", p=128),
                arena[:, 0:ngrp * T].rearrange("p (g t) -> p g t", g=ngrp),
                sem)

        def fm_load(arena, src_ap, ngrp, T, sem, eng="sync"):
            return em.dma(
                eng, arena[:, 0:ngrp * T].rearrange("p (g t) -> p g t",
                                                    g=ngrp),
                src_ap.rearrange("(g p) t -> p g t", p=128), sem)

        def p0(em):
            mv = 0
            for dg in range(DG):
                em.w("sync", "V", mv)
                ld = em.dma("sync", SLA[:, 0:144],
                            xprojT[dg * 128:(dg + 1) * 128, :], "ldc")
                em.w("vector", "ldc", ld)
                mv = em.e("vector", lambda e, dg=dg: e.tensor_copy(
                    xprojTb[:, dg * 144:(dg + 1) * 144], SLA[:, 0:144]))
            em.w("sync", "V", mv)
            ld = em.dma("sync", SLA[:, 0:D], dtprojT, "ldc")
            em.w("vector", "ldc", ld)
            mv = em.e("vector", lambda e: e.tensor_copy(dtprojTb[:],
                                                        SLA[:, 0:D]))
            em.w("sync", "V", mv)
            for dst, src in [(dtbias_s, dtbias_c), (D_s, D_cols),
                             (obias_s, obias_c), (H128f, H128),
                             (identf, ident)]:
                em.dma("sync", dst[:], src, "ldc")
            em.dma("sync", onesf_r[:], ones_r, "ldc")
            ld = em.dma("sync", SLA[:, 0:DG * NH], A_cols, "ldc")
            em.w("vector", "ldc", ld)
            em.w("scalar", "ldc", ld)
            em.e("vector", lambda e: e.tensor_copy(onesb_r[:], onesf_r[:]))
            em.e("vector", lambda e: e.tensor_copy(identb[:], identf[:]))
            em.e("vector", lambda e: e.memset(onesf_c[:], 1.0))
            em.e("vector", lambda e: e.memset(onesb_c_[:], 1.0))
            em.e("vector", lambda e: e.memset(cst[:, 0:1], 1.0))
            em.e("vector", lambda e: e.memset(cst[:, 1:2], EPS_LN))
            em.e("vector", lambda e: e.memset(cst[:, 2:3], -0.5))
            ms = em.e("scalar", lambda e: e.activation(
                A_s[:], SLA[:, 0:DG * NH], AF.Exp))
            em.w("vector", "S", ms)
            em.e("vector", lambda e: e.tensor_scalar(
                A_s[:], A_s[:], -1.0, None, ALU.mult))
            if multi:
                mv = em.e("vector", lambda e: e.memset(rowtmp[0:1, 0:16],
                                                       0.0))
                em.w("sync", "V", mv)
                em.dma("sync", bar_i, rowtmp[0:1, 0:16], "stb")

            wst = SLA[:, 0:4096]
            wstb = SLQ.bitcast(BF16)

            def quant_weight(src, dst, rows, cols, sup_col):
                nrow = rows // 128
                mv = em.mark("V")
                for r in range(nrow):
                    em.w("sync", "V", mv)
                    ld = em.dma("sync", wst[:, 0:cols],
                                src[r * 128:(r + 1) * 128, :], "ldc")
                    em.w("vector", "ldc", ld)
                    c01 = coltmp[:, 0:1] if r == 0 else coltmp[:, 1:2]
                    em.e("vector", lambda e, c01=c01, cols=cols:
                         e.tensor_reduce(c01, wst[:, 0:cols], AX_, ALU.add,
                                         apply_absolute_value=True))
                    em.w("vector", "V")
                    if r > 0:
                        em.e("vector", lambda e: e.tensor_tensor(
                            coltmp[:, 0:1], coltmp[:, 0:1], coltmp[:, 1:2],
                            ALU.add))
                        em.w("vector", "V")
                    mv = em.mark("V")
                em.w("tensor", "V", mv)
                mt = em.e("tensor", lambda e: e.matmul(
                    ps_t[0:1, 0:1], onesf_c[:], coltmp[:, 0:1],
                    start=True, stop=True))
                em.w("vector", "T", mt)
                em.w("vector", "stb")
                em.e("vector", lambda e, rows=rows, cols=cols:
                     e.tensor_scalar(
                         rowtmp[0:1, 0:1], ps_t[0:1, 0:1],
                         1.0 / (rows * cols), EPS_Q, ALU.mult, ALU.max))
                em.w("vector", "V")
                mv = em.e("vector", lambda e: e.reciprocal(
                    rowtmp[0:1, 1:2], rowtmp[0:1, 0:1]))
                em.w("tensor", "V", mv)
                mt = em.e("tensor", lambda e: e.matmul(
                    ps_y[:, 0:2], onesf_r[0:1, :], rowtmp[0:1, 0:2],
                    start=True, stop=True))
                em.w("vector", "T", mt)
                em.e("vector", lambda e: e.tensor_copy(sup_col[:],
                                                       ps_y[:, 0:1]))
                mv = em.e("vector", lambda e: e.tensor_copy(coltmp[:, 2:3],
                                                            ps_y[:, 1:2]))
                for r in range(nrow):
                    em.w("sync", "V", mv)
                    em.w("sync", "stw")
                    ld = em.dma("sync", wst[:, 0:cols],
                                src[r * 128:(r + 1) * 128, :], "ldc")
                    em.w("vector", "ldc", ld)
                    em.w("vector", "stw")
                    em.e("vector", lambda e, cols=cols: e.tensor_scalar(
                        wst[:, 0:cols], wst[:, 0:cols], coltmp[:, 2:3],
                        None, ALU.mult))
                    em.w("vector", "V")
                    em.e("vector", lambda e, cols=cols: e.tensor_scalar(
                        wst[:, 0:cols], wst[:, 0:cols], RC, RC, ALU.add,
                        ALU.subtract))
                    em.w("vector", "V")
                    mv = em.e("vector", lambda e, cols=cols: e.tensor_scalar(
                        wstb[:, 0:cols], wst[:, 0:cols], 1.0, -1.0,
                        ALU.min, ALU.max))
                    em.w("sync", "V", mv)
                    nblk = cols // 128
                    em.dma("sync",
                           dst.rearrange("p (b rest) -> p b rest", b=nblk)[
                               :, :, r * 128:(r + 1) * 128],
                           wstb[:, 0:cols].rearrange("p (b t) -> p b t",
                                                     b=nblk), "stw")

            quant_weight(wupT, tupQ, D, FFDIM, supc)
            quant_weight(wdownT, tdownQ, FFDIM, D, sdnc)
            mv = em.mark("V")
            for r in range(DG):
                em.w("sync", "V", mv)
                em.w("sync", "stw")
                ld = em.dma("sync", wst[:, 0:D],
                            WoutT[r * 128:(r + 1) * 128, :], "ldc")
                em.w("vector", "ldc", ld)
                em.w("vector", "stw")
                mv = em.e("vector", lambda e: e.tensor_copy(wstb[:, 0:D],
                                                            wst[:, 0:D]))
                em.w("sync", "V", mv)
                em.dma("sync",
                       WoutTb.rearrange("p (b rest) -> p b rest", b=DG)[
                           :, :, r * 128:(r + 1) * 128],
                       wstb[:, 0:D].rearrange("p (b t) -> p b t", b=DG),
                       "stw")
            em.barrier()

        def emit_ln_rows(em, AXa, T, ldmark, ldsem):
            xbv = am3.bitcast(BF16)
            em.w("vector", ldsem, ldmark)
            em.w("scalar", ldsem, ldmark)
            em.w("scalar", "S")
            em.w("scalar", "V")
            em.w("vector", "V")
            sq = []
            for dg in range(DG):
                amx = (am1 if dg % 2 == 0 else am2).bitcast(BF16)
                xbs = xbv[:, (dg % 2) * 512:(dg % 2) * 512 + T]
                if dg >= 2:
                    em.w("vector", "T", sq[dg - 2][0])
                mvx = em.e("vector", lambda e, dg=dg, T=T, xbs=xbs:
                           e.tensor_copy(xbs, AXa[:, dg * T:(dg + 1) * T]))
                if dg >= 2:
                    em.w("scalar", "T", sq[dg - 2][1])
                ms = em.e("scalar", lambda e, dg=dg, amx=amx, T=T:
                          e.activation(amx[:, 0:T],
                                       AXa[:, dg * T:(dg + 1) * T],
                                       AF.Square))
                em.w("tensor", "V", mvx)
                m1 = em.e("tensor", lambda e, dg=dg, T=T, xbs=xbs: e.matmul(
                    ps_r[0:1, 0:T], onesb_c_[:], xbs,
                    start=(dg == 0), stop=(dg == DG - 1)))
                em.w("tensor", "S", ms)
                m2 = em.e("tensor", lambda e, amx=amx, dg=dg, T=T:
                          e.matmul(ps_r[0:1, 512:512 + T], onesb_c_[:],
                                   amx[:, 0:T], start=(dg == 0),
                                   stop=(dg == DG - 1)))
                sq.append((m1, m2))
            em.w("vector", "T", sq[-1][1])
            em.e("vector", lambda e, T=T: e.tensor_scalar(
                rows_ln[0:1, 0:T], ps_r[0:1, 0:T], 1.0 / D, None, ALU.mult))
            em.e("vector", lambda e, T=T: e.tensor_scalar(
                rows_ln[0:1, 512:512 + T], ps_r[0:1, 512:512 + T], 1.0 / D,
                None, ALU.mult))
            em.w("vector", "V")
            em.e("vector", lambda e, T=T: e.tensor_tensor(
                am1[0:1, 0:T], rows_ln[0:1, 0:T], rows_ln[0:1, 0:T],
                ALU.mult))
            em.w("vector", "V")
            mv = em.e("vector", lambda e, T=T: e.tensor_tensor(
                rows_ln[0:1, 512:512 + T], rows_ln[0:1, 512:512 + T],
                am1[0:1, 0:T], ALU.subtract))
            em.w("scalar", "V", mv)
            em.e("scalar", lambda e, T=T: e.activation(
                rows_ln[0:1, 512:512 + T], rows_ln[0:1, 512:512 + T], AF.Ln,
                bias=cst[0:1, 1:2]))
            em.w("scalar", "S")
            ms = em.e("scalar", lambda e, T=T: e.activation(
                rows_ln[0:1, 512:512 + T], rows_ln[0:1, 512:512 + T], AF.Exp,
                scale=cst[0:1, 2:3]))
            em.w("vector", "S", ms)
            mv = em.e("vector", lambda e, T=T: e.scalar_tensor_tensor(
                rows_ln[0:1, 0:T], rows_ln[0:1, 0:T], -1.0,
                rows_ln[0:1, 512:512 + T], ALU.mult, ALU.mult))
            em.w("tensor", "V", mv)
            em.w("tensor", "S", ms)
            em.e("tensor", lambda e, T=T: e.matmul(
                ps_bc[:, 0:T], onesf_r[0:1, :], rows_ln[0:1, 512:512 + T],
                start=True, stop=True))
            mt = em.e("tensor", lambda e, T=T: e.matmul(
                ps_bc[:, 512:512 + T], onesf_r[0:1, :], rows_ln[0:1, 0:T],
                start=True, stop=True))
            em.w("vector", "T", mt)
            return mt

        def emit_ln(em, AXa, ANa, T, ldmark, ldsem):
            xbv = am3.bitcast(BF16)
            em.w("vector", ldsem, ldmark)
            em.w("scalar", ldsem, ldmark)
            em.w("scalar", "S")
            em.w("scalar", "V")
            em.w("vector", "V")
            sq = []
            for dg in range(DG):
                amx = (am1 if dg % 2 == 0 else am2).bitcast(BF16)
                xbs = xbv[:, (dg % 2) * 512:(dg % 2) * 512 + T]
                if dg >= 2:
                    em.w("vector", "T", sq[dg - 2][0])
                mvx = em.e("vector", lambda e, dg=dg, T=T, xbs=xbs:
                           e.tensor_copy(xbs, AXa[:, dg * T:(dg + 1) * T]))
                if dg >= 2:
                    em.w("scalar", "T", sq[dg - 2][1])
                ms = em.e("scalar", lambda e, dg=dg, amx=amx, T=T:
                          e.activation(amx[:, 0:T],
                                       AXa[:, dg * T:(dg + 1) * T],
                                       AF.Square))
                em.w("tensor", "V", mvx)
                m1 = em.e("tensor", lambda e, dg=dg, T=T, xbs=xbs: e.matmul(
                    ps_r[0:1, 0:T], onesb_c_[:], xbs,
                    start=(dg == 0), stop=(dg == DG - 1)))
                em.w("tensor", "S", ms)
                m2 = em.e("tensor", lambda e, amx=amx, dg=dg, T=T:
                          e.matmul(ps_r[0:1, 512:512 + T], onesb_c_[:],
                                   amx[:, 0:T], start=(dg == 0),
                                   stop=(dg == DG - 1)))
                sq.append((m1, m2))
            em.w("vector", "T", sq[-1][1])
            em.e("vector", lambda e, T=T: e.tensor_scalar(
                rows_ln[0:1, 0:T], ps_r[0:1, 0:T], 1.0 / D, None, ALU.mult))
            em.e("vector", lambda e, T=T: e.tensor_scalar(
                rows_ln[0:1, 512:512 + T], ps_r[0:1, 512:512 + T], 1.0 / D,
                None, ALU.mult))
            em.w("vector", "V")
            em.e("vector", lambda e, T=T: e.tensor_tensor(
                am1[0:1, 0:T], rows_ln[0:1, 0:T], rows_ln[0:1, 0:T],
                ALU.mult))
            em.w("vector", "V")
            mv = em.e("vector", lambda e, T=T: e.tensor_tensor(
                rows_ln[0:1, 512:512 + T], rows_ln[0:1, 512:512 + T],
                am1[0:1, 0:T], ALU.subtract))
            em.w("scalar", "V", mv)
            em.e("scalar", lambda e, T=T: e.activation(
                rows_ln[0:1, 512:512 + T], rows_ln[0:1, 512:512 + T], AF.Ln,
                bias=cst[0:1, 1:2]))
            em.w("scalar", "S")
            ms = em.e("scalar", lambda e, T=T: e.activation(
                rows_ln[0:1, 512:512 + T], rows_ln[0:1, 512:512 + T], AF.Exp,
                scale=cst[0:1, 2:3]))
            em.w("vector", "S", ms)
            mv = em.e("vector", lambda e, T=T: e.scalar_tensor_tensor(
                rows_ln[0:1, 0:T], rows_ln[0:1, 0:T], -1.0,
                rows_ln[0:1, 512:512 + T], ALU.mult, ALU.mult))
            em.w("tensor", "V", mv)
            em.w("tensor", "S", ms)
            em.e("tensor", lambda e, T=T: e.matmul(
                ps_bc[:, 0:T], onesf_r[0:1, :], rows_ln[0:1, 512:512 + T],
                start=True, stop=True))
            mt = em.e("tensor", lambda e, T=T: e.matmul(
                ps_bc[:, 512:512 + T], onesf_r[0:1, :], rows_ln[0:1, 0:T],
                start=True, stop=True))
            em.w("vector", "T", mt)
            for dg in range(DG):
                em.e("vector", lambda e, dg=dg, T=T: e.tensor_tensor(
                    ANa[:, dg * T:(dg + 1) * T], AXa[:, dg * T:(dg + 1) * T],
                    ps_bc[:, 0:T], ALU.mult))
            em.w("vector", "V")
            for dg in range(DG):
                em.e("vector", lambda e, dg=dg, T=T: e.tensor_tensor(
                    ANa[:, dg * T:(dg + 1) * T], ANa[:, dg * T:(dg + 1) * T],
                    ps_bc[:, 512:512 + T], ALU.add))
            return em.mark("V")

        def emit_bfly(em, arena, ntile, T, steps=None):
            nbits = ntile.bit_length() - 1
            if steps is None:
                steps = [1 << b for b in range(nbits)]
            for step in steps:
                em.w("vector", "V")
                for c0 in range(ntile):
                    if c0 & step:
                        continue
                    c1 = c0 + step
                    em.e("vector", lambda e, c0=c0, c1=c1, T=T:
                         e.tensor_tensor(
                             arena[:, c0 * T:(c0 + 1) * T],
                             arena[:, c0 * T:(c0 + 1) * T],
                             arena[:, c1 * T:(c1 + 1) * T], ALU.add))
                em.w("vector", "V")
                for c0 in range(ntile):
                    if c0 & step:
                        continue
                    c1 = c0 + step
                    em.e("vector", lambda e, c0=c0, c1=c1, T=T:
                         e.scalar_tensor_tensor(
                             arena[:, c1 * T:(c1 + 1) * T],
                             arena[:, c1 * T:(c1 + 1) * T], -2.0,
                             arena[:, c0 * T:(c0 + 1) * T], ALU.mult,
                             ALU.add))
            return em.mark("V")

        def emit_h128(em, arena, ntile, T, pp0=None, pp1=None):
            pp0 = pp0 if pp0 is not None else ps_mm
            pp1 = pp1 if pp1 is not None else ps_mm2
            em.w("tensor", "V")
            em.w("tensor", "S")
            evs = []
            for c in range(ntile):
                pp = pp0 if c % 2 == 0 else pp1
                if c >= 2:
                    em.w("tensor", "S", evs[c - 2])
                mt = em.e("tensor", lambda e, c=c, pp=pp, T=T: e.matmul(
                    pp[:, 0:T], H128f[:], arena[:, c * T:(c + 1) * T],
                    start=True, stop=True))
                em.w("scalar", "T", mt)
                evs.append(em.e("scalar", lambda e, c=c, pp=pp, T=T: e.copy(
                    arena[:, c * T:(c + 1) * T], pp[:, 0:T])))
            return evs[-1]

        def emit_colbcast(em, colsrc, c0, nb, vmark, pst=None, psc=None,
                          rq0=0):
            pst = pst if pst is not None else ps_t
            psc = psc if psc is not None else ps_bc
            em.w("tensor", "V", vmark)
            mt = 0
            for b in range(nb):
                mt = em.e("tensor", lambda e, b=b, pst=pst: e.transpose(
                    pst[0:1, b * 128:(b + 1) * 128],
                    colsrc[:, c0 + b:c0 + b + 1], identf[:]))
            em.w("vector", "T", mt)
            mv = em.e("vector", lambda e, nb=nb, pst=pst, rq0=rq0:
                      e.tensor_copy(rows_q[0:1, rq0:rq0 + nb * 128],
                                    pst[0:1, 0:nb * 128]))
            em.w("tensor", "V", mv)
            for b in range(nb):
                mt = em.e("tensor", lambda e, b=b, psc=psc, rq0=rq0:
                          e.matmul(
                              psc[:, b * 128:(b + 1) * 128], onesf_r[0:1, :],
                              rows_q[0:1, rq0 + b * 128:rq0 + (b + 1) * 128],
                              start=True, stop=True))
            return mt

        def emit_absmax(em, arena, ntile, T, clipmin, smark, pst=None,
                        psc=None, mcol=0, mx0=None, mx1=None, rq0=0):
            pst = pst if pst is not None else ps_t
            psc = psc if psc is not None else ps_bc
            mx0 = mx0 if mx0 is not None else mxa
            mx1 = mx1 if mx1 is not None else mxb
            nb = T // 128
            em.w("vector", "S", smark)
            em.e("vector", lambda e, T=T: e.tensor_tensor(
                mx0[:, 0:T], arena[:, 0:T], arena[:, T:2 * T], ALU.max))
            em.e("vector", lambda e, T=T: e.tensor_tensor(
                mx1[:, 0:T], arena[:, 0:T], arena[:, T:2 * T], ALU.min))
            for c in range(2, ntile):
                em.w("vector", "V")
                em.e("vector", lambda e, c=c, T=T: e.tensor_tensor(
                    mx0[:, 0:T], mx0[:, 0:T], arena[:, c * T:(c + 1) * T],
                    ALU.max))
                em.e("vector", lambda e, c=c, T=T: e.tensor_tensor(
                    mx1[:, 0:T], mx1[:, 0:T], arena[:, c * T:(c + 1) * T],
                    ALU.min))
            em.w("vector", "V")
            mv = em.e("vector", lambda e, T=T: e.scalar_tensor_tensor(
                mx0[:, 0:T], mx1[:, 0:T], -1.0, mx0[:, 0:T], ALU.mult,
                ALU.max))
            em.w("tensor", "V", mv)
            mt = 0
            for b in range(nb):
                mt = em.e("tensor", lambda e, b=b, pst=pst: e.transpose(
                    pst[:, b * 128:(b + 1) * 128],
                    mx0[:, b * 128:(b + 1) * 128], identf[:]))
            em.w("vector", "T", mt)
            for b in range(nb):
                em.e("vector", lambda e, b=b, pst=pst, mcol=mcol:
                     e.tensor_reduce(
                         mca[:, mcol + b:mcol + b + 1],
                         pst[:, b * 128:(b + 1) * 128], AX_,
                         ALU.max, apply_absolute_value=True))
            em.w("vector", "V")
            em.e("vector", lambda e, nb=nb, clipmin=clipmin, mcol=mcol:
                 e.tensor_scalar(mca[:, mcol:mcol + nb],
                                 mca[:, mcol:mcol + nb], clipmin, None,
                                 ALU.max))
            em.w("vector", "V")
            em.e("vector", lambda e, nb=nb, mcol=mcol: e.reciprocal(
                mca[:, 8 + mcol:8 + mcol + nb], mca[:, mcol:mcol + nb]))
            em.w("vector", "V")
            mv = em.e("vector", lambda e, nb=nb, mcol=mcol: e.tensor_scalar(
                mca[:, 8 + mcol:8 + mcol + nb],
                mca[:, 8 + mcol:8 + mcol + nb], QB, None, ALU.mult))
            return emit_colbcast(em, mca, 8 + mcol, nb, mv, pst=pst, psc=psc,
                                 rq0=rq0)

        def emit_h128tree(em, arena, ntile, T, pp0, pp1, mx0, mx1):
            em.w("tensor", "V")
            em.w("tensor", "S")
            evs = []
            for c in range(ntile):
                pp = pp0 if c % 2 == 0 else pp1
                if c >= 2:
                    em.w("tensor", "S", evs[c - 2])
                mt = em.e("tensor", lambda e, c=c, pp=pp, T=T: e.matmul(
                    pp[:, 0:T], H128f[:], arena[:, c * T:(c + 1) * T],
                    start=True, stop=True))
                em.w("scalar", "T", mt)
                evs.append(em.e("scalar", lambda e, c=c, pp=pp, T=T: e.copy(
                    arena[:, c * T:(c + 1) * T], pp[:, 0:T])))
                if c >= 1:
                    em.w("vector", "S", evs[c])
                    em.w("vector", "V")
                    if c == 1:
                        em.e("vector", lambda e, T=T, mx0=mx0:
                             e.tensor_tensor(mx0[:, 0:T], arena[:, 0:T],
                                             arena[:, T:2 * T], ALU.max))
                        em.e("vector", lambda e, T=T, mx1=mx1:
                             e.tensor_tensor(mx1[:, 0:T], arena[:, 0:T],
                                             arena[:, T:2 * T], ALU.min))
                    else:
                        em.e("vector", lambda e, c=c, T=T, mx0=mx0:
                             e.tensor_tensor(
                                 mx0[:, 0:T], mx0[:, 0:T],
                                 arena[:, c * T:(c + 1) * T], ALU.max))
                        em.e("vector", lambda e, c=c, T=T, mx1=mx1:
                             e.tensor_tensor(
                                 mx1[:, 0:T], mx1[:, 0:T],
                                 arena[:, c * T:(c + 1) * T], ALU.min))
            return evs[-1]

        def emit_absmax_tail(em, T, clipmin, pst, psc, mcol, rq0, mx0, mx1):
            nb = T // 128
            em.w("vector", "V")
            mv = em.e("vector", lambda e, T=T, mx0=mx0, mx1=mx1:
                      e.scalar_tensor_tensor(
                          mx0[:, 0:T], mx1[:, 0:T], -1.0, mx0[:, 0:T],
                          ALU.mult, ALU.max))
            em.w("tensor", "V", mv)
            mt = 0
            for b in range(nb):
                mt = em.e("tensor", lambda e, b=b, pst=pst, mx0=mx0:
                          e.transpose(pst[:, b * 128:(b + 1) * 128],
                                      mx0[:, b * 128:(b + 1) * 128],
                                      identf[:]))
            em.w("vector", "T", mt)
            for b in range(nb):
                em.e("vector", lambda e, b=b, pst=pst, mcol=mcol:
                     e.tensor_reduce(
                         mca[:, mcol + b:mcol + b + 1],
                         pst[:, b * 128:(b + 1) * 128], AX_,
                         ALU.max, apply_absolute_value=True))
            em.w("vector", "V")
            em.e("vector", lambda e, nb=nb, clipmin=clipmin, mcol=mcol:
                 e.tensor_scalar(mca[:, mcol:mcol + nb],
                                 mca[:, mcol:mcol + nb], clipmin, None,
                                 ALU.max))
            em.w("vector", "V")
            em.e("vector", lambda e, nb=nb, mcol=mcol: e.reciprocal(
                mca[:, 8 + mcol:8 + mcol + nb], mca[:, mcol:mcol + nb]))
            em.w("vector", "V")
            mv = em.e("vector", lambda e, nb=nb, mcol=mcol: e.tensor_scalar(
                mca[:, 8 + mcol:8 + mcol + nb],
                mca[:, 8 + mcol:8 + mcol + nb], QB, None, ALU.mult))
            return emit_colbcast(em, mca, 8 + mcol, nb, mv, pst=pst, psc=psc,
                                 rq0=rq0)

        def emit_quant(em, arena, qdst, ntile, T, bcmark, psc=None,
                       qams=None):
            psc = psc if psc is not None else ps_bc
            qams = qams if qams is not None else (am1, am2)
            em.w("vector", "T", bcmark)
            em.w("vector", "V")
            clm = {}
            for c0 in range(0, ntile, 2):
                pair = [c0] + ([c0 + 1] if c0 + 1 < ntile else [])
                mm_ = {}
                rr_ = {}
                for c in pair:
                    if c - 2 in clm:
                        em.w("vector", "V", clm[c - 2])
                    mm_[c] = em.e(
                        "vector", lambda e, c=c, T=T, psc=psc,
                        q=qams[c0 % 2 if False else (c % 2)]:
                        e.tensor_tensor(
                            q[:, 0:T], arena[:, c * T:(c + 1) * T],
                            psc[:, 0:T], ALU.mult))
                for c in pair:
                    em.w("vector", "V", mm_[c])
                    rr_[c] = em.e(
                        "vector", lambda e, c=c, T=T, q=qams[c % 2]:
                        e.tensor_scalar(q[:, 0:T], q[:, 0:T], RC, RC,
                                        ALU.add, ALU.subtract))
                for c in pair:
                    em.w("vector", "V", rr_[c])
                    clm[c] = em.e(
                        "vector", lambda e, c=c, T=T, q=qams[c % 2]:
                        e.tensor_scalar(qdst[:, c * T:(c + 1) * T],
                                        q[:, 0:T], QB, -QB, ALU.min,
                                        ALU.max))
            return em.mark("V")

        def pa(em):
            T = TCA
            AXa = SLA
            ANa = SLB
            R2a = SLR
            R2v = SLR.bitcast(BF16)
            Qa = SLQ.bitcast(BF16)
            Q2a = R2v
            Ga = SLG.bitcast(BF16)
            wupb = SLW.bitcast(BF16)
            nb = T // 128
            dn = SLR[:, 8192:8192 + T]
            dng = SLR[:, 8192 + T:8192 + 2 * T]
            bbeta = SLR[:, 8192 + 2 * T:8192 + 3 * T]
            st = {"qm": 0, "qm2": 0, "lnap": 0, "bbm": 0, "gplast": 0,
                  "sillast": 0, "dncp": [0, 0], "rel2": 0, "ldm": 0,
                  "bcmt": 0}
            last_wup = [0, 0, 0, 0]
            last_wdn = [0, 0]
            x0 = 0 if multi else (n_half * L_own)

            seq = []
            for g in range(4):
                seq += [g, g + 16, g + 8, g + 24, g + 4, g + 20, g + 12,
                        g + 28]
            S_desc = (16, 8, 4, 2)
            GPLVL = 0
            lvl = {}
            sched = []
            for f in seq:
                lvl[f] = 0
                ops = []
                prog = True
                while prog:
                    prog = False
                    for li, s in enumerate(S_desc):
                        for c0 in range(FG):
                            c1 = c0 + s
                            if (c0 & s) == 0 and c1 < FG and \
                                    lvl.get(c0) == li and lvl.get(c1) == li:
                                ops.append((c0, c1, li))
                                lvl[c0] = li + 1
                                lvl[c1] = li + 1
                                prog = True
                sched.append(ops)
            assert all(v == len(S_desc) for v in lvl.values())

            def issue_ldx(em, k):
                t0 = k * T
                em.w("sync", "V", st["lnap"])
                if multi:
                    em.cnt["ldx"] += 16
                    if em.active == "sync":
                        for arm in em.engines["sync"].switch_core_id(n=128):
                            h = arm.logical % 2
                            o0 = h * L_own
                            em.engines["sync"].dma_start(
                                out=AXa[:, 0:DG * T].rearrange(
                                    "p (g t) -> p g t", g=DG),
                                in_=x_fm[:, o0 + t0:o0 + t0 + T].rearrange(
                                    "(g p) t -> p g t", p=128)).then_inc(
                                em.sems["ldx"], 16)
                    st["ldm"] = em.mark("ldx")
                else:
                    st["ldm"] = fm_load(AXa, x_fm[:, x0 + t0:x0 + t0 + T],
                                        DG, T, "ldx")

            def a_stats(em, k):
                st["bcmt"] = emit_ln_rows(em, AXa, T, st["ldm"], "ldx")

            def a_apply(em, k):
                em.w("vector", "T", st["bcmt"])
                for dg in range(DG):
                    em.e("vector", lambda e, dg=dg: e.tensor_tensor(
                        ANa[:, dg * T:(dg + 1) * T],
                        AXa[:, dg * T:(dg + 1) * T],
                        ps_bc[:, 0:T], ALU.mult))
                st["lnap"] = em.mark("V")
                em.w("vector", "V")
                for dg in range(DG):
                    em.e("vector", lambda e, dg=dg: e.tensor_tensor(
                        ANa[:, dg * T:(dg + 1) * T],
                        ANa[:, dg * T:(dg + 1) * T],
                        ps_bc[:, 512:512 + T], ALU.add))
                if "xn" in dbg and k == 0:
                    em.w("sync", "V")
                    fm_store(dbg["xn"], ANa, DG, T, "std")
                    em.barrier()

            def a_bfly(em, k):
                emit_bfly(em, ANa, DG, T)

            def a_h128(em, k):
                st["sm_a"] = emit_h128tree(em, ANa, DG, T, ps_t, ps_x,
                                           am3, am2)
                if "y2" in dbg and k == 0:
                    em.w("sync", "S")
                    fm_store(dbg["y2"], ANa, DG, T, "std")
                    em.barrier()

            def a_tail(em, k):
                em.w("vector", "G")
                bcm = emit_absmax_tail(em, T, EPS_Q * SQ_D, ps_t, ps_bc,
                                       0, 0, am3, am2)
                em.w("vector", "V")
                em.e("vector", lambda e: e.tensor_scalar(
                    acol1[:, 0:nb], mca[:, 0:nb], supc[:],
                    1.0 / (QB * SQ_D), ALU.mult, ALU.mult))
                st["qm"] = emit_quant(em, ANa, Qa, DG, T, bcm,
                                      qams=(mxb, am3))
                if "q1" in dbg and k == 0:
                    em.w("sync", "V")
                    fm_store(dbg["q1"], Qa, DG, T, "std")
                    em.barrier()

            def part_up(em, k):
                nxt = k + 1 < nch_a
                if nxt:
                    issue_ldx(em, k + 1)
                rel = []
                tl = {}
                tupv = tupQ.rearrange("p (g x) -> p g x", g=FG)
                for pos in range(FG):
                    fb = seq[pos]
                    j = pos % 4
                    wsl = wupb[:, j * 2048:(j + 1) * 2048]
                    if pos % 2 == 0:
                        assert seq[pos + 1] == fb + 16
                        q = (pos // 2) % 2
                        wsem = ("lwa", "lwb")[q]
                        wm = max(last_wup[2 * q], last_wup[2 * q + 1])
                        if q == 1:
                            wm = max(wm, last_wdn[0])
                        em.w("sync", "T", wm)
                        lw = em.dma(
                            "sync",
                            wupb[:, 2 * q * 2048:(2 * q + 2) * 2048]
                            .rearrange("p (g x) -> p g x", g=2),
                            tupv[:, fb:fb + 17:16, :], wsem)
                        em.w("tensor", wsem, lw)
                    if pos == 0:
                        em.w("tensor", "V", st["qm"])
                        em.w("tensor", "S", st["dncp"][0])
                    if pos == 1:
                        em.w("tensor", "S", st["dncp"][1])
                    if pos >= 2:
                        em.w("tensor", "S", rel[pos - 2][0])
                    pp = ps_mm if pos % 2 == 0 else ps_mm2
                    for dgk in range(DG):
                        mt = em.e("tensor",
                                  lambda e, dgk=dgk, pp=pp, w=wsl, T=T:
                                  e.matmul(
                                      pp[:, 0:T],
                                      w[:, dgk * 128:(dgk + 1) * 128],
                                      Qa[:, dgk * T:(dgk + 1) * T],
                                      start=(dgk == 0),
                                      stop=(dgk == DG - 1)))
                    last_wup[j] = mt
                    em.w("scalar", "T", mt)
                    if pos == 0:
                        em.w("scalar", "S")
                        em.w("scalar", "V", st["qm2"])
                        em.w("scalar", "G", st["gplast"])
                    amx = am1 if pos % 2 == 0 else am2
                    m1 = em.e("scalar", lambda e, pp=pp, amx=amx, T=T:
                              e.activation(amx[:, 0:T], pp[:, 0:T], AF.Relu))
                    em.w("scalar", "S", m1)
                    m2 = em.e("scalar", lambda e, fb=fb, amx=amx, T=T:
                              e.activation(R2a[:, fb * T:(fb + 1) * T],
                                           amx[:, 0:T], AF.Square))
                    rel.append((m1, m2))
                    for c0, c1, li in sched[pos]:
                        if li < GPLVL:
                            em.w("gpsimd", "S", m2)
                            em.w("gpsimd", "G")
                            em.e("gpsimd", lambda e, c0=c0, c1=c1, T=T:
                                 e.tensor_tensor(
                                     mxb[:, 0:T],
                                     R2a[:, c0 * T:(c0 + 1) * T],
                                     R2a[:, c1 * T:(c1 + 1) * T],
                                     ALU.subtract))
                            em.w("gpsimd", "G")
                            em.e("gpsimd", lambda e, c0=c0, c1=c1, T=T:
                                 e.tensor_tensor(
                                     R2a[:, c0 * T:(c0 + 1) * T],
                                     R2a[:, c0 * T:(c0 + 1) * T],
                                     R2a[:, c1 * T:(c1 + 1) * T], ALU.add))
                            em.w("gpsimd", "G")
                            gm = em.e("gpsimd",
                                      lambda e, c1=c1, T=T:
                                      e.tensor_copy(
                                          R2a[:, c1 * T:(c1 + 1) * T],
                                          mxb[:, 0:T]))
                            tl[c0] = tl[c1] = gm
                        else:
                            em.w("vector", "S", m2)
                            for t in (c0, c1):
                                if t in tl:
                                    em.w("vector", "G", tl.pop(t))
                            em.w("vector", "V")
                            em.e("vector", lambda e, c0=c0, c1=c1, T=T:
                                 e.tensor_tensor(
                                     R2a[:, c0 * T:(c0 + 1) * T],
                                     R2a[:, c0 * T:(c0 + 1) * T],
                                     R2a[:, c1 * T:(c1 + 1) * T], ALU.add))
                            em.w("vector", "V")
                            em.e("vector", lambda e, c0=c0, c1=c1, T=T:
                                 e.scalar_tensor_tensor(
                                     R2a[:, c1 * T:(c1 + 1) * T],
                                     R2a[:, c1 * T:(c1 + 1) * T], -2.0,
                                     R2a[:, c0 * T:(c0 + 1) * T], ALU.mult,
                                     ALU.add))
                    if nxt and pos == 7:
                        a_stats(em, k + 1)
                    if nxt and pos == 15:
                        a_apply(em, k + 1)
                st["rel2"] = rel[-1][1]

            def part_b(em, k):
                if "r2" in dbg and k == 0:
                    em.w("sync", "S", st["rel2"])
                    fm_store(dbg["r2"], R2a, FG, T, "std")
                    em.barrier()
                em.w("vector", "S", st["rel2"])
                em.w("vector", "G")
                emit_bfly(em, R2a, FG, T, steps=[1])
                emit_h128tree(em, R2a, FG, T, ps_y, ps_y2, mxa, mxb)
                bcm = emit_absmax_tail(em, T, 1e-30, ps_y, ps_y2, 4, 512,
                                       mxa, mxb)
                em.w("vector", "V")
                em.e("vector", lambda e: e.tensor_tensor(
                    bcol1[:, 0:nb], acol1[:, 0:nb], acol1[:, 0:nb],
                    ALU.mult))
                em.e("vector", lambda e: e.tensor_scalar(
                    coltmp[:, 0:1], sdnc[:], EPS_Q / QB, None, ALU.mult))
                em.w("vector", "V")
                em.e("vector", lambda e: e.tensor_tensor(
                    bcol1[:, 0:nb], bcol1[:, 0:nb], mca[:, 4:4 + nb],
                    ALU.mult))
                em.w("vector", "V")
                em.e("vector", lambda e: e.tensor_scalar(
                    bcol1[:, 0:nb], bcol1[:, 0:nb], sdnc[:],
                    1.0 / (QB * 64.0), ALU.mult, ALU.mult))
                em.w("vector", "V")
                for b in range(nb):
                    em.e("vector", lambda e, b=b: e.tensor_tensor(
                        bcol1[:, b:b + 1], bcol1[:, b:b + 1], coltmp[:, 0:1],
                        ALU.max))
                st["qm2"] = emit_quant(em, R2a, Q2a, FG, T, bcm,
                                       psc=ps_y2, qams=(am1, mxa))
                if "q2" in dbg and k == 0:
                    em.w("sync", "V")
                    fm_store(dbg["q2"], Q2a, FG, T, "std")
                    em.barrier()
                bmt = emit_colbcast(em, bcol1, 0, nb, em.mark("V"),
                                    pst=ps_y, psc=ps_y2, rq0=512)
                em.w("scalar", "T", bmt)
                em.w("scalar", "V", st["qm2"])
                em.w("scalar", "S")
                em.w("scalar", "G")
                st["bbm"] = em.e("scalar", lambda e: e.copy(
                    bbeta, ps_y2[:, 0:T]))

            def part_dn(em, k, mids=None):
                t0 = k * T
                cpy = [0] * DG
                gpm = [0] * DG
                sil = [0] * DG
                for eb in range(DG):
                    if mids and eb in mids:
                        mids[eb]()
                    wsl = SLW.bitcast(BF16)[:, 4096 + (eb % 2) * 4096:
                                            8192 + (eb % 2) * 4096]
                    wsem = "lwc" if eb % 2 == 0 else "lwd"
                    wm = last_wdn[eb % 2]
                    if eb == 0:
                        wm = max(wm, last_wup[2], last_wup[3])
                    em.w("sync", "T", wm)
                    lw = em.dma("sync", wsl,
                                tdownQ[:, eb * FFDIM:(eb + 1) * FFDIM],
                                wsem)
                    em.w("tensor", wsem, lw)
                    if eb == 0:
                        em.w("tensor", "V", st["qm2"])
                    if eb >= 2:
                        em.w("tensor", "S", cpy[eb - 2])
                    pp = ps_mm if eb % 2 == 0 else ps_mm2
                    for fgk in range(FG):
                        mt = em.e("tensor",
                                  lambda e, fgk=fgk, pp=pp, w=wsl, T=T:
                                  e.matmul(
                                      pp[:, 0:T],
                                      w[:, fgk * 128:(fgk + 1) * 128],
                                      Q2a[:, fgk * T:(fgk + 1) * T],
                                      start=(fgk == 0),
                                      stop=(fgk == FG - 1)))
                    last_wdn[eb % 2] = mt
                    em.w("scalar", "T", mt)
                    em.w("scalar", "S", st["bbm"])
                    if eb > 0:
                        em.w("scalar", "G", gpm[eb - 1])
                    else:
                        em.w("scalar", "G")
                        em.w("gpsimd", "V", st["qm2"])
                    cpy[eb] = em.e("scalar", lambda e, pp=pp, T=T: e.copy(
                        dn, pp[:, 0:T]))
                    em.w("gpsimd", "S", cpy[eb])
                    em.w("gpsimd", "G")
                    gpm[eb] = em.e("gpsimd", lambda e: e.tensor_tensor(
                        dng, dn, bbeta, ALU.mult))
                    em.w("scalar", "G", gpm[eb])
                    hslot = eb % 8
                    if hslot == 0:
                        em.w("scalar", "stg")
                    sil[eb] = em.e("scalar",
                                   lambda e, hslot=hslot, T=T: e.activation(
                                       Ga[:, hslot * T:(hslot + 1) * T],
                                       dng, AF.Silu))
                    if hslot == 7:
                        em.w("sync", "S", sil[eb])
                        half = eb // 8
                        em.dma("sync",
                               g_all[half * 1024:(half + 1) * 1024,
                                     t0:t0 + T].rearrange(
                                   "(g p) t -> p g t", p=128),
                               Ga[:, 0:8 * T].rearrange("p (g t) -> p g t",
                                                        g=8), "stg")
                st["gplast"] = gpm[-1]
                st["sillast"] = sil[-1]
                st["dncp"] = [cpy[DG - 2], cpy[DG - 1]]

            issue_ldx(em, 0)
            a_stats(em, 0)
            a_apply(em, 0)
            a_bfly(em, 0)
            a_h128(em, 0)
            a_tail(em, 0)
            for k in range(nch_a):
                part_up(em, k)
                part_b(em, k)
                kk = k + 1
                if kk < nch_a:
                    part_dn(em, k, mids={
                        2: (lambda kk=kk: a_bfly(em, kk)),
                        8: (lambda kk=kk: a_h128(em, kk)),
                        12: (lambda kk=kk: a_tail(em, kk))})
                else:
                    part_dn(em, k)
            em.barrier()

        def xnb_store(em, t0, T, XB, xbm):
            em.w("sync", "V", xbm)
            em.w("sync", "stx")
            if multi:
                em.cnt["stx"] += 16
                if em.active == "sync":
                    for arm in em.engines["sync"].switch_core_id(n=128):
                        h = arm.logical % 2
                        own0 = h * L_own
                        if own0 <= t0 < own0 + L_own:
                            dst = xnb_own[:, t0 - own0:t0 - own0 + T]
                        else:
                            dst = dsink[:, 0:T]
                        em.engines["sync"].dma_start(
                            out=dst.rearrange("(g p) t -> p g t", p=128),
                            in_=XB[:, 0:DG * T].rearrange(
                                "p (g t) -> p g t", g=DG)).then_inc(
                            em.sems["stx"], 16)
            else:
                own0 = n_half * L_own
                if own0 <= t0 < own0 + L_own:
                    fm_store(xnb_own[:, t0 - own0:t0 - own0 + T], XB, DG, T,
                             "stx")

        def pb(em):
            T = TCB
            AXa = SLA
            XB = SLQ.bitcast(BF16)
            SLRb = SLR.bitcast(BF16)
            SLBb = SLB.bitcast(BF16)
            DLT = [SLRb[:, 0:DG * T], SLRb[:, DG * 512:DG * 512 + DG * T]]
            UU = [SLRb[:, 2 * DG * 512:2 * DG * 512 + DG * T],
                  SLRb[:, 3 * DG * 512:3 * DG * 512 + DG * T]]
            BCB = [SLBb[:, 0:16 * T], SLBb[:, 16 * 512:16 * 512 + 16 * T]]
            DW = SLG.bitcast(BF16)
            sww = SLW.bitcast(BF16)
            dtl = sww[:, 0:512]
            yev = [sww[:, 512:1024], sww[:, 1024:1536]]
            scr = [sww[:, 1536:2048], sww[:, 2048:2560]]
            scr_f0 = SLW[:, 1280:1792]
            wbg = [sww[:, 3584 + i * 512:4096 + i * 512] for i in range(NH)]
            bcrow1 = SLA.bitcast(BF16)
            em.w("vector", "V")
            em.e("vector", lambda e: e.memset(statec[:], 0.0))
            prep_marks = {}

            def prep(k):
                q = k % 2
                t0 = k * T
                em.w("sync", "V")
                em.w("sync", "T")
                em.w("sync", "S")
                ldm = fm_load(AXa, x_fm[:, t0:t0 + T], DG, T, "ldx")
                emit_ln_rows(em, AXa, T, ldm, "ldx")
                em.w("vector", "stx")
                mb0 = em.e("vector", lambda e, T=T: e.tensor_copy(
                    scr_f0[:, 0:T], ps_bc[:, 0:T]))
                mb1 = em.e("vector", lambda e, T=T: e.tensor_copy(
                    scr[1][:, 0:T], ps_bc[:, 512:512 + T]))
                em.w("vector", "V", mb1)
                for dg in range(DG):
                    em.e("vector", lambda e, dg=dg, T=T: e.tensor_tensor(
                        XB[:, dg * T:(dg + 1) * T],
                        AXa[:, dg * T:(dg + 1) * T], scr_f0[:, 0:T],
                        ALU.mult))
                em.w("vector", "V")
                for dg in range(DG):
                    em.e("vector", lambda e, dg=dg, T=T: e.tensor_tensor(
                        XB[:, dg * T:(dg + 1) * T],
                        XB[:, dg * T:(dg + 1) * T], scr[1][:, 0:T],
                        ALU.add))
                xbm = em.mark("V")
                xnb_store(em, t0, T, XB, xbm)
                em.w("tensor", "V", xbm)
                for dg in range(DG):
                    em.e("tensor", lambda e, dg=dg, T=T: e.matmul(
                        ps_mm[:, 0:T], xprojTb[:, dg * 144:dg * 144 + 128],
                        XB[:, dg * T:(dg + 1) * T],
                        start=(dg == 0), stop=(dg == DG - 1)))
                for dg in range(DG):
                    mt = em.e("tensor", lambda e, dg=dg, T=T: e.matmul(
                        ps_t[0:16, 0:T],
                        xprojTb[:, dg * 144 + 128:dg * 144 + 144],
                        XB[:, dg * T:(dg + 1) * T],
                        start=(dg == 0), stop=(dg == DG - 1)))
                em.w("vector", "T", mt)
                em.e("vector", lambda e, T=T: e.tensor_copy(dtl[:, 0:T],
                                                            ps_mm[:, 0:T]))
                mv = em.e("vector", lambda e, T=T: e.tensor_copy(
                    bcs16[:, 0:T], ps_t[0:16, 0:T]))
                em.w("sync", "V", mv)
                em.w("sync", "ldb")
                stbm = em.dma("sync",
                              bcscr.rearrange("p (j t) -> (p j) t", j=16)[
                                  :, 0:T],
                              bcs16[:, 0:T], "stb")
                em.w("sync", "stb", stbm)
                ldbm = em.dma("sync",
                              bcrow1[0:1, 0:16 * T].rearrange(
                                  "p (j t) -> p j t", j=16),
                              bcscr.rearrange("p (j t) -> p j t", j=16)[
                                  0:1, :, 0:T], "ldb")
                em.w("tensor", "ldb", ldbm)
                em.w("scalar", "V")
                em.w("scalar", "S")
                em.w("scalar", "G")
                bcm = []
                for j in range(16):
                    if j >= 2:
                        em.w("tensor", "S", bcm[j - 2])
                    mt = em.e("tensor", lambda e, j=j, T=T: e.matmul(
                        ps_bc[:, (j % 2) * 512:(j % 2) * 512 + T],
                        onesb_r[0:1, :], bcrow1[0:1, j * T:(j + 1) * T],
                        start=True, stop=True))
                    em.w("scalar", "T", mt)
                    bcm.append(em.e("scalar", lambda e, j=j, q=q, T=T: e.copy(
                        BCB[q][:, j * T:(j + 1) * T],
                        ps_bc[:, (j % 2) * 512:(j % 2) * 512 + T])))
                dtm = []
                for dg in range(DG):
                    pp = ps_mm2 if dg % 2 == 0 else ps_mm
                    if dg >= 2:
                        em.w("tensor", "S", dtm[dg - 2])
                    mt = em.e("tensor", lambda e, dg=dg, pp=pp, T=T:
                              e.matmul(pp[:, 0:T],
                                       dtprojTb[:, dg * 128:(dg + 1) * 128],
                                       dtl[:, 0:T], start=True, stop=True))
                    em.w("scalar", "T", mt)
                    amx = am1 if dg % 2 == 0 else am2
                    m1 = em.e("scalar", lambda e, dg=dg, pp=pp, amx=amx, T=T:
                              e.activation(amx[:, 0:T], pp[:, 0:T], AF.Exp,
                                           bias=dtbias_s[:, dg:dg + 1]))
                    em.w("scalar", "S", m1)
                    dtm.append(em.e("scalar",
                                    lambda e, dg=dg, amx=amx, q=q, T=T:
                                    e.activation(
                                        DLT[q][:, dg * T:(dg + 1) * T],
                                        amx[:, 0:T], AF.Ln,
                                        bias=cst[:, 0:1])))
                em.w("vector", "S", dtm[-1])
                em.w("vector", "G")
                for dg in range(DG):
                    em.e("vector", lambda e, dg=dg, q=q, T=T: e.tensor_tensor(
                        UU[q][:, dg * T:(dg + 1) * T],
                        DLT[q][:, dg * T:(dg + 1) * T],
                        XB[:, dg * T:(dg + 1) * T], ALU.mult))
                prep_marks[k] = (em.mark("V"), bcm[-1], dtm[-1])

            def scan(k):
                q = k % 2
                t0 = k * T
                um, bce, dte = prep_marks[k]
                em.w("vector", "V", um)
                em.w("vector", "S", bce)
                em.w("vector", "G")
                em.w("gpsimd", "V")
                em.w("gpsimd", "S", bce)
                em.w("tensor", "S")
                S = DG * NH
                tts_m = {}
                mmv_m = {}
                w_m = {}
                stc_m = {}
                da_m = {}
                imm_m = {}
                ev_marks = {}
                ydma = {}

                def buf(base, s):
                    par = s % 2
                    return DW[:, base + par * 512:base + par * 512 + 512]

                def emit_da(s):
                    dg, j = divmod(s, NH)
                    dab = DW[:, (s % 2) * 1024:(s % 2) * 1024 + 512]
                    if s >= 2:
                        em.w("scalar", "V", tts_m[s - 2])
                    da_m[s] = em.e("scalar", lambda e, dg=dg, j=j, dab=dab,
                                   q=q, T=T:
                                   e.activation(
                                       dab[:, 0:T],
                                       DLT[q][:, dg * T:(dg + 1) * T],
                                       AF.Exp,
                                       scale=A_s[:, dg * NH + j:
                                                 dg * NH + j + 1]))

                def emit_w(s):
                    dg, j = divmod(s, NH)
                    wb_ = wbg[j]
                    if dg >= 1:
                        em.w("gpsimd", "V", tts_m[(dg - 1) * NH + j])
                    w_m[s] = em.e("gpsimd", lambda e, dg=dg, j=j, wb_=wb_,
                                  q=q, T=T:
                                  e.tensor_tensor(
                                      wb_[:, 0:T],
                                      UU[q][:, dg * T:(dg + 1) * T],
                                      BCB[q][:, j * T:(j + 1) * T],
                                      ALU.mult))

                def emit_tts(s):
                    dg, j = divmod(s, NH)
                    dab = DW[:, (s % 2) * 1024:(s % 2) * 1024 + 512]
                    wb_ = wbg[j]
                    hb_ = DW[:, 2048 + (s % 2) * 512:
                             2048 + (s % 2) * 512 + 512]
                    em.w("vector", "S", da_m[s])
                    if s >= 2:
                        em.w("vector", "V", mmv_m[s - 2])
                        em.w("vector", "G", stc_m[s - 2])
                    em.w("vector", "G", w_m[s])
                    tts_m[s] = em.e("vector",
                                    lambda e, dab=dab, wb_=wb_, hb_=hb_,
                                    dg=dg, j=j, T=T:
                                    e.tensor_tensor_scan(
                                        hb_[:, 0:T], dab[:, 0:T],
                                        wb_[:, 0:T],
                                        statec[:, dg * NH + j:
                                               dg * NH + j + 1],
                                        ALU.mult, ALU.add))

                def emit_consume(s):
                    dg, j = divmod(s, NH)
                    hb_ = DW[:, 2048 + (s % 2) * 512:
                             2048 + (s % 2) * 512 + 512]
                    mb_ = DW[:, 3072 + (s % 2) * 512:
                             3072 + (s % 2) * 512 + 512]
                    em.w("gpsimd", "V", tts_m[s])
                    stc_m[s] = em.e("gpsimd",
                                    lambda e, hb_=hb_, dg=dg, j=j, T=T:
                                    e.tensor_copy(
                                        statec[:, dg * NH + j:
                                               dg * NH + j + 1],
                                        hb_[:, T - 1:T]))
                    em.w("vector", "V", tts_m[s])
                    if s >= 2:
                        em.w("vector", "T", imm_m[s - 2])
                    mm = em.e("vector",
                              lambda e, hb_=hb_, mb_=mb_, j=j, q=q, T=T:
                              e.tensor_tensor(
                                  mb_[:, 0:T], hb_[:, 0:T],
                                  BCB[q][:, (8 + j) * T:(9 + j) * T],
                                  ALU.mult))
                    mmv_m[s] = mm
                    psy = ps_y if dg % 2 == 0 else ps_y2
                    em.w("tensor", "V", mm)
                    if j == 0 and dg >= 2:
                        em.w("tensor", "S", ev_marks[dg - 2])
                    imm_m[s] = em.e(
                        "tensor", lambda e, mb_=mb_, psy=psy, j=j, T=T:
                        e.matmul(psy[:, 0:T], identb[:], mb_[:, 0:T],
                                 start=(j == 0), stop=(j == NH - 1)))
                    if j == NH - 1:
                        ysem = "sty" if dg % 2 == 0 else "sty2"
                        em.w("scalar", "T", imm_m[s])
                        em.w("scalar", ysem)
                        ev_marks[dg] = em.e(
                            "scalar", lambda e, psy=psy, dg=dg, T=T:
                            e.copy(yev[dg % 2][:, 0:T], psy[:, 0:T]))
                        em.w("sync", "S", ev_marks[dg])
                        ydma[dg] = em.dma(
                            "sync",
                            y_all[dg * 128:(dg + 1) * 128, t0:t0 + T],
                            yev[dg % 2][:, 0:T], ysem)

                for s in range(S + 2):
                    dg, j = divmod(s, NH) if s < S else (None, None)
                    if s < S and j == 1:
                        em.w("gpsimd", "G")
                        for jj in range(NH):
                            emit_w(dg * NH + jj)
                    if s < S:
                        emit_da(s)
                    if 1 <= s < S + 1:
                        emit_tts(s - 1)
                    if s >= 2:
                        emit_consume(s - 2)

            prep(0)
            for k in range(nch_b):
                if k + 1 < nch_b:
                    prep(k + 1)
                scan(k)
            if multi:
                em.w("sync", "sty")
                em.w("sync", "sty2")
                em.cnt["styp"] += 16
                if em.active == "sync":
                    for arm in em.engines["sync"].switch_core_id(n=128):
                        h = arm.logical % 2
                        po = (1 - h) * L_own
                        em.engines["sync"].dma_start(
                            out=ysh[:, h * L_own:(h + 1) * L_own],
                            in_=y_all[:, po:po + L_own]).then_inc(
                            em.sems["styp"], 16)
                em.w("gpsimd", "styp")
                em.e("gpsimd", lambda e: e.collective_compute(
                    "AllReduce", ALU.add,
                    replica_groups=[[0, 1], [2, 3], [4, 5], [6, 7]],
                    ins=[bar_i], outs=[bar_o]))
            em.barrier()

        def pc(em):
            T = TCC
            AXa = SLA
            OUT = SLB
            YT = SLQ.bitcast(BF16)
            SLRb = SLR.bitcast(BF16)
            YP = SLRb[:, 0:DG * T]
            SS = SLRb[:, DG * 512:DG * 512 + DG * T]
            XBc = SLRb[:, 2 * DG * 512:2 * DG * 512 + DG * T]
            GG = SLRb[:, 3 * DG * 512:3 * DG * 512 + DG * T]
            wob = SLW.bitcast(BF16)
            last_wo = [0, 0]
            own0 = 0 if not multi else None
            for k in range(nch_c):
                t0 = k * T
                em.w("sync", "V")
                em.w("sync", "T")
                em.w("sync", "S")
                if multi:
                    em.cnt["ldx"] += 16
                    em.cnt["ldx2"] += 16
                    if em.active == "sync":
                        for arm in em.engines["sync"].switch_core_id(n=128):
                            h = arm.logical % 2
                            o0 = h * L_own
                            em.engines["sync"].dma_start(
                                out=AXa[:, 0:DG * T].rearrange(
                                    "p (g t) -> p g t", g=DG),
                                in_=x_fm[:, o0 + t0:o0 + t0 + T].rearrange(
                                    "(g p) t -> p g t", p=128)).then_inc(
                                em.sems["ldx"], 16)
                            em.engines["sync"].dma_start(
                                out=YP[:, 0:DG * T].rearrange(
                                    "p (g t) -> p g t", g=DG),
                                in_=ysh[:, (1 - h) * L_own + t0:
                                        (1 - h) * L_own + t0 + T].rearrange(
                                    "(g p) t -> p g t", p=128)).then_inc(
                                em.sems["ldx2"], 16)
                    l1 = em.mark("ldx")
                    l2 = em.mark("ldx2")
                    l2b = fm_load(YT, y_all[:, t0:t0 + T], DG, T, "ldx2") \
                        if False else None
                    em.cnt["ldx2"] += 16
                    if em.active == "sync":
                        for arm in em.engines["sync"].switch_core_id(n=128):
                            h = arm.logical % 2
                            o0 = h * L_own
                            em.engines["sync"].dma_start(
                                out=YT[:, 0:DG * T].rearrange(
                                    "p (g t) -> p g t", g=DG),
                                in_=y_all[:, o0 + t0:o0 + t0 + T].rearrange(
                                    "(g p) t -> p g t", p=128)).then_inc(
                                em.sems["ldx2"], 16)
                    l2 = em.mark("ldx2")
                else:
                    o0 = n_half * L_own
                    l1 = fm_load(AXa, x_fm[:, o0 + t0:o0 + t0 + T], DG, T,
                                 "ldx")
                    fm_load(YP, yprt[:, t0:t0 + T], DG, T, "ldx2")
                    l2 = fm_load(YT, y_all[:, o0 + t0:o0 + t0 + T], DG, T,
                                 "ldx2")
                l3 = fm_load(XBc, xnb_own[:, t0:t0 + T], DG, T, "ldc")
                l4 = fm_load(GG, g_all[:, t0:t0 + T], DG, T, "ldc")
                em.w("vector", "ldx2", l2)
                for dg in range(DG):
                    em.e("vector", lambda e, dg=dg, T=T: e.tensor_tensor(
                        YT[:, dg * T:(dg + 1) * T],
                        YT[:, dg * T:(dg + 1) * T],
                        YP[:, dg * T:(dg + 1) * T], ALU.add))
                ytm = em.mark("V")
                em.w("vector", "ldc", l4)
                for eb in range(DG):
                    wsl = wob[:, (eb % 2) * 2048:(eb % 2) * 2048 + 2048]
                    wsem = "lwa" if eb % 2 == 0 else "lwb"
                    em.w("sync", "T", last_wo[eb % 2])
                    lw = em.dma("sync", wsl,
                                WoutTb[:, eb * D:(eb + 1) * D], wsem)
                    em.w("tensor", wsem, lw)
                    if eb == 0:
                        em.w("tensor", "V", ytm)
                    pp = ps_mm if eb % 2 == 0 else ps_mm2
                    for dgk in range(DG):
                        mt = em.e("tensor",
                                  lambda e, dgk=dgk, pp=pp, w=wsl, T=T:
                                  e.matmul(
                                      pp[:, 0:T],
                                      w[:, dgk * 128:(dgk + 1) * 128],
                                      YT[:, dgk * T:(dgk + 1) * T],
                                      start=(dgk == 0),
                                      stop=(dgk == DG - 1)))
                    last_wo[eb % 2] = mt
                    em.w("scalar", "T", mt)
                    amx = am1 if eb % 2 == 0 else am2
                    ms = em.e("scalar", lambda e, pp=pp, amx=amx, eb=eb, T=T:
                              e.activation(amx[:, 0:T], pp[:, 0:T], AF.Relu,
                                           bias=obias_s[:, eb:eb + 1]))
                    em.w("vector", "S", ms)
                    mv = em.e("vector", lambda e, eb=eb, amx=amx, T=T:
                              e.scalar_tensor_tensor(
                                  SS[:, eb * T:(eb + 1) * T],
                                  XBc[:, eb * T:(eb + 1) * T],
                                  D_s[:, eb:eb + 1], amx[:, 0:T],
                                  ALU.mult, ALU.add))
                    em.w("tensor", "V", mv)
                    em.w("scalar", "V", mv)
                if "s2" in dbg and k == 0:
                    em.w("sync", "V")
                    fm_store(dbg["s2"], SS, DG, T, "std")
                    em.barrier()
                em.w("vector", "V")
                em.w("vector", "ldx", l1)
                em.w("vector", "sto")
                for dg in range(DG):
                    em.e("vector", lambda e, dg=dg, T=T: e.tensor_tensor(
                        am3[:, 0:T], SS[:, dg * T:(dg + 1) * T],
                        GG[:, dg * T:(dg + 1) * T], ALU.mult))
                    em.w("vector", "V")
                    em.e("vector", lambda e, dg=dg, T=T: e.tensor_tensor(
                        OUT[:, dg * T:(dg + 1) * T],
                        AXa[:, dg * T:(dg + 1) * T], am3[:, 0:T], ALU.add))
                    em.w("vector", "V")
                om = em.mark("V")
                em.w("sync", "V", om)
                em.w("sync", "sto")
                fm_store(out_fm[:, t0:t0 + T], OUT, DG, T, "sto")
            em.w("sync", "sto")
            em.w("sync", "std")
            em.barrier()

        def prog(em):
            p0(em)
            pa(em)
            pb(em)
            pc(em)

        block = ctx.enter_context(nc.Block())
        em.run(block, prog)

    return nc



def _hadamard128():
    H = np.array([[1.0]], dtype=np.float32)
    while H.shape[0] < 128:
        H = np.block([[H, H], [H, -H]])
    return np.ascontiguousarray(H.astype(np.float32))


def host_inputs(x_fm, xproj, dtproj, dtbias, A_log, Dv, Wout, obias, w_up,
                w_down, half):
    d = {}
    d["x_fm"] = np.ascontiguousarray(x_fm)
    WT = xproj.T
    d["xprojT"] = np.ascontiguousarray(np.concatenate(
        [WT[:, 0:128], WT[:, 128 + 8 * half:136 + 8 * half],
         WT[:, 144 + 8 * half:152 + 8 * half]], axis=1))
    d["dtprojT"] = np.ascontiguousarray(dtproj.T)
    d["dtbias_c"] = np.ascontiguousarray(dtbias.reshape(DG, 128).T)
    d["A_cols"] = np.ascontiguousarray(
        A_log[:, 8 * half:8 * half + 8].reshape(DG, 128, NH)
        .transpose(1, 0, 2).reshape(128, DG * NH))
    d["D_cols"] = np.ascontiguousarray(Dv.reshape(DG, 128).T)
    d["WoutT"] = np.ascontiguousarray(Wout.T)
    d["obias_c"] = np.ascontiguousarray(obias.reshape(DG, 128).T)
    d["wupT"] = np.ascontiguousarray(w_up.T)
    d["wdownT"] = np.ascontiguousarray(w_down.T)
    d["H128"] = _hadamard128()
    d["ident"] = np.ascontiguousarray(np.eye(128, dtype=np.float32))
    d["ones_r"] = np.ones((1, 128), np.float32)
    return d


_nc_cache = {}


def kernel(L0_attn_hidden_states, L0_attn_x_proj_weight,
           L0_attn_dt_proj_weight, L0_attn_dt_proj_bias, L0_attn_A_log,
           L0_attn_D, L0_attn_out_proj_weight, L0_attn_out_proj_bias,
           L0_ffn_w_up, L0_ffn_w_down):
    x = np.asarray(L0_attn_hidden_states)
    B, L, _ = x.shape
    L_own = L // 2
    nc = _nc_cache.get((L, L_own))
    if nc is None:
        nc = build_program(L_all=L, L_own=L_own, TCA=512, TCB=512, TCC=512,
                           n_half=None)
        _nc_cache[(L, L_own)] = nc
    in_maps = []
    for c in range(8):
        b, h = c // 2, c % 2
        in_maps.append(host_inputs(
            x[b].T, np.asarray(L0_attn_x_proj_weight),
            np.asarray(L0_attn_dt_proj_weight),
            np.asarray(L0_attn_dt_proj_bias),
            np.asarray(L0_attn_A_log), np.asarray(L0_attn_D),
            np.asarray(L0_attn_out_proj_weight),
            np.asarray(L0_attn_out_proj_bias),
            np.asarray(L0_ffn_w_up), np.asarray(L0_ffn_w_down), half=h))
    res = run_bass_kernel_spmd(nc, in_maps, core_ids=list(range(8)))
    out = np.empty((B, L, D), np.float32)
    for c in range(8):
        b, h = c // 2, c % 2
        out[b, h * L_own:(h + 1) * L_own, :] = res.results[c]["out_fm"].T
    return out


# version 15
# speedup vs baseline: 1.0831x; 1.0831x over previous
import numpy as np
import ml_dtypes
import concourse.bass as bass
import concourse.mybir as mybir
from concourse.bass_utils import run_bass_kernel_spmd
from contextlib import ExitStack

F32 = mybir.dt.float32
BF16 = mybir.dt.bfloat16
AF = mybir.ActivationFunctionType
ALU = mybir.AluOpType
AX_ = mybir.AxisListType.X
BFNP = ml_dtypes.bfloat16

D = 2048
DG = 16
FFDIM = 4096
FG = 32
NH = 8
RC = float(1.5 * 2 ** 23)
EPS_LN = 1e-5
EPS_Q = 1e-5
QB = 127.0
SQ_D = float(np.sqrt(2048.0))

ENGS = ("sync", "scalar", "vector", "tensor", "gpsimd")
ESEM = {"scalar": "S", "vector": "V", "tensor": "T", "gpsimd": "G"}


class Em:

    def __init__(self, nc):
        self.nc = nc
        self.sems = {}
        self.cnt = {}
        self.active = None
        self.engines = {"sync": nc.sync, "scalar": nc.scalar,
                        "vector": nc.vector, "tensor": nc.tensor,
                        "gpsimd": nc.gpsimd}

    def sem(self, name, ctx):
        self.sems[name] = ctx.enter_context(self.nc.semaphore(name))
        self.cnt[name] = 0

    def e(self, eng, f, extra_sem=None):
        s = ESEM[eng]
        self.cnt[s] += 1
        if extra_sem is not None:
            self.cnt[extra_sem] += 1
        if self.active == eng:
            inst = f(self.engines[eng])
            inst.then_inc(self.sems[s], 1)
            if extra_sem is not None:
                inst.then_inc(self.sems[extra_sem], 1)
        return self.cnt[s]

    def dma(self, eng, out, in_, sem):
        self.cnt[sem] += 16
        if self.active == eng:
            self.engines[eng].dma_start(out=out, in_=in_).then_inc(
                self.sems[sem], 16)
        return self.cnt[sem]

    def w(self, eng, sem, at=None):
        tgt = self.cnt[sem] if at is None else at
        if tgt <= 0:
            return
        if self.active == eng:
            self.engines[eng].wait_ge(self.sems[sem], tgt)

    def mark(self, sem):
        return self.cnt[sem]

    def barrier(self, sems=None):
        if sems is None:
            sems = list(self.sems.keys())
        for e in ENGS:
            for s in sems:
                self.w(e, s)

    def run(self, block, fn):
        for name in ENGS:
            for k in self.cnt:
                self.cnt[k] = 0
            self.active = name
            deco = getattr(block, name)

            def body(eng, _n=name):
                fn(self)
            deco(body)


def build_program(L_all, L_own, TCA, TCB, TCC, n_half=None, debug=()):
    nc = bass.Bass("TRN2", target_bir_lowering=False)
    nch_a = L_own // TCA
    nch_b = L_all // TCB
    nch_c = L_own // TCC
    multi = n_half is None

    def din(name, shape, dt=F32):
        return nc.dram_tensor(name, shape, dt, kind="ExternalInput").ap()

    x_fm = din("x_fm", [D, L_all])
    xprojT = din("xprojT", [D, 144])
    dtprojT = din("dtprojT", [128, D])
    dtbias_c = din("dtbias_c", [128, DG])
    A_cols = din("A_cols", [128, DG * NH])
    D_cols = din("D_cols", [128, DG])
    WoutT = din("WoutT", [D, D])
    obias_c = din("obias_c", [128, DG])
    wupT = din("wupT", [D, FFDIM])
    wdownT = din("wdownT", [FFDIM, D])
    H128 = din("H128", [128, 128])
    ident = din("ident", [128, 128])
    ones_r = din("ones_r", [1, 128])

    out_fm = nc.dram_tensor("out_fm", [D, L_own], F32,
                            kind="ExternalOutput").ap()

    def scratch(name, shape, dt=BF16, dbg_key=None, **kw):
        kind = "ExternalOutput" if (dbg_key and dbg_key in debug) \
            else "Internal"
        return nc.dram_tensor(name, shape, dt, kind=kind, **kw).ap()

    tupQ = scratch("tupQ", [128, FG * D], dbg_key="tupq")
    tdownQ = scratch("tdownQ", [128, DG * FFDIM], dbg_key="tdnq")
    WoutTb = scratch("WoutTb", [128, DG * D])
    g_all = scratch("g_all", [D, L_own], dbg_key="gall")
    xnb_own = scratch("xnb_own", [D, L_own], dbg_key="xnball")
    y_all = scratch("y_all", [D, L_all], dbg_key="yall")
    dsink = nc.dram_tensor("dsink", [D, 512], BF16, kind="Internal").ap()
    bcscr = nc.dram_tensor("bcscr", [1, 8192], BF16, kind="Internal").ap()
    if multi:
        ysh = nc.dram_tensor("ysh", [D, 2 * L_own], BF16, kind="Internal",
                             addr_space="Shared").ap()
        bar_i = nc.dram_tensor("bar_i", [1, 16], F32, kind="Internal").ap()
        bar_o = nc.dram_tensor("bar_o", [1, 16], F32, kind="Internal",
                               addr_space="Local").ap()
    else:
        yprt = din("yprt", [D, L_own], BF16)

    dbg_specs = {
        "xn": ([D, TCA], F32), "y2": ([D, TCA], F32), "q1": ([D, TCA], BF16),
        "r2": ([FFDIM, TCA], F32), "q2": ([FFDIM, TCA], BF16),
        "delta": ([D, TCB], BF16), "u": ([D, TCB], BF16),
        "bc": ([16, TCB], BF16), "s2": ([D, TCC], BF16),
    }
    dbg = {k: nc.dram_tensor("dbg_" + k, *dbg_specs[k],
                             kind="ExternalOutput").ap()
           for k in debug if k in dbg_specs}

    ctx = ExitStack()
    with ctx:
        em = Em(nc)
        for s in ["V", "S", "T", "G",
                  "ldx", "ldx2", "ldc", "lwa", "lwb", "lwc", "lwd",
                  "stg", "stw", "stx", "sty", "sty2", "styp", "sto", "std",
                  "stb", "ldb"]:
            em.sem(s, ctx)

        def sb(name, shape, dt=F32):
            return ctx.enter_context(nc.sbuf_tensor(name, shape, dt))

        def psum(name, shape):
            return ctx.enter_context(nc.psum_tensor(name, shape, F32))

        xprojTb = sb("xprojTb", [128, DG * 144], BF16)
        dtprojTb = sb("dtprojTb", [128, D], BF16)
        dtbias_s = sb("dtbias_s", [128, DG])
        A_s = sb("A_s", [128, DG * NH])
        D_s = sb("D_s", [128, DG])
        obias_s = sb("obias_s", [128, DG])
        H128f = sb("H128f", [128, 128])
        identf = sb("identf", [128, 128])
        onesf_r = sb("onesf_r", [1, 128])
        onesb_r = sb("onesb_r", [1, 128], BF16)
        identb = sb("identb", [128, 128], BF16)
        onesf_c = sb("onesf_c", [128, 1])
        onesb_c_ = sb("onesb_c", [128, 1], BF16)
        supc = sb("supc", [128, 1])
        sdnc = sb("sdnc", [128, 1])
        cst = sb("cst", [128, 4])
        am1 = sb("am1", [128, 512])
        am2 = sb("am2", [128, 512])
        am3 = sb("am3", [128, 512])
        mxa = sb("mxa", [128, 512])
        mxb = sb("mxb", [128, 512])
        mca = sb("mca", [128, 16])
        acol1 = sb("acol1", [128, 8])
        bcol1 = sb("bcol1", [128, 8])
        rows_q = sb("rows_q", [1, 1024])
        rows_ln = sb("rows_ln", [1, 1024])
        rowtmp = sb("rowtmp", [1, 64])
        coltmp = sb("coltmp", [128, 4])
        statec = sb("statec", [128, 128])
        bcs16 = sb("bcs16", [16, 512], BF16)

        SLA = sb("SLA", [128, 8192])
        SLB = sb("SLB", [128, 8192])
        SLR = sb("SLR", [128, 16384])
        SLQ = sb("SLQ", [128, 4096])
        SLG = sb("SLG", [128, 2048])
        SLW = sb("SLW", [128, 6144])

        ps_mm = psum("ps_mm", [128, 512])
        ps_mm2 = psum("ps_mm2", [128, 512])
        ps_bc = psum("ps_bc", [128, 1024])
        ps_t = psum("ps_t", [128, 512])
        ps_y = psum("ps_y", [128, 512])
        ps_y2 = psum("ps_y2", [128, 512])
        ps_x = psum("ps_x", [128, 512])
        ps_r = ps_bc

        def fm_store(dst_ap, arena, ngrp, T, sem, eng="sync"):
            return em.dma(
                eng, dst_ap.rearrange("(g p) t -> p g t", p=128),
                arena[:, 0:ngrp * T].rearrange("p (g t) -> p g t", g=ngrp),
                sem)

        def fm_load(arena, src_ap, ngrp, T, sem, eng="sync"):
            return em.dma(
                eng, arena[:, 0:ngrp * T].rearrange("p (g t) -> p g t",
                                                    g=ngrp),
                src_ap.rearrange("(g p) t -> p g t", p=128), sem)

        def p0(em):
            mv = 0
            for dg in range(DG):
                em.w("sync", "V", mv)
                ld = em.dma("sync", SLA[:, 0:144],
                            xprojT[dg * 128:(dg + 1) * 128, :], "ldc")
                em.w("vector", "ldc", ld)
                mv = em.e("vector", lambda e, dg=dg: e.tensor_copy(
                    xprojTb[:, dg * 144:(dg + 1) * 144], SLA[:, 0:144]))
            em.w("sync", "V", mv)
            ld = em.dma("sync", SLA[:, 0:D], dtprojT, "ldc")
            em.w("vector", "ldc", ld)
            mv = em.e("vector", lambda e: e.tensor_copy(dtprojTb[:],
                                                        SLA[:, 0:D]))
            em.w("sync", "V", mv)
            for dst, src in [(dtbias_s, dtbias_c), (D_s, D_cols),
                             (obias_s, obias_c), (H128f, H128),
                             (identf, ident)]:
                em.dma("sync", dst[:], src, "ldc")
            em.dma("sync", onesf_r[:], ones_r, "ldc")
            ld = em.dma("sync", SLA[:, 0:DG * NH], A_cols, "ldc")
            em.w("vector", "ldc", ld)
            em.w("scalar", "ldc", ld)
            em.e("vector", lambda e: e.tensor_copy(onesb_r[:], onesf_r[:]))
            em.e("vector", lambda e: e.tensor_copy(identb[:], identf[:]))
            em.e("vector", lambda e: e.memset(onesf_c[:], 1.0))
            em.e("vector", lambda e: e.memset(onesb_c_[:], 1.0))
            em.e("vector", lambda e: e.memset(cst[:, 0:1], 1.0))
            em.e("vector", lambda e: e.memset(cst[:, 1:2], EPS_LN))
            em.e("vector", lambda e: e.memset(cst[:, 2:3], -0.5))
            ms = em.e("scalar", lambda e: e.activation(
                A_s[:], SLA[:, 0:DG * NH], AF.Exp))
            em.w("vector", "S", ms)
            em.e("vector", lambda e: e.tensor_scalar(
                A_s[:], A_s[:], -1.0, None, ALU.mult))
            if multi:
                mv = em.e("vector", lambda e: e.memset(rowtmp[0:1, 0:16],
                                                       0.0))
                em.w("sync", "V", mv)
                em.dma("sync", bar_i, rowtmp[0:1, 0:16], "stb")

            wstp = [SLA[:, 0:4096], SLA[:, 4096:8192]]
            wstbp = [SLQ.bitcast(BF16)[:, 0:4096],
                     SLQ.bitcast(BF16)[:, 4096:8192]]
            ldsems = [("ldc", "ldx"), ("sty", "sty2")]
            stsems = ["stw", "stx"]

            def wload(r, src, cols, freemark):
                p = r % 2
                h = cols // 2
                em.w("sync", "V", freemark)
                l1 = em.dma("sync", wstp[p][:, 0:h],
                            src[r * 128:(r + 1) * 128, 0:h], ldsems[p][0])
                l2 = em.dma("sync", wstp[p][:, h:cols],
                            src[r * 128:(r + 1) * 128, h:cols], ldsems[p][1])
                return l1, l2

            def quant_weight(src, dst, rows, cols, sup_col):
                nrow = rows // 128
                base = em.mark("V")
                vm = {}
                lds = {}
                for r in range(nrow):
                    lds[r] = wload(r, src, cols, vm.get(r - 2, base))
                    p = r % 2
                    wst = wstp[p]
                    em.w("vector", ldsems[p][0], lds[r][0])
                    em.w("vector", ldsems[p][1], lds[r][1])
                    c01 = coltmp[:, 0:1] if r == 0 else coltmp[:, 1:2]
                    em.w("vector", "V")
                    em.e("vector", lambda e, c01=c01, cols=cols, wst=wst:
                         e.tensor_reduce(c01, wst[:, 0:cols], AX_, ALU.add,
                                         apply_absolute_value=True))
                    em.w("vector", "V")
                    if r > 0:
                        em.e("vector", lambda e: e.tensor_tensor(
                            coltmp[:, 0:1], coltmp[:, 0:1], coltmp[:, 1:2],
                            ALU.add))
                        em.w("vector", "V")
                    vm[r] = em.mark("V")
                mv = vm[nrow - 1]
                em.w("tensor", "V", mv)
                mt = em.e("tensor", lambda e: e.matmul(
                    ps_t[0:1, 0:1], onesf_c[:], coltmp[:, 0:1],
                    start=True, stop=True))
                em.w("vector", "T", mt)
                em.w("vector", "stb")
                em.e("vector", lambda e, rows=rows, cols=cols:
                     e.tensor_scalar(
                         rowtmp[0:1, 0:1], ps_t[0:1, 0:1],
                         1.0 / (rows * cols), EPS_Q, ALU.mult, ALU.max))
                em.w("vector", "V")
                mv = em.e("vector", lambda e: e.reciprocal(
                    rowtmp[0:1, 1:2], rowtmp[0:1, 0:1]))
                em.w("tensor", "V", mv)
                mt = em.e("tensor", lambda e: e.matmul(
                    ps_y[:, 0:2], onesf_r[0:1, :], rowtmp[0:1, 0:2],
                    start=True, stop=True))
                em.w("vector", "T", mt)
                em.e("vector", lambda e: e.tensor_copy(sup_col[:],
                                                       ps_y[:, 0:1]))
                mv = em.e("vector", lambda e: e.tensor_copy(coltmp[:, 2:3],
                                                            ps_y[:, 1:2]))
                vm2 = {nrow - 1: mv}
                stm = {}
                stbase = [em.mark(s) for s in stsems]
                for r in range(nrow):
                    p = r % 2
                    wst = wstp[p]
                    wstb = wstbp[p]
                    lds[nrow + r] = wload(r, src, cols,
                                          vm2.get(nrow + r - 2, mv))
                    em.w("vector", ldsems[p][0], lds[nrow + r][0])
                    em.w("vector", ldsems[p][1], lds[nrow + r][1])
                    em.w("vector", stsems[p], stm.get(r - 2, stbase[p]))
                    em.w("vector", "V")
                    em.e("vector", lambda e, cols=cols, wst=wst:
                         e.tensor_scalar(
                             wst[:, 0:cols], wst[:, 0:cols], coltmp[:, 2:3],
                             None, ALU.mult))
                    em.w("vector", "V")
                    em.e("vector", lambda e, cols=cols, wst=wst:
                         e.tensor_scalar(
                             wst[:, 0:cols], wst[:, 0:cols], RC, RC, ALU.add,
                             ALU.subtract))
                    em.w("vector", "V")
                    qv = em.e("vector",
                              lambda e, cols=cols, wst=wst, wstb=wstb:
                              e.tensor_scalar(
                                  wstb[:, 0:cols], wst[:, 0:cols], 1.0, -1.0,
                                  ALU.min, ALU.max))
                    vm2[nrow + r] = qv
                    em.w("sync", "V", qv)
                    nblk = cols // 128
                    stm[r] = em.dma(
                        "sync",
                        dst.rearrange("p (b rest) -> p b rest", b=nblk)[
                            :, :, r * 128:(r + 1) * 128],
                        wstb[:, 0:cols].rearrange("p (b t) -> p b t",
                                                  b=nblk), stsems[p])

            quant_weight(wupT, tupQ, D, FFDIM, supc)
            quant_weight(wdownT, tdownQ, FFDIM, D, sdnc)
            mv = em.mark("V")
            vmo = {}
            ldo = {}
            sto = {}
            stbase = [em.mark(s) for s in stsems]
            for r in range(DG):
                p = r % 2
                wst = wstp[p]
                wstb = wstbp[p]
                ldo[r] = wload(r, WoutT, D, vmo.get(r - 2, mv))
                em.w("vector", ldsems[p][0], ldo[r][0])
                em.w("vector", ldsems[p][1], ldo[r][1])
                em.w("vector", stsems[p], sto.get(r - 2, stbase[p]))
                em.w("vector", "V")
                cv = em.e("vector", lambda e, wst=wst, wstb=wstb:
                          e.tensor_copy(wstb[:, 0:D], wst[:, 0:D]))
                vmo[r] = cv
                em.w("sync", "V", cv)
                sto[r] = em.dma(
                    "sync",
                    WoutTb.rearrange("p (b rest) -> p b rest", b=DG)[
                        :, :, r * 128:(r + 1) * 128],
                    wstb[:, 0:D].rearrange("p (b t) -> p b t", b=DG),
                    stsems[p])
            em.barrier()

        def emit_ln_rows(em, AXa, T, ldmark, ldsem):
            xbv = am3.bitcast(BF16)
            em.w("vector", ldsem, ldmark)
            em.w("scalar", ldsem, ldmark)
            em.w("scalar", "S")
            em.w("scalar", "V")
            em.w("vector", "V")
            sq = []
            for dg in range(DG):
                amx = (am1 if dg % 2 == 0 else am2).bitcast(BF16)
                xbs = xbv[:, (dg % 2) * 512:(dg % 2) * 512 + T]
                if dg >= 2:
                    em.w("vector", "T", sq[dg - 2][0])
                mvx = em.e("vector", lambda e, dg=dg, T=T, xbs=xbs:
                           e.tensor_copy(xbs, AXa[:, dg * T:(dg + 1) * T]))
                if dg >= 2:
                    em.w("scalar", "T", sq[dg - 2][1])
                ms = em.e("scalar", lambda e, dg=dg, amx=amx, T=T:
                          e.activation(amx[:, 0:T],
                                       AXa[:, dg * T:(dg + 1) * T],
                                       AF.Square))
                em.w("tensor", "V", mvx)
                m1 = em.e("tensor", lambda e, dg=dg, T=T, xbs=xbs: e.matmul(
                    ps_r[0:1, 0:T], onesb_c_[:], xbs,
                    start=(dg == 0), stop=(dg == DG - 1)))
                em.w("tensor", "S", ms)
                m2 = em.e("tensor", lambda e, amx=amx, dg=dg, T=T:
                          e.matmul(ps_r[0:1, 512:512 + T], onesb_c_[:],
                                   amx[:, 0:T], start=(dg == 0),
                                   stop=(dg == DG - 1)))
                sq.append((m1, m2))
            em.w("vector", "T", sq[-1][1])
            em.e("vector", lambda e, T=T: e.tensor_scalar(
                rows_ln[0:1, 0:T], ps_r[0:1, 0:T], 1.0 / D, None, ALU.mult))
            em.e("vector", lambda e, T=T: e.tensor_scalar(
                rows_ln[0:1, 512:512 + T], ps_r[0:1, 512:512 + T], 1.0 / D,
                None, ALU.mult))
            em.w("vector", "V")
            em.e("vector", lambda e, T=T: e.tensor_tensor(
                am1[0:1, 0:T], rows_ln[0:1, 0:T], rows_ln[0:1, 0:T],
                ALU.mult))
            em.w("vector", "V")
            mv = em.e("vector", lambda e, T=T: e.tensor_tensor(
                rows_ln[0:1, 512:512 + T], rows_ln[0:1, 512:512 + T],
                am1[0:1, 0:T], ALU.subtract))
            em.w("scalar", "V", mv)
            em.e("scalar", lambda e, T=T: e.activation(
                rows_ln[0:1, 512:512 + T], rows_ln[0:1, 512:512 + T], AF.Ln,
                bias=cst[0:1, 1:2]))
            em.w("scalar", "S")
            ms = em.e("scalar", lambda e, T=T: e.activation(
                rows_ln[0:1, 512:512 + T], rows_ln[0:1, 512:512 + T], AF.Exp,
                scale=cst[0:1, 2:3]))
            em.w("vector", "S", ms)
            mv = em.e("vector", lambda e, T=T: e.scalar_tensor_tensor(
                rows_ln[0:1, 0:T], rows_ln[0:1, 0:T], -1.0,
                rows_ln[0:1, 512:512 + T], ALU.mult, ALU.mult))
            em.w("tensor", "V", mv)
            em.w("tensor", "S", ms)
            em.e("tensor", lambda e, T=T: e.matmul(
                ps_bc[:, 0:T], onesf_r[0:1, :], rows_ln[0:1, 512:512 + T],
                start=True, stop=True))
            mt = em.e("tensor", lambda e, T=T: e.matmul(
                ps_bc[:, 512:512 + T], onesf_r[0:1, :], rows_ln[0:1, 0:T],
                start=True, stop=True))
            em.w("vector", "T", mt)
            return mt

        def emit_ln(em, AXa, ANa, T, ldmark, ldsem):
            xbv = am3.bitcast(BF16)
            em.w("vector", ldsem, ldmark)
            em.w("scalar", ldsem, ldmark)
            em.w("scalar", "S")
            em.w("scalar", "V")
            em.w("vector", "V")
            sq = []
            for dg in range(DG):
                amx = (am1 if dg % 2 == 0 else am2).bitcast(BF16)
                xbs = xbv[:, (dg % 2) * 512:(dg % 2) * 512 + T]
                if dg >= 2:
                    em.w("vector", "T", sq[dg - 2][0])
                mvx = em.e("vector", lambda e, dg=dg, T=T, xbs=xbs:
                           e.tensor_copy(xbs, AXa[:, dg * T:(dg + 1) * T]))
                if dg >= 2:
                    em.w("scalar", "T", sq[dg - 2][1])
                ms = em.e("scalar", lambda e, dg=dg, amx=amx, T=T:
                          e.activation(amx[:, 0:T],
                                       AXa[:, dg * T:(dg + 1) * T],
                                       AF.Square))
                em.w("tensor", "V", mvx)
                m1 = em.e("tensor", lambda e, dg=dg, T=T, xbs=xbs: e.matmul(
                    ps_r[0:1, 0:T], onesb_c_[:], xbs,
                    start=(dg == 0), stop=(dg == DG - 1)))
                em.w("tensor", "S", ms)
                m2 = em.e("tensor", lambda e, amx=amx, dg=dg, T=T:
                          e.matmul(ps_r[0:1, 512:512 + T], onesb_c_[:],
                                   amx[:, 0:T], start=(dg == 0),
                                   stop=(dg == DG - 1)))
                sq.append((m1, m2))
            em.w("vector", "T", sq[-1][1])
            em.e("vector", lambda e, T=T: e.tensor_scalar(
                rows_ln[0:1, 0:T], ps_r[0:1, 0:T], 1.0 / D, None, ALU.mult))
            em.e("vector", lambda e, T=T: e.tensor_scalar(
                rows_ln[0:1, 512:512 + T], ps_r[0:1, 512:512 + T], 1.0 / D,
                None, ALU.mult))
            em.w("vector", "V")
            em.e("vector", lambda e, T=T: e.tensor_tensor(
                am1[0:1, 0:T], rows_ln[0:1, 0:T], rows_ln[0:1, 0:T],
                ALU.mult))
            em.w("vector", "V")
            mv = em.e("vector", lambda e, T=T: e.tensor_tensor(
                rows_ln[0:1, 512:512 + T], rows_ln[0:1, 512:512 + T],
                am1[0:1, 0:T], ALU.subtract))
            em.w("scalar", "V", mv)
            em.e("scalar", lambda e, T=T: e.activation(
                rows_ln[0:1, 512:512 + T], rows_ln[0:1, 512:512 + T], AF.Ln,
                bias=cst[0:1, 1:2]))
            em.w("scalar", "S")
            ms = em.e("scalar", lambda e, T=T: e.activation(
                rows_ln[0:1, 512:512 + T], rows_ln[0:1, 512:512 + T], AF.Exp,
                scale=cst[0:1, 2:3]))
            em.w("vector", "S", ms)
            mv = em.e("vector", lambda e, T=T: e.scalar_tensor_tensor(
                rows_ln[0:1, 0:T], rows_ln[0:1, 0:T], -1.0,
                rows_ln[0:1, 512:512 + T], ALU.mult, ALU.mult))
            em.w("tensor", "V", mv)
            em.w("tensor", "S", ms)
            em.e("tensor", lambda e, T=T: e.matmul(
                ps_bc[:, 0:T], onesf_r[0:1, :], rows_ln[0:1, 512:512 + T],
                start=True, stop=True))
            mt = em.e("tensor", lambda e, T=T: e.matmul(
                ps_bc[:, 512:512 + T], onesf_r[0:1, :], rows_ln[0:1, 0:T],
                start=True, stop=True))
            em.w("vector", "T", mt)
            for dg in range(DG):
                em.e("vector", lambda e, dg=dg, T=T: e.tensor_tensor(
                    ANa[:, dg * T:(dg + 1) * T], AXa[:, dg * T:(dg + 1) * T],
                    ps_bc[:, 0:T], ALU.mult))
            em.w("vector", "V")
            for dg in range(DG):
                em.e("vector", lambda e, dg=dg, T=T: e.tensor_tensor(
                    ANa[:, dg * T:(dg + 1) * T], ANa[:, dg * T:(dg + 1) * T],
                    ps_bc[:, 512:512 + T], ALU.add))
            return em.mark("V")

        def emit_bfly(em, arena, ntile, T, steps=None):
            nbits = ntile.bit_length() - 1
            if steps is None:
                steps = [1 << b for b in range(nbits)]
            for step in steps:
                em.w("vector", "V")
                for c0 in range(ntile):
                    if c0 & step:
                        continue
                    c1 = c0 + step
                    em.e("vector", lambda e, c0=c0, c1=c1, T=T:
                         e.tensor_tensor(
                             arena[:, c0 * T:(c0 + 1) * T],
                             arena[:, c0 * T:(c0 + 1) * T],
                             arena[:, c1 * T:(c1 + 1) * T], ALU.add))
                em.w("vector", "V")
                for c0 in range(ntile):
                    if c0 & step:
                        continue
                    c1 = c0 + step
                    em.e("vector", lambda e, c0=c0, c1=c1, T=T:
                         e.scalar_tensor_tensor(
                             arena[:, c1 * T:(c1 + 1) * T],
                             arena[:, c1 * T:(c1 + 1) * T], -2.0,
                             arena[:, c0 * T:(c0 + 1) * T], ALU.mult,
                             ALU.add))
            return em.mark("V")

        def emit_h128(em, arena, ntile, T, pp0=None, pp1=None):
            pp0 = pp0 if pp0 is not None else ps_mm
            pp1 = pp1 if pp1 is not None else ps_mm2
            em.w("tensor", "V")
            em.w("tensor", "S")
            evs = []
            for c in range(ntile):
                pp = pp0 if c % 2 == 0 else pp1
                if c >= 2:
                    em.w("tensor", "S", evs[c - 2])
                mt = em.e("tensor", lambda e, c=c, pp=pp, T=T: e.matmul(
                    pp[:, 0:T], H128f[:], arena[:, c * T:(c + 1) * T],
                    start=True, stop=True))
                em.w("scalar", "T", mt)
                evs.append(em.e("scalar", lambda e, c=c, pp=pp, T=T: e.copy(
                    arena[:, c * T:(c + 1) * T], pp[:, 0:T])))
            return evs[-1]

        def emit_colbcast(em, colsrc, c0, nb, vmark, pst=None, psc=None,
                          rq0=0):
            pst = pst if pst is not None else ps_t
            psc = psc if psc is not None else ps_bc
            em.w("tensor", "V", vmark)
            mt = 0
            for b in range(nb):
                mt = em.e("tensor", lambda e, b=b, pst=pst: e.transpose(
                    pst[0:1, b * 128:(b + 1) * 128],
                    colsrc[:, c0 + b:c0 + b + 1], identf[:]))
            em.w("vector", "T", mt)
            mv = em.e("vector", lambda e, nb=nb, pst=pst, rq0=rq0:
                      e.tensor_copy(rows_q[0:1, rq0:rq0 + nb * 128],
                                    pst[0:1, 0:nb * 128]))
            em.w("tensor", "V", mv)
            for b in range(nb):
                mt = em.e("tensor", lambda e, b=b, psc=psc, rq0=rq0:
                          e.matmul(
                              psc[:, b * 128:(b + 1) * 128], onesf_r[0:1, :],
                              rows_q[0:1, rq0 + b * 128:rq0 + (b + 1) * 128],
                              start=True, stop=True))
            return mt

        def emit_absmax(em, arena, ntile, T, clipmin, smark, pst=None,
                        psc=None, mcol=0, mx0=None, mx1=None, rq0=0):
            pst = pst if pst is not None else ps_t
            psc = psc if psc is not None else ps_bc
            mx0 = mx0 if mx0 is not None else mxa
            mx1 = mx1 if mx1 is not None else mxb
            nb = T // 128
            em.w("vector", "S", smark)
            em.e("vector", lambda e, T=T: e.tensor_tensor(
                mx0[:, 0:T], arena[:, 0:T], arena[:, T:2 * T], ALU.max))
            em.e("vector", lambda e, T=T: e.tensor_tensor(
                mx1[:, 0:T], arena[:, 0:T], arena[:, T:2 * T], ALU.min))
            for c in range(2, ntile):
                em.w("vector", "V")
                em.e("vector", lambda e, c=c, T=T: e.tensor_tensor(
                    mx0[:, 0:T], mx0[:, 0:T], arena[:, c * T:(c + 1) * T],
                    ALU.max))
                em.e("vector", lambda e, c=c, T=T: e.tensor_tensor(
                    mx1[:, 0:T], mx1[:, 0:T], arena[:, c * T:(c + 1) * T],
                    ALU.min))
            em.w("vector", "V")
            mv = em.e("vector", lambda e, T=T: e.scalar_tensor_tensor(
                mx0[:, 0:T], mx1[:, 0:T], -1.0, mx0[:, 0:T], ALU.mult,
                ALU.max))
            em.w("tensor", "V", mv)
            mt = 0
            for b in range(nb):
                mt = em.e("tensor", lambda e, b=b, pst=pst: e.transpose(
                    pst[:, b * 128:(b + 1) * 128],
                    mx0[:, b * 128:(b + 1) * 128], identf[:]))
            em.w("vector", "T", mt)
            for b in range(nb):
                em.e("vector", lambda e, b=b, pst=pst, mcol=mcol:
                     e.tensor_reduce(
                         mca[:, mcol + b:mcol + b + 1],
                         pst[:, b * 128:(b + 1) * 128], AX_,
                         ALU.max, apply_absolute_value=True))
            em.w("vector", "V")
            em.e("vector", lambda e, nb=nb, clipmin=clipmin, mcol=mcol:
                 e.tensor_scalar(mca[:, mcol:mcol + nb],
                                 mca[:, mcol:mcol + nb], clipmin, None,
                                 ALU.max))
            em.w("vector", "V")
            em.e("vector", lambda e, nb=nb, mcol=mcol: e.reciprocal(
                mca[:, 8 + mcol:8 + mcol + nb], mca[:, mcol:mcol + nb]))
            em.w("vector", "V")
            mv = em.e("vector", lambda e, nb=nb, mcol=mcol: e.tensor_scalar(
                mca[:, 8 + mcol:8 + mcol + nb],
                mca[:, 8 + mcol:8 + mcol + nb], QB, None, ALU.mult))
            return emit_colbcast(em, mca, 8 + mcol, nb, mv, pst=pst, psc=psc,
                                 rq0=rq0)

        def emit_h128tree(em, arena, ntile, T, pp0, pp1, mx0, mx1):
            em.w("tensor", "V")
            em.w("tensor", "S")
            evs = []
            for c in range(ntile):
                pp = pp0 if c % 2 == 0 else pp1
                if c >= 2:
                    em.w("tensor", "S", evs[c - 2])
                mt = em.e("tensor", lambda e, c=c, pp=pp, T=T: e.matmul(
                    pp[:, 0:T], H128f[:], arena[:, c * T:(c + 1) * T],
                    start=True, stop=True))
                em.w("scalar", "T", mt)
                evs.append(em.e("scalar", lambda e, c=c, pp=pp, T=T: e.copy(
                    arena[:, c * T:(c + 1) * T], pp[:, 0:T])))
                if c >= 1:
                    em.w("vector", "S", evs[c])
                    em.w("vector", "V")
                    if c == 1:
                        em.e("vector", lambda e, T=T, mx0=mx0:
                             e.tensor_tensor(mx0[:, 0:T], arena[:, 0:T],
                                             arena[:, T:2 * T], ALU.max))
                        em.e("vector", lambda e, T=T, mx1=mx1:
                             e.tensor_tensor(mx1[:, 0:T], arena[:, 0:T],
                                             arena[:, T:2 * T], ALU.min))
                    else:
                        em.e("vector", lambda e, c=c, T=T, mx0=mx0:
                             e.tensor_tensor(
                                 mx0[:, 0:T], mx0[:, 0:T],
                                 arena[:, c * T:(c + 1) * T], ALU.max))
                        em.e("vector", lambda e, c=c, T=T, mx1=mx1:
                             e.tensor_tensor(
                                 mx1[:, 0:T], mx1[:, 0:T],
                                 arena[:, c * T:(c + 1) * T], ALU.min))
            return evs[-1]

        def emit_absmax_tail(em, T, clipmin, pst, psc, mcol, rq0, mx0, mx1):
            nb = T // 128
            em.w("vector", "V")
            mv = em.e("vector", lambda e, T=T, mx0=mx0, mx1=mx1:
                      e.scalar_tensor_tensor(
                          mx0[:, 0:T], mx1[:, 0:T], -1.0, mx0[:, 0:T],
                          ALU.mult, ALU.max))
            em.w("tensor", "V", mv)
            mt = 0
            for b in range(nb):
                mt = em.e("tensor", lambda e, b=b, pst=pst, mx0=mx0:
                          e.transpose(pst[:, b * 128:(b + 1) * 128],
                                      mx0[:, b * 128:(b + 1) * 128],
                                      identf[:]))
            em.w("vector", "T", mt)
            for b in range(nb):
                em.e("vector", lambda e, b=b, pst=pst, mcol=mcol:
                     e.tensor_reduce(
                         mca[:, mcol + b:mcol + b + 1],
                         pst[:, b * 128:(b + 1) * 128], AX_,
                         ALU.max, apply_absolute_value=True))
            em.w("vector", "V")
            em.e("vector", lambda e, nb=nb, clipmin=clipmin, mcol=mcol:
                 e.tensor_scalar(mca[:, mcol:mcol + nb],
                                 mca[:, mcol:mcol + nb], clipmin, None,
                                 ALU.max))
            em.w("vector", "V")
            em.e("vector", lambda e, nb=nb, mcol=mcol: e.reciprocal(
                mca[:, 8 + mcol:8 + mcol + nb], mca[:, mcol:mcol + nb]))
            em.w("vector", "V")
            mv = em.e("vector", lambda e, nb=nb, mcol=mcol: e.tensor_scalar(
                mca[:, 8 + mcol:8 + mcol + nb],
                mca[:, 8 + mcol:8 + mcol + nb], QB, None, ALU.mult))
            return emit_colbcast(em, mca, 8 + mcol, nb, mv, pst=pst, psc=psc,
                                 rq0=rq0)

        def emit_quant(em, arena, qdst, ntile, T, bcmark, psc=None,
                       qams=None):
            psc = psc if psc is not None else ps_bc
            qams = qams if qams is not None else (am1, am2)
            em.w("vector", "T", bcmark)
            em.w("vector", "V")
            clm = {}
            for c0 in range(0, ntile, 2):
                pair = [c0] + ([c0 + 1] if c0 + 1 < ntile else [])
                mm_ = {}
                rr_ = {}
                for c in pair:
                    if c - 2 in clm:
                        em.w("vector", "V", clm[c - 2])
                    mm_[c] = em.e(
                        "vector", lambda e, c=c, T=T, psc=psc,
                        q=qams[c0 % 2 if False else (c % 2)]:
                        e.tensor_tensor(
                            q[:, 0:T], arena[:, c * T:(c + 1) * T],
                            psc[:, 0:T], ALU.mult))
                for c in pair:
                    em.w("vector", "V", mm_[c])
                    rr_[c] = em.e(
                        "vector", lambda e, c=c, T=T, q=qams[c % 2]:
                        e.tensor_scalar(q[:, 0:T], q[:, 0:T], RC, RC,
                                        ALU.add, ALU.subtract))
                for c in pair:
                    em.w("vector", "V", rr_[c])
                    clm[c] = em.e(
                        "vector", lambda e, c=c, T=T, q=qams[c % 2]:
                        e.tensor_scalar(qdst[:, c * T:(c + 1) * T],
                                        q[:, 0:T], QB, -QB, ALU.min,
                                        ALU.max))
            return em.mark("V")

        def pa(em):
            T = TCA
            AXa = SLA
            ANa = SLB
            R2a = SLR
            R2v = SLR.bitcast(BF16)
            Qa = SLQ.bitcast(BF16)
            Q2a = R2v
            Ga = SLG.bitcast(BF16)
            wupb = SLW.bitcast(BF16)
            nb = T // 128
            dn = SLR[:, 8192:8192 + T]
            dng = SLR[:, 8192 + T:8192 + 2 * T]
            bbeta = SLR[:, 8192 + 2 * T:8192 + 3 * T]
            st = {"qm": 0, "qm2": 0, "lnap": 0, "bbm": 0, "gplast": 0,
                  "sillast": 0, "dncp": [0, 0], "rel2": 0, "ldm": 0,
                  "bcmt": 0}
            last_wup = [0, 0, 0, 0]
            last_wdn = [0, 0]
            x0 = 0 if multi else (n_half * L_own)

            seq = []
            for g in range(4):
                seq += [g, g + 16, g + 8, g + 24, g + 4, g + 20, g + 12,
                        g + 28]
            S_desc = (16, 8, 4, 2)
            GPLVL = 0
            lvl = {}
            sched = []
            for f in seq:
                lvl[f] = 0
                ops = []
                prog = True
                while prog:
                    prog = False
                    for li, s in enumerate(S_desc):
                        for c0 in range(FG):
                            c1 = c0 + s
                            if (c0 & s) == 0 and c1 < FG and \
                                    lvl.get(c0) == li and lvl.get(c1) == li:
                                ops.append((c0, c1, li))
                                lvl[c0] = li + 1
                                lvl[c1] = li + 1
                                prog = True
                sched.append(ops)
            assert all(v == len(S_desc) for v in lvl.values())

            def issue_ldx(em, k):
                t0 = k * T
                em.w("sync", "V", st["lnap"])
                if multi:
                    em.cnt["ldx"] += 16
                    if em.active == "sync":
                        for arm in em.engines["sync"].switch_core_id(n=128):
                            h = arm.logical % 2
                            o0 = h * L_own
                            em.engines["sync"].dma_start(
                                out=AXa[:, 0:DG * T].rearrange(
                                    "p (g t) -> p g t", g=DG),
                                in_=x_fm[:, o0 + t0:o0 + t0 + T].rearrange(
                                    "(g p) t -> p g t", p=128)).then_inc(
                                em.sems["ldx"], 16)
                    st["ldm"] = em.mark("ldx")
                else:
                    st["ldm"] = fm_load(AXa, x_fm[:, x0 + t0:x0 + t0 + T],
                                        DG, T, "ldx")

            def a_stats(em, k):
                st["bcmt"] = emit_ln_rows(em, AXa, T, st["ldm"], "ldx")

            def a_apply(em, k):
                em.w("vector", "T", st["bcmt"])
                for dg in range(DG):
                    em.e("vector", lambda e, dg=dg: e.tensor_tensor(
                        ANa[:, dg * T:(dg + 1) * T],
                        AXa[:, dg * T:(dg + 1) * T],
                        ps_bc[:, 0:T], ALU.mult))
                st["lnap"] = em.mark("V")
                em.w("vector", "V")
                for dg in range(DG):
                    em.e("vector", lambda e, dg=dg: e.tensor_tensor(
                        ANa[:, dg * T:(dg + 1) * T],
                        ANa[:, dg * T:(dg + 1) * T],
                        ps_bc[:, 512:512 + T], ALU.add))
                if "xn" in dbg and k == 0:
                    em.w("sync", "V")
                    fm_store(dbg["xn"], ANa, DG, T, "std")
                    em.barrier()

            def a_bfly(em, k):
                emit_bfly(em, ANa, DG, T)

            def a_h128(em, k):
                st["sm_a"] = emit_h128tree(em, ANa, DG, T, ps_t, ps_x,
                                           am3, am2)
                if "y2" in dbg and k == 0:
                    em.w("sync", "S")
                    fm_store(dbg["y2"], ANa, DG, T, "std")
                    em.barrier()

            def a_tail(em, k):
                em.w("vector", "G")
                bcm = emit_absmax_tail(em, T, EPS_Q * SQ_D, ps_t, ps_bc,
                                       0, 0, am3, am2)
                em.w("vector", "V")
                em.e("vector", lambda e: e.tensor_scalar(
                    acol1[:, 0:nb], mca[:, 0:nb], supc[:],
                    1.0 / (QB * SQ_D), ALU.mult, ALU.mult))
                st["qm"] = emit_quant(em, ANa, Qa, DG, T, bcm,
                                      qams=(mxb, am3))
                if "q1" in dbg and k == 0:
                    em.w("sync", "V")
                    fm_store(dbg["q1"], Qa, DG, T, "std")
                    em.barrier()

            def part_up(em, k):
                nxt = k + 1 < nch_a
                if nxt:
                    issue_ldx(em, k + 1)
                rel = []
                tl = {}
                tupv = tupQ.rearrange("p (g x) -> p g x", g=FG)
                for pos in range(FG):
                    fb = seq[pos]
                    j = pos % 4
                    wsl = wupb[:, j * 2048:(j + 1) * 2048]
                    if pos % 2 == 0:
                        assert seq[pos + 1] == fb + 16
                        q = (pos // 2) % 2
                        wsem = ("lwa", "lwb")[q]
                        wm = max(last_wup[2 * q], last_wup[2 * q + 1])
                        if q == 1:
                            wm = max(wm, last_wdn[0])
                        em.w("sync", "T", wm)
                        lw = em.dma(
                            "sync",
                            wupb[:, 2 * q * 2048:(2 * q + 2) * 2048]
                            .rearrange("p (g x) -> p g x", g=2),
                            tupv[:, fb:fb + 17:16, :], wsem)
                        em.w("tensor", wsem, lw)
                    if pos == 0:
                        em.w("tensor", "V", st["qm"])
                        em.w("tensor", "S", st["dncp"][0])
                    if pos == 1:
                        em.w("tensor", "S", st["dncp"][1])
                    if pos >= 2:
                        em.w("tensor", "S", rel[pos - 2][0])
                    pp = ps_mm if pos % 2 == 0 else ps_mm2
                    for dgk in range(DG):
                        mt = em.e("tensor",
                                  lambda e, dgk=dgk, pp=pp, w=wsl, T=T:
                                  e.matmul(
                                      pp[:, 0:T],
                                      w[:, dgk * 128:(dgk + 1) * 128],
                                      Qa[:, dgk * T:(dgk + 1) * T],
                                      start=(dgk == 0),
                                      stop=(dgk == DG - 1)))
                    last_wup[j] = mt
                    em.w("scalar", "T", mt)
                    if pos == 0:
                        em.w("scalar", "S")
                        em.w("scalar", "V", st["qm2"])
                        em.w("scalar", "G", st["gplast"])
                    amx = am1 if pos % 2 == 0 else am2
                    m1 = em.e("scalar", lambda e, pp=pp, amx=amx, T=T:
                              e.activation(amx[:, 0:T], pp[:, 0:T], AF.Relu))
                    em.w("scalar", "S", m1)
                    m2 = em.e("scalar", lambda e, fb=fb, amx=amx, T=T:
                              e.activation(R2a[:, fb * T:(fb + 1) * T],
                                           amx[:, 0:T], AF.Square))
                    rel.append((m1, m2))
                    for c0, c1, li in sched[pos]:
                        if li < GPLVL:
                            em.w("gpsimd", "S", m2)
                            em.w("gpsimd", "G")
                            em.e("gpsimd", lambda e, c0=c0, c1=c1, T=T:
                                 e.tensor_tensor(
                                     mxb[:, 0:T],
                                     R2a[:, c0 * T:(c0 + 1) * T],
                                     R2a[:, c1 * T:(c1 + 1) * T],
                                     ALU.subtract))
                            em.w("gpsimd", "G")
                            em.e("gpsimd", lambda e, c0=c0, c1=c1, T=T:
                                 e.tensor_tensor(
                                     R2a[:, c0 * T:(c0 + 1) * T],
                                     R2a[:, c0 * T:(c0 + 1) * T],
                                     R2a[:, c1 * T:(c1 + 1) * T], ALU.add))
                            em.w("gpsimd", "G")
                            gm = em.e("gpsimd",
                                      lambda e, c1=c1, T=T:
                                      e.tensor_copy(
                                          R2a[:, c1 * T:(c1 + 1) * T],
                                          mxb[:, 0:T]))
                            tl[c0] = tl[c1] = gm
                        else:
                            em.w("vector", "S", m2)
                            for t in (c0, c1):
                                if t in tl:
                                    em.w("vector", "G", tl.pop(t))
                            em.w("vector", "V")
                            em.e("vector", lambda e, c0=c0, c1=c1, T=T:
                                 e.tensor_tensor(
                                     R2a[:, c0 * T:(c0 + 1) * T],
                                     R2a[:, c0 * T:(c0 + 1) * T],
                                     R2a[:, c1 * T:(c1 + 1) * T], ALU.add))
                            em.w("vector", "V")
                            em.e("vector", lambda e, c0=c0, c1=c1, T=T:
                                 e.scalar_tensor_tensor(
                                     R2a[:, c1 * T:(c1 + 1) * T],
                                     R2a[:, c1 * T:(c1 + 1) * T], -2.0,
                                     R2a[:, c0 * T:(c0 + 1) * T], ALU.mult,
                                     ALU.add))
                    if nxt and pos == 7:
                        a_stats(em, k + 1)
                    if nxt and pos == 15:
                        a_apply(em, k + 1)
                st["rel2"] = rel[-1][1]

            def part_b(em, k):
                if "r2" in dbg and k == 0:
                    em.w("sync", "S", st["rel2"])
                    fm_store(dbg["r2"], R2a, FG, T, "std")
                    em.barrier()
                em.w("vector", "S", st["rel2"])
                em.w("vector", "G")
                emit_bfly(em, R2a, FG, T, steps=[1])
                emit_h128tree(em, R2a, FG, T, ps_y, ps_y2, mxa, mxb)
                bcm = emit_absmax_tail(em, T, 1e-30, ps_y, ps_y2, 4, 512,
                                       mxa, mxb)
                em.w("vector", "V")
                em.e("vector", lambda e: e.tensor_tensor(
                    bcol1[:, 0:nb], acol1[:, 0:nb], acol1[:, 0:nb],
                    ALU.mult))
                em.e("vector", lambda e: e.tensor_scalar(
                    coltmp[:, 0:1], sdnc[:], EPS_Q / QB, None, ALU.mult))
                em.w("vector", "V")
                em.e("vector", lambda e: e.tensor_tensor(
                    bcol1[:, 0:nb], bcol1[:, 0:nb], mca[:, 4:4 + nb],
                    ALU.mult))
                em.w("vector", "V")
                em.e("vector", lambda e: e.tensor_scalar(
                    bcol1[:, 0:nb], bcol1[:, 0:nb], sdnc[:],
                    1.0 / (QB * 64.0), ALU.mult, ALU.mult))
                em.w("vector", "V")
                for b in range(nb):
                    em.e("vector", lambda e, b=b: e.tensor_tensor(
                        bcol1[:, b:b + 1], bcol1[:, b:b + 1], coltmp[:, 0:1],
                        ALU.max))
                st["qm2"] = emit_quant(em, R2a, Q2a, FG, T, bcm,
                                       psc=ps_y2, qams=(am1, mxa))
                if "q2" in dbg and k == 0:
                    em.w("sync", "V")
                    fm_store(dbg["q2"], Q2a, FG, T, "std")
                    em.barrier()
                bmt = emit_colbcast(em, bcol1, 0, nb, em.mark("V"),
                                    pst=ps_y, psc=ps_y2, rq0=512)
                em.w("scalar", "T", bmt)
                em.w("scalar", "V", st["qm2"])
                em.w("scalar", "S")
                em.w("scalar", "G")
                st["bbm"] = em.e("scalar", lambda e: e.copy(
                    bbeta, ps_y2[:, 0:T]))

            def part_dn(em, k, mids=None):
                t0 = k * T
                cpy = [0] * DG
                gpm = [0] * DG
                sil = [0] * DG
                for eb in range(DG):
                    if mids and eb in mids:
                        mids[eb]()
                    wsl = SLW.bitcast(BF16)[:, 4096 + (eb % 2) * 4096:
                                            8192 + (eb % 2) * 4096]
                    wsem = "lwc" if eb % 2 == 0 else "lwd"
                    wm = last_wdn[eb % 2]
                    if eb == 0:
                        wm = max(wm, last_wup[2], last_wup[3])
                    em.w("sync", "T", wm)
                    lw = em.dma("sync", wsl,
                                tdownQ[:, eb * FFDIM:(eb + 1) * FFDIM],
                                wsem)
                    em.w("tensor", wsem, lw)
                    if eb == 0:
                        em.w("tensor", "V", st["qm2"])
                    if eb >= 2:
                        em.w("tensor", "S", cpy[eb - 2])
                    pp = ps_mm if eb % 2 == 0 else ps_mm2
                    for fgk in range(FG):
                        mt = em.e("tensor",
                                  lambda e, fgk=fgk, pp=pp, w=wsl, T=T:
                                  e.matmul(
                                      pp[:, 0:T],
                                      w[:, fgk * 128:(fgk + 1) * 128],
                                      Q2a[:, fgk * T:(fgk + 1) * T],
                                      start=(fgk == 0),
                                      stop=(fgk == FG - 1)))
                    last_wdn[eb % 2] = mt
                    em.w("scalar", "T", mt)
                    em.w("scalar", "S", st["bbm"])
                    if eb > 0:
                        em.w("scalar", "G", gpm[eb - 1])
                    else:
                        em.w("scalar", "G")
                        em.w("gpsimd", "V", st["qm2"])
                    cpy[eb] = em.e("scalar", lambda e, pp=pp, T=T: e.copy(
                        dn, pp[:, 0:T]))
                    em.w("gpsimd", "S", cpy[eb])
                    em.w("gpsimd", "G")
                    gpm[eb] = em.e("gpsimd", lambda e: e.tensor_tensor(
                        dng, dn, bbeta, ALU.mult))
                    em.w("scalar", "G", gpm[eb])
                    hslot = eb % 8
                    if hslot == 0:
                        em.w("scalar", "stg")
                    sil[eb] = em.e("scalar",
                                   lambda e, hslot=hslot, T=T: e.activation(
                                       Ga[:, hslot * T:(hslot + 1) * T],
                                       dng, AF.Silu))
                    if hslot == 7:
                        em.w("sync", "S", sil[eb])
                        half = eb // 8
                        em.dma("sync",
                               g_all[half * 1024:(half + 1) * 1024,
                                     t0:t0 + T].rearrange(
                                   "(g p) t -> p g t", p=128),
                               Ga[:, 0:8 * T].rearrange("p (g t) -> p g t",
                                                        g=8), "stg")
                st["gplast"] = gpm[-1]
                st["sillast"] = sil[-1]
                st["dncp"] = [cpy[DG - 2], cpy[DG - 1]]

            issue_ldx(em, 0)
            a_stats(em, 0)
            a_apply(em, 0)
            a_bfly(em, 0)
            a_h128(em, 0)
            a_tail(em, 0)
            for k in range(nch_a):
                part_up(em, k)
                part_b(em, k)
                kk = k + 1
                if kk < nch_a:
                    part_dn(em, k, mids={
                        2: (lambda kk=kk: a_bfly(em, kk)),
                        8: (lambda kk=kk: a_h128(em, kk)),
                        12: (lambda kk=kk: a_tail(em, kk))})
                else:
                    part_dn(em, k)
            em.barrier()

        def xnb_store(em, t0, T, XB, xbm):
            em.w("sync", "V", xbm)
            em.w("sync", "stx")
            if multi:
                em.cnt["stx"] += 16
                if em.active == "sync":
                    for arm in em.engines["sync"].switch_core_id(n=128):
                        h = arm.logical % 2
                        own0 = h * L_own
                        if own0 <= t0 < own0 + L_own:
                            dst = xnb_own[:, t0 - own0:t0 - own0 + T]
                        else:
                            dst = dsink[:, 0:T]
                        em.engines["sync"].dma_start(
                            out=dst.rearrange("(g p) t -> p g t", p=128),
                            in_=XB[:, 0:DG * T].rearrange(
                                "p (g t) -> p g t", g=DG)).then_inc(
                            em.sems["stx"], 16)
            else:
                own0 = n_half * L_own
                if own0 <= t0 < own0 + L_own:
                    fm_store(xnb_own[:, t0 - own0:t0 - own0 + T], XB, DG, T,
                             "stx")

        def pb(em):
            T = TCB
            AXa = SLA
            XB = SLQ.bitcast(BF16)
            SLRb = SLR.bitcast(BF16)
            SLBb = SLB.bitcast(BF16)
            DLT = [SLRb[:, 0:DG * T], SLRb[:, DG * 512:DG * 512 + DG * T]]
            UU = [SLRb[:, 2 * DG * 512:2 * DG * 512 + DG * T],
                  SLRb[:, 3 * DG * 512:3 * DG * 512 + DG * T]]
            BCB = [SLBb[:, 0:16 * T], SLBb[:, 16 * 512:16 * 512 + 16 * T]]
            DW = SLG.bitcast(BF16)
            sww = SLW.bitcast(BF16)
            dtl = sww[:, 0:512]
            yev = [sww[:, 512:1024], sww[:, 1024:1536]]
            scr = [sww[:, 1536:2048], sww[:, 2048:2560]]
            wbg = [sww[:, 2560 + i * 512:3072 + i * 512] for i in range(3)]
            bcrow1 = SLA.bitcast(BF16)
            em.w("vector", "V")
            em.e("vector", lambda e: e.memset(statec[:], 0.0))
            prep_marks = {}

            def prep(k):
                q = k % 2
                t0 = k * T
                em.w("sync", "V")
                em.w("sync", "T")
                em.w("sync", "S")
                ldm = fm_load(AXa, x_fm[:, t0:t0 + T], DG, T, "ldx")
                emit_ln_rows(em, AXa, T, ldm, "ldx")
                em.w("vector", "stx")
                mb0 = em.e("vector", lambda e, T=T: e.tensor_copy(
                    scr[0][:, 0:T], ps_bc[:, 0:T]))
                mb1 = em.e("vector", lambda e, T=T: e.tensor_copy(
                    scr[1][:, 0:T], ps_bc[:, 512:512 + T]))
                for dg in range(DG):
                    em.w("vector", "V")
                    em.e("vector", lambda e, dg=dg, T=T: e.tensor_copy(
                        XB[:, dg * T:(dg + 1) * T],
                        AXa[:, dg * T:(dg + 1) * T]))
                    em.w("vector", "V")
                    em.e("vector", lambda e, dg=dg, T=T: e.tensor_tensor(
                        XB[:, dg * T:(dg + 1) * T],
                        XB[:, dg * T:(dg + 1) * T], scr[0][:, 0:T],
                        ALU.mult))
                    em.w("vector", "V")
                    em.e("vector", lambda e, dg=dg, T=T: e.tensor_tensor(
                        XB[:, dg * T:(dg + 1) * T],
                        XB[:, dg * T:(dg + 1) * T], scr[1][:, 0:T],
                        ALU.add))
                xbm = em.mark("V")
                xnb_store(em, t0, T, XB, xbm)
                em.w("tensor", "V", xbm)
                for dg in range(DG):
                    em.e("tensor", lambda e, dg=dg, T=T: e.matmul(
                        ps_mm[:, 0:T], xprojTb[:, dg * 144:dg * 144 + 128],
                        XB[:, dg * T:(dg + 1) * T],
                        start=(dg == 0), stop=(dg == DG - 1)))
                for dg in range(DG):
                    mt = em.e("tensor", lambda e, dg=dg, T=T: e.matmul(
                        ps_t[0:16, 0:T],
                        xprojTb[:, dg * 144 + 128:dg * 144 + 144],
                        XB[:, dg * T:(dg + 1) * T],
                        start=(dg == 0), stop=(dg == DG - 1)))
                em.w("vector", "T", mt)
                em.e("vector", lambda e, T=T: e.tensor_copy(dtl[:, 0:T],
                                                            ps_mm[:, 0:T]))
                mv = em.e("vector", lambda e, T=T: e.tensor_copy(
                    bcs16[:, 0:T], ps_t[0:16, 0:T]))
                em.w("sync", "V", mv)
                em.w("sync", "ldb")
                stbm = em.dma("sync",
                              bcscr.rearrange("p (j t) -> (p j) t", j=16)[
                                  :, 0:T],
                              bcs16[:, 0:T], "stb")
                em.w("sync", "stb", stbm)
                ldbm = em.dma("sync",
                              bcrow1[0:1, 0:16 * T].rearrange(
                                  "p (j t) -> p j t", j=16),
                              bcscr.rearrange("p (j t) -> p j t", j=16)[
                                  0:1, :, 0:T], "ldb")
                em.w("tensor", "ldb", ldbm)
                em.w("scalar", "V")
                em.w("scalar", "S")
                em.w("scalar", "G")
                bcm = []
                for j in range(16):
                    if j >= 2:
                        em.w("tensor", "S", bcm[j - 2])
                    mt = em.e("tensor", lambda e, j=j, T=T: e.matmul(
                        ps_bc[:, (j % 2) * 512:(j % 2) * 512 + T],
                        onesb_r[0:1, :], bcrow1[0:1, j * T:(j + 1) * T],
                        start=True, stop=True))
                    em.w("scalar", "T", mt)
                    bcm.append(em.e("scalar", lambda e, j=j, q=q, T=T: e.copy(
                        BCB[q][:, j * T:(j + 1) * T],
                        ps_bc[:, (j % 2) * 512:(j % 2) * 512 + T])))
                dtm = []
                for dg in range(DG):
                    pp = ps_mm2 if dg % 2 == 0 else ps_mm
                    if dg >= 2:
                        em.w("tensor", "S", dtm[dg - 2])
                    mt = em.e("tensor", lambda e, dg=dg, pp=pp, T=T:
                              e.matmul(pp[:, 0:T],
                                       dtprojTb[:, dg * 128:(dg + 1) * 128],
                                       dtl[:, 0:T], start=True, stop=True))
                    em.w("scalar", "T", mt)
                    amx = am1 if dg % 2 == 0 else am2
                    m1 = em.e("scalar", lambda e, dg=dg, pp=pp, amx=amx, T=T:
                              e.activation(amx[:, 0:T], pp[:, 0:T], AF.Exp,
                                           bias=dtbias_s[:, dg:dg + 1]))
                    em.w("scalar", "S", m1)
                    dtm.append(em.e("scalar",
                                    lambda e, dg=dg, amx=amx, q=q, T=T:
                                    e.activation(
                                        DLT[q][:, dg * T:(dg + 1) * T],
                                        amx[:, 0:T], AF.Ln,
                                        bias=cst[:, 0:1])))
                em.w("vector", "S", dtm[-1])
                em.w("vector", "G")
                for dg in range(DG):
                    em.e("vector", lambda e, dg=dg, q=q, T=T: e.tensor_tensor(
                        UU[q][:, dg * T:(dg + 1) * T],
                        DLT[q][:, dg * T:(dg + 1) * T],
                        XB[:, dg * T:(dg + 1) * T], ALU.mult))
                prep_marks[k] = (em.mark("V"), bcm[-1], dtm[-1])

            def scan(k):
                q = k % 2
                t0 = k * T
                um, bce, dte = prep_marks[k]
                em.w("vector", "V", um)
                em.w("vector", "S", bce)
                em.w("gpsimd", "V")
                em.w("gpsimd", "S", bce)
                em.w("tensor", "S")
                S = DG * NH
                tts_m = {}
                mmv_m = {}
                w_m = {}
                weng = {}
                da_m = {}
                imm_m = {}
                ev_marks = {}
                ydma = {}

                def buf(base, s):
                    par = s % 2
                    return DW[:, base + par * 512:base + par * 512 + 512]

                def emit_da(s):
                    dg, j = divmod(s, NH)
                    dab = DW[:, (s % 2) * 1024:(s % 2) * 1024 + 512]
                    if s >= 2:
                        em.w("scalar", "V", tts_m[s - 2])
                    da_m[s] = em.e("scalar", lambda e, dg=dg, j=j, dab=dab,
                                   q=q, T=T:
                                   e.activation(
                                       dab[:, 0:T],
                                       DLT[q][:, dg * T:(dg + 1) * T],
                                       AF.Exp,
                                       scale=A_s[:, dg * NH + j:
                                                 dg * NH + j + 1]))

                def emit_w(s, eng):
                    dg, j = divmod(s, NH)
                    if j >= 5:
                        wb_ = wbg[j - 5]
                        if dg >= 1:
                            em.w(eng, "V", tts_m[(dg - 1) * NH + j])
                    else:
                        wb_ = DW[:, (s % 2) * 1024 + 512:
                                 (s % 2) * 1024 + 1024]
                        if s >= 2:
                            em.w(eng, "V", tts_m[s - 2])
                    weng[s] = eng
                    w_m[s] = em.e(eng, lambda e, dg=dg, j=j, wb_=wb_, q=q,
                                  T=T:
                                  e.tensor_tensor(
                                      wb_[:, 0:T],
                                      UU[q][:, dg * T:(dg + 1) * T],
                                      BCB[q][:, j * T:(j + 1) * T],
                                      ALU.mult))

                def emit_tts(s):
                    dg, j = divmod(s, NH)
                    dab = DW[:, (s % 2) * 1024:(s % 2) * 1024 + 512]
                    if j >= 5:
                        wb_ = wbg[j - 5]
                    else:
                        wb_ = DW[:, (s % 2) * 1024 + 512:
                                 (s % 2) * 1024 + 1024]
                    hb_ = DW[:, 2048 + (s % 2) * 512:
                             2048 + (s % 2) * 512 + 512]
                    em.w("vector", "S", da_m[s])
                    if s >= 2:
                        em.w("vector", "V", mmv_m[s - 2])
                    if weng[s] == "gpsimd":
                        em.w("vector", "G", w_m[s])
                    else:
                        em.w("vector", "V", w_m[s])
                    tts_m[s] = em.e("vector",
                                    lambda e, dab=dab, wb_=wb_, hb_=hb_,
                                    dg=dg, j=j, T=T:
                                    e.tensor_tensor_scan(
                                        hb_[:, 0:T], dab[:, 0:T],
                                        wb_[:, 0:T],
                                        statec[:, dg * NH + j:
                                               dg * NH + j + 1],
                                        ALU.mult, ALU.add))

                def emit_consume(s):
                    dg, j = divmod(s, NH)
                    hb_ = DW[:, 2048 + (s % 2) * 512:
                             2048 + (s % 2) * 512 + 512]
                    mb_ = DW[:, 3072 + (s % 2) * 512:
                             3072 + (s % 2) * 512 + 512]
                    em.w("vector", "V", tts_m[s])
                    em.e("vector", lambda e, hb_=hb_, dg=dg, j=j, T=T:
                         e.tensor_copy(
                             statec[:, dg * NH + j:dg * NH + j + 1],
                             hb_[:, T - 1:T]))
                    if s >= 2:
                        em.w("vector", "T", imm_m[s - 2])
                    mm = em.e("vector",
                              lambda e, hb_=hb_, mb_=mb_, j=j, q=q, T=T:
                              e.tensor_tensor(
                                  mb_[:, 0:T], hb_[:, 0:T],
                                  BCB[q][:, (8 + j) * T:(9 + j) * T],
                                  ALU.mult))
                    mmv_m[s] = mm
                    psy = ps_y if dg % 2 == 0 else ps_y2
                    em.w("tensor", "V", mm)
                    if j == 0 and dg >= 2:
                        em.w("tensor", "S", ev_marks[dg - 2])
                    imm_m[s] = em.e(
                        "tensor", lambda e, mb_=mb_, psy=psy, j=j, T=T:
                        e.matmul(psy[:, 0:T], identb[:], mb_[:, 0:T],
                                 start=(j == 0), stop=(j == NH - 1)))
                    if j == NH - 1:
                        ysem = "sty" if dg % 2 == 0 else "sty2"
                        em.w("scalar", "T", imm_m[s])
                        em.w("scalar", ysem)
                        ev_marks[dg] = em.e(
                            "scalar", lambda e, psy=psy, dg=dg, T=T:
                            e.copy(yev[dg % 2][:, 0:T], psy[:, 0:T]))
                        em.w("sync", "S", ev_marks[dg])
                        ydma[dg] = em.dma(
                            "sync",
                            y_all[dg * 128:(dg + 1) * 128, t0:t0 + T],
                            yev[dg % 2][:, 0:T], ysem)

                for s in range(S + 2):
                    dg, j = divmod(s, NH) if s < S else (None, None)
                    if s < S and j == 1:
                        em.w("gpsimd", "G")
                        for jj in range(5, NH):
                            emit_w(dg * NH + jj, "gpsimd")
                    if s < S:
                        emit_da(s)
                        if j < 5:
                            emit_w(s, "vector")
                    if 1 <= s < S + 1:
                        emit_tts(s - 1)
                    if s >= 2:
                        emit_consume(s - 2)

            prep(0)
            for k in range(nch_b):
                if k + 1 < nch_b:
                    prep(k + 1)
                scan(k)
            if multi:
                em.w("sync", "sty")
                em.w("sync", "sty2")
                em.cnt["styp"] += 16
                if em.active == "sync":
                    for arm in em.engines["sync"].switch_core_id(n=128):
                        h = arm.logical % 2
                        po = (1 - h) * L_own
                        em.engines["sync"].dma_start(
                            out=ysh[:, h * L_own:(h + 1) * L_own],
                            in_=y_all[:, po:po + L_own]).then_inc(
                            em.sems["styp"], 16)
                em.w("gpsimd", "styp")
                em.e("gpsimd", lambda e: e.collective_compute(
                    "AllReduce", ALU.add,
                    replica_groups=[[0, 1], [2, 3], [4, 5], [6, 7]],
                    ins=[bar_i], outs=[bar_o]))
            em.barrier()

        def pc(em):
            T = TCC
            AXa = SLA
            OUT = SLB
            YT = SLQ.bitcast(BF16)
            SLRb = SLR.bitcast(BF16)
            YP = SLRb[:, 0:DG * T]
            SS = SLRb[:, DG * 512:DG * 512 + DG * T]
            XBc = SLRb[:, 2 * DG * 512:2 * DG * 512 + DG * T]
            GG = SLRb[:, 3 * DG * 512:3 * DG * 512 + DG * T]
            wob = SLW.bitcast(BF16)
            last_wo = [0, 0]
            own0 = 0 if not multi else None
            for k in range(nch_c):
                t0 = k * T
                em.w("sync", "V")
                em.w("sync", "T")
                em.w("sync", "S")
                if multi:
                    em.cnt["ldx"] += 16
                    em.cnt["ldx2"] += 16
                    if em.active == "sync":
                        for arm in em.engines["sync"].switch_core_id(n=128):
                            h = arm.logical % 2
                            o0 = h * L_own
                            em.engines["sync"].dma_start(
                                out=AXa[:, 0:DG * T].rearrange(
                                    "p (g t) -> p g t", g=DG),
                                in_=x_fm[:, o0 + t0:o0 + t0 + T].rearrange(
                                    "(g p) t -> p g t", p=128)).then_inc(
                                em.sems["ldx"], 16)
                            em.engines["sync"].dma_start(
                                out=YP[:, 0:DG * T].rearrange(
                                    "p (g t) -> p g t", g=DG),
                                in_=ysh[:, (1 - h) * L_own + t0:
                                        (1 - h) * L_own + t0 + T].rearrange(
                                    "(g p) t -> p g t", p=128)).then_inc(
                                em.sems["ldx2"], 16)
                    l1 = em.mark("ldx")
                    l2 = em.mark("ldx2")
                    l2b = fm_load(YT, y_all[:, t0:t0 + T], DG, T, "ldx2") \
                        if False else None
                    em.cnt["ldx2"] += 16
                    if em.active == "sync":
                        for arm in em.engines["sync"].switch_core_id(n=128):
                            h = arm.logical % 2
                            o0 = h * L_own
                            em.engines["sync"].dma_start(
                                out=YT[:, 0:DG * T].rearrange(
                                    "p (g t) -> p g t", g=DG),
                                in_=y_all[:, o0 + t0:o0 + t0 + T].rearrange(
                                    "(g p) t -> p g t", p=128)).then_inc(
                                em.sems["ldx2"], 16)
                    l2 = em.mark("ldx2")
                else:
                    o0 = n_half * L_own
                    l1 = fm_load(AXa, x_fm[:, o0 + t0:o0 + t0 + T], DG, T,
                                 "ldx")
                    fm_load(YP, yprt[:, t0:t0 + T], DG, T, "ldx2")
                    l2 = fm_load(YT, y_all[:, o0 + t0:o0 + t0 + T], DG, T,
                                 "ldx2")
                l3 = fm_load(XBc, xnb_own[:, t0:t0 + T], DG, T, "ldc")
                l4 = fm_load(GG, g_all[:, t0:t0 + T], DG, T, "ldc")
                em.w("vector", "ldx2", l2)
                for dg in range(DG):
                    em.e("vector", lambda e, dg=dg, T=T: e.tensor_tensor(
                        YT[:, dg * T:(dg + 1) * T],
                        YT[:, dg * T:(dg + 1) * T],
                        YP[:, dg * T:(dg + 1) * T], ALU.add))
                ytm = em.mark("V")
                em.w("vector", "ldc", l4)
                for eb in range(DG):
                    wsl = wob[:, (eb % 2) * 2048:(eb % 2) * 2048 + 2048]
                    wsem = "lwa" if eb % 2 == 0 else "lwb"
                    em.w("sync", "T", last_wo[eb % 2])
                    lw = em.dma("sync", wsl,
                                WoutTb[:, eb * D:(eb + 1) * D], wsem)
                    em.w("tensor", wsem, lw)
                    if eb == 0:
                        em.w("tensor", "V", ytm)
                    pp = ps_mm if eb % 2 == 0 else ps_mm2
                    for dgk in range(DG):
                        mt = em.e("tensor",
                                  lambda e, dgk=dgk, pp=pp, w=wsl, T=T:
                                  e.matmul(
                                      pp[:, 0:T],
                                      w[:, dgk * 128:(dgk + 1) * 128],
                                      YT[:, dgk * T:(dgk + 1) * T],
                                      start=(dgk == 0),
                                      stop=(dgk == DG - 1)))
                    last_wo[eb % 2] = mt
                    em.w("scalar", "T", mt)
                    amx = am1 if eb % 2 == 0 else am2
                    ms = em.e("scalar", lambda e, pp=pp, amx=amx, eb=eb, T=T:
                              e.activation(amx[:, 0:T], pp[:, 0:T], AF.Relu,
                                           bias=obias_s[:, eb:eb + 1]))
                    em.w("vector", "S", ms)
                    mv = em.e("vector", lambda e, eb=eb, amx=amx, T=T:
                              e.scalar_tensor_tensor(
                                  SS[:, eb * T:(eb + 1) * T],
                                  XBc[:, eb * T:(eb + 1) * T],
                                  D_s[:, eb:eb + 1], amx[:, 0:T],
                                  ALU.mult, ALU.add))
                    em.w("tensor", "V", mv)
                    em.w("scalar", "V", mv)
                if "s2" in dbg and k == 0:
                    em.w("sync", "V")
                    fm_store(dbg["s2"], SS, DG, T, "std")
                    em.barrier()
                em.w("vector", "V")
                em.w("vector", "ldx", l1)
                em.w("vector", "sto")
                for dg in range(DG):
                    em.e("vector", lambda e, dg=dg, T=T: e.tensor_tensor(
                        am3[:, 0:T], SS[:, dg * T:(dg + 1) * T],
                        GG[:, dg * T:(dg + 1) * T], ALU.mult))
                    em.w("vector", "V")
                    em.e("vector", lambda e, dg=dg, T=T: e.tensor_tensor(
                        OUT[:, dg * T:(dg + 1) * T],
                        AXa[:, dg * T:(dg + 1) * T], am3[:, 0:T], ALU.add))
                    em.w("vector", "V")
                om = em.mark("V")
                em.w("sync", "V", om)
                em.w("sync", "sto")
                fm_store(out_fm[:, t0:t0 + T], OUT, DG, T, "sto")
            em.w("sync", "sto")
            em.w("sync", "std")
            em.barrier()

        def prog(em):
            p0(em)
            pa(em)
            pb(em)
            pc(em)

        block = ctx.enter_context(nc.Block())
        em.run(block, prog)

    return nc



def _hadamard128():
    H = np.array([[1.0]], dtype=np.float32)
    while H.shape[0] < 128:
        H = np.block([[H, H], [H, -H]])
    return np.ascontiguousarray(H.astype(np.float32))


def host_inputs(x_fm, xproj, dtproj, dtbias, A_log, Dv, Wout, obias, w_up,
                w_down, half):
    d = {}
    d["x_fm"] = np.ascontiguousarray(x_fm)
    WT = xproj.T
    d["xprojT"] = np.ascontiguousarray(np.concatenate(
        [WT[:, 0:128], WT[:, 128 + 8 * half:136 + 8 * half],
         WT[:, 144 + 8 * half:152 + 8 * half]], axis=1))
    d["dtprojT"] = np.ascontiguousarray(dtproj.T)
    d["dtbias_c"] = np.ascontiguousarray(dtbias.reshape(DG, 128).T)
    d["A_cols"] = np.ascontiguousarray(
        A_log[:, 8 * half:8 * half + 8].reshape(DG, 128, NH)
        .transpose(1, 0, 2).reshape(128, DG * NH))
    d["D_cols"] = np.ascontiguousarray(Dv.reshape(DG, 128).T)
    d["WoutT"] = np.ascontiguousarray(Wout.T)
    d["obias_c"] = np.ascontiguousarray(obias.reshape(DG, 128).T)
    d["wupT"] = np.ascontiguousarray(w_up.T)
    d["wdownT"] = np.ascontiguousarray(w_down.T)
    d["H128"] = _hadamard128()
    d["ident"] = np.ascontiguousarray(np.eye(128, dtype=np.float32))
    d["ones_r"] = np.ones((1, 128), np.float32)
    return d


_nc_cache = {}


def kernel(L0_attn_hidden_states, L0_attn_x_proj_weight,
           L0_attn_dt_proj_weight, L0_attn_dt_proj_bias, L0_attn_A_log,
           L0_attn_D, L0_attn_out_proj_weight, L0_attn_out_proj_bias,
           L0_ffn_w_up, L0_ffn_w_down):
    x = np.asarray(L0_attn_hidden_states)
    B, L, _ = x.shape
    L_own = L // 2
    nc = _nc_cache.get((L, L_own))
    if nc is None:
        nc = build_program(L_all=L, L_own=L_own, TCA=512, TCB=512, TCC=512,
                           n_half=None)
        _nc_cache[(L, L_own)] = nc
    in_maps = []
    for c in range(8):
        b, h = c // 2, c % 2
        in_maps.append(host_inputs(
            x[b].T, np.asarray(L0_attn_x_proj_weight),
            np.asarray(L0_attn_dt_proj_weight),
            np.asarray(L0_attn_dt_proj_bias),
            np.asarray(L0_attn_A_log), np.asarray(L0_attn_D),
            np.asarray(L0_attn_out_proj_weight),
            np.asarray(L0_attn_out_proj_bias),
            np.asarray(L0_ffn_w_up), np.asarray(L0_ffn_w_down), half=h))
    res = run_bass_kernel_spmd(nc, in_maps, core_ids=list(range(8)))
    out = np.empty((B, L, D), np.float32)
    for c in range(8):
        b, h = c // 2, c % 2
        out[b, h * L_own:(h + 1) * L_own, :] = res.results[c]["out_fm"].T
    return out


# version 16
# speedup vs baseline: 1.0911x; 1.0074x over previous
import numpy as np
import ml_dtypes
import concourse.bass as bass
import concourse.mybir as mybir
from concourse.bass_utils import run_bass_kernel_spmd
from contextlib import ExitStack

F32 = mybir.dt.float32
BF16 = mybir.dt.bfloat16
AF = mybir.ActivationFunctionType
ALU = mybir.AluOpType
AX_ = mybir.AxisListType.X
BFNP = ml_dtypes.bfloat16

D = 2048
DG = 16
FFDIM = 4096
FG = 32
NH = 8
RC = float(1.5 * 2 ** 23)
EPS_LN = 1e-5
EPS_Q = 1e-5
QB = 127.0
SQ_D = float(np.sqrt(2048.0))

ENGS = ("sync", "scalar", "vector", "tensor", "gpsimd")
ESEM = {"scalar": "S", "vector": "V", "tensor": "T", "gpsimd": "G"}


class Em:

    def __init__(self, nc):
        self.nc = nc
        self.sems = {}
        self.cnt = {}
        self.active = None
        self.engines = {"sync": nc.sync, "scalar": nc.scalar,
                        "vector": nc.vector, "tensor": nc.tensor,
                        "gpsimd": nc.gpsimd}

    def sem(self, name, ctx):
        self.sems[name] = ctx.enter_context(self.nc.semaphore(name))
        self.cnt[name] = 0

    def e(self, eng, f, extra_sem=None):
        s = ESEM[eng]
        self.cnt[s] += 1
        if extra_sem is not None:
            self.cnt[extra_sem] += 1
        if self.active == eng:
            inst = f(self.engines[eng])
            inst.then_inc(self.sems[s], 1)
            if extra_sem is not None:
                inst.then_inc(self.sems[extra_sem], 1)
        return self.cnt[s]

    def dma(self, eng, out, in_, sem):
        self.cnt[sem] += 16
        if self.active == eng:
            self.engines[eng].dma_start(out=out, in_=in_).then_inc(
                self.sems[sem], 16)
        return self.cnt[sem]

    def w(self, eng, sem, at=None):
        tgt = self.cnt[sem] if at is None else at
        if tgt <= 0:
            return
        if self.active == eng:
            self.engines[eng].wait_ge(self.sems[sem], tgt)

    def mark(self, sem):
        return self.cnt[sem]

    def barrier(self, sems=None):
        if sems is None:
            sems = list(self.sems.keys())
        for e in ENGS:
            for s in sems:
                self.w(e, s)

    def run(self, block, fn):
        for name in ENGS:
            for k in self.cnt:
                self.cnt[k] = 0
            self.active = name
            deco = getattr(block, name)

            def body(eng, _n=name):
                fn(self)
            deco(body)


def build_program(L_all, L_own, TCA, TCB, TCC, n_half=None, debug=()):
    nc = bass.Bass("TRN2", target_bir_lowering=False)
    nch_a = L_own // TCA
    nch_b = L_all // TCB
    nch_c = L_own // TCC
    multi = n_half is None

    def din(name, shape, dt=F32):
        return nc.dram_tensor(name, shape, dt, kind="ExternalInput").ap()

    x_fm = din("x_fm", [D, L_all])
    xprojT = din("xprojT", [D, 144])
    dtprojT = din("dtprojT", [128, D])
    dtbias_c = din("dtbias_c", [128, DG])
    A_cols = din("A_cols", [128, DG * NH])
    D_cols = din("D_cols", [128, DG])
    WoutT = din("WoutT", [D, D])
    obias_c = din("obias_c", [128, DG])
    wupT = din("wupT", [D, FFDIM])
    wdownT = din("wdownT", [FFDIM, D])
    H128 = din("H128", [128, 128])
    ident = din("ident", [128, 128])
    ones_r = din("ones_r", [1, 128])

    out_fm = nc.dram_tensor("out_fm", [D, L_own], F32,
                            kind="ExternalOutput").ap()

    def scratch(name, shape, dt=BF16, dbg_key=None, **kw):
        kind = "ExternalOutput" if (dbg_key and dbg_key in debug) \
            else "Internal"
        return nc.dram_tensor(name, shape, dt, kind=kind, **kw).ap()

    tupQ = scratch("tupQ", [128, FG * D], dbg_key="tupq")
    tdownQ = scratch("tdownQ", [128, DG * FFDIM], dbg_key="tdnq")
    WoutTb = scratch("WoutTb", [128, DG * D])
    g_all = scratch("g_all", [D, L_own], dbg_key="gall")
    xnb_own = scratch("xnb_own", [D, L_own], dbg_key="xnball")
    y_all = scratch("y_all", [D, L_all], dbg_key="yall")
    dsink = nc.dram_tensor("dsink", [D, 512], BF16, kind="Internal").ap()
    bcscr = nc.dram_tensor("bcscr", [1, 8192], BF16, kind="Internal").ap()
    if multi:
        ysh = nc.dram_tensor("ysh", [D, 2 * L_own], BF16, kind="Internal",
                             addr_space="Shared").ap()
        bar_i = nc.dram_tensor("bar_i", [1, 16], F32, kind="Internal").ap()
        bar_o = nc.dram_tensor("bar_o", [1, 16], F32, kind="Internal",
                               addr_space="Local").ap()
    else:
        yprt = din("yprt", [D, L_own], BF16)

    dbg_specs = {
        "xn": ([D, TCA], F32), "y2": ([D, TCA], F32), "q1": ([D, TCA], BF16),
        "r2": ([FFDIM, TCA], F32), "q2": ([FFDIM, TCA], BF16),
        "delta": ([D, TCB], BF16), "u": ([D, TCB], BF16),
        "bc": ([16, TCB], BF16), "s2": ([D, TCC], BF16),
    }
    dbg = {k: nc.dram_tensor("dbg_" + k, *dbg_specs[k],
                             kind="ExternalOutput").ap()
           for k in debug if k in dbg_specs}

    ctx = ExitStack()
    with ctx:
        em = Em(nc)
        for s in ["V", "S", "T", "G",
                  "ldx", "ldx2", "ldc", "lwa", "lwb", "lwc", "lwd",
                  "stg", "stw", "stx", "sty", "sty2", "styp", "sto", "std",
                  "stb", "ldb"]:
            em.sem(s, ctx)

        def sb(name, shape, dt=F32):
            return ctx.enter_context(nc.sbuf_tensor(name, shape, dt))

        def psum(name, shape):
            return ctx.enter_context(nc.psum_tensor(name, shape, F32))

        xprojTb = sb("xprojTb", [128, DG * 144], BF16)
        dtprojTb = sb("dtprojTb", [128, D], BF16)
        dtbias_s = sb("dtbias_s", [128, DG])
        A_s = sb("A_s", [128, DG * NH])
        D_s = sb("D_s", [128, DG])
        obias_s = sb("obias_s", [128, DG])
        H128f = sb("H128f", [128, 128])
        identf = sb("identf", [128, 128])
        onesf_r = sb("onesf_r", [1, 128])
        onesb_r = sb("onesb_r", [1, 128], BF16)
        identb = sb("identb", [128, 128], BF16)
        onesf_c = sb("onesf_c", [128, 1])
        onesb_c_ = sb("onesb_c", [128, 1], BF16)
        supc = sb("supc", [128, 1])
        sdnc = sb("sdnc", [128, 1])
        cst = sb("cst", [128, 4])
        am1 = sb("am1", [128, 512])
        am2 = sb("am2", [128, 512])
        am3 = sb("am3", [128, 512])
        mxa = sb("mxa", [128, 512])
        mxb = sb("mxb", [128, 512])
        mca = sb("mca", [128, 16])
        acol1 = sb("acol1", [128, 8])
        bcol1 = sb("bcol1", [128, 8])
        rows_q = sb("rows_q", [1, 1024])
        rows_ln = sb("rows_ln", [1, 1024])
        rowtmp = sb("rowtmp", [1, 64])
        coltmp = sb("coltmp", [128, 4])
        statec = sb("statec", [128, 128])
        bcs16 = sb("bcs16", [16, 512], BF16)

        SLA = sb("SLA", [128, 8192])
        SLB = sb("SLB", [128, 8192])
        SLR = sb("SLR", [128, 16384])
        SLQ = sb("SLQ", [128, 4096])
        SLG = sb("SLG", [128, 2048])
        SLW = sb("SLW", [128, 6144])

        ps_mm = psum("ps_mm", [128, 512])
        ps_mm2 = psum("ps_mm2", [128, 512])
        ps_bc = psum("ps_bc", [128, 1024])
        ps_t = psum("ps_t", [128, 512])
        ps_y = psum("ps_y", [128, 512])
        ps_y2 = psum("ps_y2", [128, 512])
        ps_x = psum("ps_x", [128, 512])
        ps_r = ps_bc

        def fm_store(dst_ap, arena, ngrp, T, sem, eng="sync"):
            return em.dma(
                eng, dst_ap.rearrange("(g p) t -> p g t", p=128),
                arena[:, 0:ngrp * T].rearrange("p (g t) -> p g t", g=ngrp),
                sem)

        def fm_load(arena, src_ap, ngrp, T, sem, eng="sync"):
            return em.dma(
                eng, arena[:, 0:ngrp * T].rearrange("p (g t) -> p g t",
                                                    g=ngrp),
                src_ap.rearrange("(g p) t -> p g t", p=128), sem)

        def p0(em):
            mv = 0
            for dg in range(DG):
                em.w("sync", "V", mv)
                ld = em.dma("sync", SLA[:, 0:144],
                            xprojT[dg * 128:(dg + 1) * 128, :], "ldc")
                em.w("vector", "ldc", ld)
                mv = em.e("vector", lambda e, dg=dg: e.tensor_copy(
                    xprojTb[:, dg * 144:(dg + 1) * 144], SLA[:, 0:144]))
            em.w("sync", "V", mv)
            ld = em.dma("sync", SLA[:, 0:D], dtprojT, "ldc")
            em.w("vector", "ldc", ld)
            mv = em.e("vector", lambda e: e.tensor_copy(dtprojTb[:],
                                                        SLA[:, 0:D]))
            em.w("sync", "V", mv)
            for dst, src in [(dtbias_s, dtbias_c), (D_s, D_cols),
                             (obias_s, obias_c), (H128f, H128),
                             (identf, ident)]:
                em.dma("sync", dst[:], src, "ldc")
            em.dma("sync", onesf_r[:], ones_r, "ldc")
            ld = em.dma("sync", SLA[:, 0:DG * NH], A_cols, "ldc")
            em.w("vector", "ldc", ld)
            em.w("scalar", "ldc", ld)
            em.e("vector", lambda e: e.tensor_copy(onesb_r[:], onesf_r[:]))
            em.e("vector", lambda e: e.tensor_copy(identb[:], identf[:]))
            em.e("vector", lambda e: e.memset(onesf_c[:], 1.0))
            em.e("vector", lambda e: e.memset(onesb_c_[:], 1.0))
            em.e("vector", lambda e: e.memset(cst[:, 0:1], 1.0))
            em.e("vector", lambda e: e.memset(cst[:, 1:2], EPS_LN))
            em.e("vector", lambda e: e.memset(cst[:, 2:3], -0.5))
            ms = em.e("scalar", lambda e: e.activation(
                A_s[:], SLA[:, 0:DG * NH], AF.Exp))
            em.w("vector", "S", ms)
            em.e("vector", lambda e: e.tensor_scalar(
                A_s[:], A_s[:], -1.0, None, ALU.mult))
            if multi:
                mv = em.e("vector", lambda e: e.memset(rowtmp[0:1, 0:16],
                                                       0.0))
                em.w("sync", "V", mv)
                em.dma("sync", bar_i, rowtmp[0:1, 0:16], "stb")

            wstp = [SLA[:, 0:4096], SLA[:, 4096:8192]]
            wstbp = [SLQ.bitcast(BF16)[:, 0:4096],
                     SLQ.bitcast(BF16)[:, 4096:8192]]
            ldsems = [("ldc", "ldx"), ("sty", "sty2")]
            stsems = ["stw", "stx"]

            def wload(r, src, cols, freemark):
                p = r % 2
                h = cols // 2
                em.w("sync", "V", freemark)
                l1 = em.dma("sync", wstp[p][:, 0:h],
                            src[r * 128:(r + 1) * 128, 0:h], ldsems[p][0])
                l2 = em.dma("sync", wstp[p][:, h:cols],
                            src[r * 128:(r + 1) * 128, h:cols], ldsems[p][1])
                return l1, l2

            def quant_weight(src, dst, rows, cols, sup_col):
                nrow = rows // 128
                base = em.mark("V")
                vm = {}
                lds = {}
                for r in range(nrow):
                    lds[r] = wload(r, src, cols, vm.get(r - 2, base))
                    p = r % 2
                    wst = wstp[p]
                    em.w("vector", ldsems[p][0], lds[r][0])
                    em.w("vector", ldsems[p][1], lds[r][1])
                    c01 = coltmp[:, 0:1] if r == 0 else coltmp[:, 1:2]
                    em.w("vector", "V")
                    em.e("vector", lambda e, c01=c01, cols=cols, wst=wst:
                         e.tensor_reduce(c01, wst[:, 0:cols], AX_, ALU.add,
                                         apply_absolute_value=True))
                    em.w("vector", "V")
                    if r > 0:
                        em.e("vector", lambda e: e.tensor_tensor(
                            coltmp[:, 0:1], coltmp[:, 0:1], coltmp[:, 1:2],
                            ALU.add))
                        em.w("vector", "V")
                    vm[r] = em.mark("V")
                mv = vm[nrow - 1]
                em.w("tensor", "V", mv)
                mt = em.e("tensor", lambda e: e.matmul(
                    ps_t[0:1, 0:1], onesf_c[:], coltmp[:, 0:1],
                    start=True, stop=True))
                em.w("vector", "T", mt)
                em.w("vector", "stb")
                em.e("vector", lambda e, rows=rows, cols=cols:
                     e.tensor_scalar(
                         rowtmp[0:1, 0:1], ps_t[0:1, 0:1],
                         1.0 / (rows * cols), EPS_Q, ALU.mult, ALU.max))
                em.w("vector", "V")
                mv = em.e("vector", lambda e: e.reciprocal(
                    rowtmp[0:1, 1:2], rowtmp[0:1, 0:1]))
                em.w("tensor", "V", mv)
                mt = em.e("tensor", lambda e: e.matmul(
                    ps_y[:, 0:2], onesf_r[0:1, :], rowtmp[0:1, 0:2],
                    start=True, stop=True))
                em.w("vector", "T", mt)
                em.e("vector", lambda e: e.tensor_copy(sup_col[:],
                                                       ps_y[:, 0:1]))
                mv = em.e("vector", lambda e: e.tensor_copy(coltmp[:, 2:3],
                                                            ps_y[:, 1:2]))
                vm2 = {nrow - 1: mv}
                stm = {}
                stbase = [em.mark(s) for s in stsems]
                for r in range(nrow):
                    p = r % 2
                    wst = wstp[p]
                    wstb = wstbp[p]
                    lds[nrow + r] = wload(r, src, cols,
                                          vm2.get(nrow + r - 2, mv))
                    em.w("vector", ldsems[p][0], lds[nrow + r][0])
                    em.w("vector", ldsems[p][1], lds[nrow + r][1])
                    em.w("vector", stsems[p], stm.get(r - 2, stbase[p]))
                    em.w("vector", "V")
                    em.e("vector", lambda e, cols=cols, wst=wst:
                         e.tensor_scalar(
                             wst[:, 0:cols], wst[:, 0:cols], coltmp[:, 2:3],
                             None, ALU.mult))
                    em.w("vector", "V")
                    em.e("vector", lambda e, cols=cols, wst=wst:
                         e.tensor_scalar(
                             wst[:, 0:cols], wst[:, 0:cols], RC, RC, ALU.add,
                             ALU.subtract))
                    em.w("vector", "V")
                    qv = em.e("vector",
                              lambda e, cols=cols, wst=wst, wstb=wstb:
                              e.tensor_scalar(
                                  wstb[:, 0:cols], wst[:, 0:cols], 1.0, -1.0,
                                  ALU.min, ALU.max))
                    vm2[nrow + r] = qv
                    em.w("sync", "V", qv)
                    nblk = cols // 128
                    stm[r] = em.dma(
                        "sync",
                        dst.rearrange("p (b rest) -> p b rest", b=nblk)[
                            :, :, r * 128:(r + 1) * 128],
                        wstb[:, 0:cols].rearrange("p (b t) -> p b t",
                                                  b=nblk), stsems[p])

            quant_weight(wupT, tupQ, D, FFDIM, supc)
            quant_weight(wdownT, tdownQ, FFDIM, D, sdnc)
            mv = em.mark("V")
            vmo = {}
            ldo = {}
            sto = {}
            stbase = [em.mark(s) for s in stsems]
            for r in range(DG):
                p = r % 2
                wst = wstp[p]
                wstb = wstbp[p]
                ldo[r] = wload(r, WoutT, D, vmo.get(r - 2, mv))
                em.w("vector", ldsems[p][0], ldo[r][0])
                em.w("vector", ldsems[p][1], ldo[r][1])
                em.w("vector", stsems[p], sto.get(r - 2, stbase[p]))
                em.w("vector", "V")
                cv = em.e("vector", lambda e, wst=wst, wstb=wstb:
                          e.tensor_copy(wstb[:, 0:D], wst[:, 0:D]))
                vmo[r] = cv
                em.w("sync", "V", cv)
                sto[r] = em.dma(
                    "sync",
                    WoutTb.rearrange("p (b rest) -> p b rest", b=DG)[
                        :, :, r * 128:(r + 1) * 128],
                    wstb[:, 0:D].rearrange("p (b t) -> p b t", b=DG),
                    stsems[p])
            em.barrier()

        def emit_ln_rows(em, AXa, T, ldmark, ldsem):
            xbv = am3.bitcast(BF16)
            em.w("vector", ldsem, ldmark)
            em.w("scalar", ldsem, ldmark)
            em.w("scalar", "S")
            em.w("scalar", "V")
            em.w("vector", "V")
            sq = []
            for dg in range(DG):
                amx = (am1 if dg % 2 == 0 else am2).bitcast(BF16)
                xbs = xbv[:, (dg % 2) * 512:(dg % 2) * 512 + T]
                if dg >= 2:
                    em.w("vector", "T", sq[dg - 2][0])
                mvx = em.e("vector", lambda e, dg=dg, T=T, xbs=xbs:
                           e.tensor_copy(xbs, AXa[:, dg * T:(dg + 1) * T]))
                if dg >= 2:
                    em.w("scalar", "T", sq[dg - 2][1])
                ms = em.e("scalar", lambda e, dg=dg, amx=amx, T=T:
                          e.activation(amx[:, 0:T],
                                       AXa[:, dg * T:(dg + 1) * T],
                                       AF.Square))
                em.w("tensor", "V", mvx)
                m1 = em.e("tensor", lambda e, dg=dg, T=T, xbs=xbs: e.matmul(
                    ps_r[0:1, 0:T], onesb_c_[:], xbs,
                    start=(dg == 0), stop=(dg == DG - 1)))
                em.w("tensor", "S", ms)
                m2 = em.e("tensor", lambda e, amx=amx, dg=dg, T=T:
                          e.matmul(ps_r[0:1, 512:512 + T], onesb_c_[:],
                                   amx[:, 0:T], start=(dg == 0),
                                   stop=(dg == DG - 1)))
                sq.append((m1, m2))
            em.w("vector", "T", sq[-1][1])
            em.e("vector", lambda e, T=T: e.tensor_scalar(
                rows_ln[0:1, 0:T], ps_r[0:1, 0:T], 1.0 / D, None, ALU.mult))
            em.e("vector", lambda e, T=T: e.tensor_scalar(
                rows_ln[0:1, 512:512 + T], ps_r[0:1, 512:512 + T], 1.0 / D,
                None, ALU.mult))
            em.w("vector", "V")
            em.e("vector", lambda e, T=T: e.tensor_tensor(
                am1[0:1, 0:T], rows_ln[0:1, 0:T], rows_ln[0:1, 0:T],
                ALU.mult))
            em.w("vector", "V")
            mv = em.e("vector", lambda e, T=T: e.tensor_tensor(
                rows_ln[0:1, 512:512 + T], rows_ln[0:1, 512:512 + T],
                am1[0:1, 0:T], ALU.subtract))
            em.w("scalar", "V", mv)
            em.e("scalar", lambda e, T=T: e.activation(
                rows_ln[0:1, 512:512 + T], rows_ln[0:1, 512:512 + T], AF.Ln,
                bias=cst[0:1, 1:2]))
            em.w("scalar", "S")
            ms = em.e("scalar", lambda e, T=T: e.activation(
                rows_ln[0:1, 512:512 + T], rows_ln[0:1, 512:512 + T], AF.Exp,
                scale=cst[0:1, 2:3]))
            em.w("vector", "S", ms)
            mv = em.e("vector", lambda e, T=T: e.scalar_tensor_tensor(
                rows_ln[0:1, 0:T], rows_ln[0:1, 0:T], -1.0,
                rows_ln[0:1, 512:512 + T], ALU.mult, ALU.mult))
            em.w("tensor", "V", mv)
            em.w("tensor", "S", ms)
            em.e("tensor", lambda e, T=T: e.matmul(
                ps_bc[:, 0:T], onesf_r[0:1, :], rows_ln[0:1, 512:512 + T],
                start=True, stop=True))
            mt = em.e("tensor", lambda e, T=T: e.matmul(
                ps_bc[:, 512:512 + T], onesf_r[0:1, :], rows_ln[0:1, 0:T],
                start=True, stop=True))
            em.w("vector", "T", mt)
            return mt

        def emit_ln(em, AXa, ANa, T, ldmark, ldsem):
            xbv = am3.bitcast(BF16)
            em.w("vector", ldsem, ldmark)
            em.w("scalar", ldsem, ldmark)
            em.w("scalar", "S")
            em.w("scalar", "V")
            em.w("vector", "V")
            sq = []
            for dg in range(DG):
                amx = (am1 if dg % 2 == 0 else am2).bitcast(BF16)
                xbs = xbv[:, (dg % 2) * 512:(dg % 2) * 512 + T]
                if dg >= 2:
                    em.w("vector", "T", sq[dg - 2][0])
                mvx = em.e("vector", lambda e, dg=dg, T=T, xbs=xbs:
                           e.tensor_copy(xbs, AXa[:, dg * T:(dg + 1) * T]))
                if dg >= 2:
                    em.w("scalar", "T", sq[dg - 2][1])
                ms = em.e("scalar", lambda e, dg=dg, amx=amx, T=T:
                          e.activation(amx[:, 0:T],
                                       AXa[:, dg * T:(dg + 1) * T],
                                       AF.Square))
                em.w("tensor", "V", mvx)
                m1 = em.e("tensor", lambda e, dg=dg, T=T, xbs=xbs: e.matmul(
                    ps_r[0:1, 0:T], onesb_c_[:], xbs,
                    start=(dg == 0), stop=(dg == DG - 1)))
                em.w("tensor", "S", ms)
                m2 = em.e("tensor", lambda e, amx=amx, dg=dg, T=T:
                          e.matmul(ps_r[0:1, 512:512 + T], onesb_c_[:],
                                   amx[:, 0:T], start=(dg == 0),
                                   stop=(dg == DG - 1)))
                sq.append((m1, m2))
            em.w("vector", "T", sq[-1][1])
            em.e("vector", lambda e, T=T: e.tensor_scalar(
                rows_ln[0:1, 0:T], ps_r[0:1, 0:T], 1.0 / D, None, ALU.mult))
            em.e("vector", lambda e, T=T: e.tensor_scalar(
                rows_ln[0:1, 512:512 + T], ps_r[0:1, 512:512 + T], 1.0 / D,
                None, ALU.mult))
            em.w("vector", "V")
            em.e("vector", lambda e, T=T: e.tensor_tensor(
                am1[0:1, 0:T], rows_ln[0:1, 0:T], rows_ln[0:1, 0:T],
                ALU.mult))
            em.w("vector", "V")
            mv = em.e("vector", lambda e, T=T: e.tensor_tensor(
                rows_ln[0:1, 512:512 + T], rows_ln[0:1, 512:512 + T],
                am1[0:1, 0:T], ALU.subtract))
            em.w("scalar", "V", mv)
            em.e("scalar", lambda e, T=T: e.activation(
                rows_ln[0:1, 512:512 + T], rows_ln[0:1, 512:512 + T], AF.Ln,
                bias=cst[0:1, 1:2]))
            em.w("scalar", "S")
            ms = em.e("scalar", lambda e, T=T: e.activation(
                rows_ln[0:1, 512:512 + T], rows_ln[0:1, 512:512 + T], AF.Exp,
                scale=cst[0:1, 2:3]))
            em.w("vector", "S", ms)
            mv = em.e("vector", lambda e, T=T: e.scalar_tensor_tensor(
                rows_ln[0:1, 0:T], rows_ln[0:1, 0:T], -1.0,
                rows_ln[0:1, 512:512 + T], ALU.mult, ALU.mult))
            em.w("tensor", "V", mv)
            em.w("tensor", "S", ms)
            em.e("tensor", lambda e, T=T: e.matmul(
                ps_bc[:, 0:T], onesf_r[0:1, :], rows_ln[0:1, 512:512 + T],
                start=True, stop=True))
            mt = em.e("tensor", lambda e, T=T: e.matmul(
                ps_bc[:, 512:512 + T], onesf_r[0:1, :], rows_ln[0:1, 0:T],
                start=True, stop=True))
            em.w("vector", "T", mt)
            for dg in range(DG):
                em.e("vector", lambda e, dg=dg, T=T: e.tensor_tensor(
                    ANa[:, dg * T:(dg + 1) * T], AXa[:, dg * T:(dg + 1) * T],
                    ps_bc[:, 0:T], ALU.mult))
            em.w("vector", "V")
            for dg in range(DG):
                em.e("vector", lambda e, dg=dg, T=T: e.tensor_tensor(
                    ANa[:, dg * T:(dg + 1) * T], ANa[:, dg * T:(dg + 1) * T],
                    ps_bc[:, 512:512 + T], ALU.add))
            return em.mark("V")

        def emit_bfly(em, arena, ntile, T, steps=None):
            nbits = ntile.bit_length() - 1
            if steps is None:
                steps = [1 << b for b in range(nbits)]
            for step in steps:
                em.w("vector", "V")
                for c0 in range(ntile):
                    if c0 & step:
                        continue
                    c1 = c0 + step
                    em.e("vector", lambda e, c0=c0, c1=c1, T=T:
                         e.tensor_tensor(
                             arena[:, c0 * T:(c0 + 1) * T],
                             arena[:, c0 * T:(c0 + 1) * T],
                             arena[:, c1 * T:(c1 + 1) * T], ALU.add))
                em.w("vector", "V")
                for c0 in range(ntile):
                    if c0 & step:
                        continue
                    c1 = c0 + step
                    em.e("vector", lambda e, c0=c0, c1=c1, T=T:
                         e.scalar_tensor_tensor(
                             arena[:, c1 * T:(c1 + 1) * T],
                             arena[:, c1 * T:(c1 + 1) * T], -2.0,
                             arena[:, c0 * T:(c0 + 1) * T], ALU.mult,
                             ALU.add))
            return em.mark("V")

        def emit_h128(em, arena, ntile, T, pp0=None, pp1=None):
            pp0 = pp0 if pp0 is not None else ps_mm
            pp1 = pp1 if pp1 is not None else ps_mm2
            em.w("tensor", "V")
            em.w("tensor", "S")
            evs = []
            for c in range(ntile):
                pp = pp0 if c % 2 == 0 else pp1
                if c >= 2:
                    em.w("tensor", "S", evs[c - 2])
                mt = em.e("tensor", lambda e, c=c, pp=pp, T=T: e.matmul(
                    pp[:, 0:T], H128f[:], arena[:, c * T:(c + 1) * T],
                    start=True, stop=True))
                em.w("scalar", "T", mt)
                evs.append(em.e("scalar", lambda e, c=c, pp=pp, T=T: e.copy(
                    arena[:, c * T:(c + 1) * T], pp[:, 0:T])))
            return evs[-1]

        def emit_colbcast(em, colsrc, c0, nb, vmark, pst=None, psc=None,
                          rq0=0):
            pst = pst if pst is not None else ps_t
            psc = psc if psc is not None else ps_bc
            em.w("tensor", "V", vmark)
            mt = 0
            for b in range(nb):
                mt = em.e("tensor", lambda e, b=b, pst=pst: e.transpose(
                    pst[0:1, b * 128:(b + 1) * 128],
                    colsrc[:, c0 + b:c0 + b + 1], identf[:]))
            em.w("vector", "T", mt)
            mv = em.e("vector", lambda e, nb=nb, pst=pst, rq0=rq0:
                      e.tensor_copy(rows_q[0:1, rq0:rq0 + nb * 128],
                                    pst[0:1, 0:nb * 128]))
            em.w("tensor", "V", mv)
            for b in range(nb):
                mt = em.e("tensor", lambda e, b=b, psc=psc, rq0=rq0:
                          e.matmul(
                              psc[:, b * 128:(b + 1) * 128], onesf_r[0:1, :],
                              rows_q[0:1, rq0 + b * 128:rq0 + (b + 1) * 128],
                              start=True, stop=True))
            return mt

        def emit_absmax(em, arena, ntile, T, clipmin, smark, pst=None,
                        psc=None, mcol=0, mx0=None, mx1=None, rq0=0):
            pst = pst if pst is not None else ps_t
            psc = psc if psc is not None else ps_bc
            mx0 = mx0 if mx0 is not None else mxa
            mx1 = mx1 if mx1 is not None else mxb
            nb = T // 128
            em.w("vector", "S", smark)
            em.e("vector", lambda e, T=T: e.tensor_tensor(
                mx0[:, 0:T], arena[:, 0:T], arena[:, T:2 * T], ALU.max))
            em.e("vector", lambda e, T=T: e.tensor_tensor(
                mx1[:, 0:T], arena[:, 0:T], arena[:, T:2 * T], ALU.min))
            for c in range(2, ntile):
                em.w("vector", "V")
                em.e("vector", lambda e, c=c, T=T: e.tensor_tensor(
                    mx0[:, 0:T], mx0[:, 0:T], arena[:, c * T:(c + 1) * T],
                    ALU.max))
                em.e("vector", lambda e, c=c, T=T: e.tensor_tensor(
                    mx1[:, 0:T], mx1[:, 0:T], arena[:, c * T:(c + 1) * T],
                    ALU.min))
            em.w("vector", "V")
            mv = em.e("vector", lambda e, T=T: e.scalar_tensor_tensor(
                mx0[:, 0:T], mx1[:, 0:T], -1.0, mx0[:, 0:T], ALU.mult,
                ALU.max))
            em.w("tensor", "V", mv)
            mt = 0
            for b in range(nb):
                mt = em.e("tensor", lambda e, b=b, pst=pst: e.transpose(
                    pst[:, b * 128:(b + 1) * 128],
                    mx0[:, b * 128:(b + 1) * 128], identf[:]))
            em.w("vector", "T", mt)
            for b in range(nb):
                em.e("vector", lambda e, b=b, pst=pst, mcol=mcol:
                     e.tensor_reduce(
                         mca[:, mcol + b:mcol + b + 1],
                         pst[:, b * 128:(b + 1) * 128], AX_,
                         ALU.max, apply_absolute_value=True))
            em.w("vector", "V")
            em.e("vector", lambda e, nb=nb, clipmin=clipmin, mcol=mcol:
                 e.tensor_scalar(mca[:, mcol:mcol + nb],
                                 mca[:, mcol:mcol + nb], clipmin, None,
                                 ALU.max))
            em.w("vector", "V")
            em.e("vector", lambda e, nb=nb, mcol=mcol: e.reciprocal(
                mca[:, 8 + mcol:8 + mcol + nb], mca[:, mcol:mcol + nb]))
            em.w("vector", "V")
            mv = em.e("vector", lambda e, nb=nb, mcol=mcol: e.tensor_scalar(
                mca[:, 8 + mcol:8 + mcol + nb],
                mca[:, 8 + mcol:8 + mcol + nb], QB, None, ALU.mult))
            return emit_colbcast(em, mca, 8 + mcol, nb, mv, pst=pst, psc=psc,
                                 rq0=rq0)

        def emit_h128tree(em, arena, ntile, T, pp0, pp1, mx0, mx1):
            em.w("tensor", "V")
            em.w("tensor", "S")
            evs = []
            for c in range(ntile):
                pp = pp0 if c % 2 == 0 else pp1
                if c >= 2:
                    em.w("tensor", "S", evs[c - 2])
                mt = em.e("tensor", lambda e, c=c, pp=pp, T=T: e.matmul(
                    pp[:, 0:T], H128f[:], arena[:, c * T:(c + 1) * T],
                    start=True, stop=True))
                em.w("scalar", "T", mt)
                evs.append(em.e("scalar", lambda e, c=c, pp=pp, T=T: e.copy(
                    arena[:, c * T:(c + 1) * T], pp[:, 0:T])))
                if c >= 1:
                    em.w("vector", "S", evs[c])
                    em.w("vector", "V")
                    if c == 1:
                        em.e("vector", lambda e, T=T, mx0=mx0:
                             e.tensor_tensor(mx0[:, 0:T], arena[:, 0:T],
                                             arena[:, T:2 * T], ALU.max))
                        em.e("vector", lambda e, T=T, mx1=mx1:
                             e.tensor_tensor(mx1[:, 0:T], arena[:, 0:T],
                                             arena[:, T:2 * T], ALU.min))
                    else:
                        em.e("vector", lambda e, c=c, T=T, mx0=mx0:
                             e.tensor_tensor(
                                 mx0[:, 0:T], mx0[:, 0:T],
                                 arena[:, c * T:(c + 1) * T], ALU.max))
                        em.e("vector", lambda e, c=c, T=T, mx1=mx1:
                             e.tensor_tensor(
                                 mx1[:, 0:T], mx1[:, 0:T],
                                 arena[:, c * T:(c + 1) * T], ALU.min))
            return evs[-1]

        def emit_absmax_tail(em, T, clipmin, pst, psc, mcol, rq0, mx0, mx1):
            nb = T // 128
            em.w("vector", "V")
            mv = em.e("vector", lambda e, T=T, mx0=mx0, mx1=mx1:
                      e.scalar_tensor_tensor(
                          mx0[:, 0:T], mx1[:, 0:T], -1.0, mx0[:, 0:T],
                          ALU.mult, ALU.max))
            em.w("tensor", "V", mv)
            mt = 0
            for b in range(nb):
                mt = em.e("tensor", lambda e, b=b, pst=pst, mx0=mx0:
                          e.transpose(pst[:, b * 128:(b + 1) * 128],
                                      mx0[:, b * 128:(b + 1) * 128],
                                      identf[:]))
            em.w("vector", "T", mt)
            for b in range(nb):
                em.e("vector", lambda e, b=b, pst=pst, mcol=mcol:
                     e.tensor_reduce(
                         mca[:, mcol + b:mcol + b + 1],
                         pst[:, b * 128:(b + 1) * 128], AX_,
                         ALU.max, apply_absolute_value=True))
            em.w("vector", "V")
            em.e("vector", lambda e, nb=nb, clipmin=clipmin, mcol=mcol:
                 e.tensor_scalar(mca[:, mcol:mcol + nb],
                                 mca[:, mcol:mcol + nb], clipmin, None,
                                 ALU.max))
            em.w("vector", "V")
            em.e("vector", lambda e, nb=nb, mcol=mcol: e.reciprocal(
                mca[:, 8 + mcol:8 + mcol + nb], mca[:, mcol:mcol + nb]))
            em.w("vector", "V")
            mv = em.e("vector", lambda e, nb=nb, mcol=mcol: e.tensor_scalar(
                mca[:, 8 + mcol:8 + mcol + nb],
                mca[:, 8 + mcol:8 + mcol + nb], QB, None, ALU.mult))
            return emit_colbcast(em, mca, 8 + mcol, nb, mv, pst=pst, psc=psc,
                                 rq0=rq0)

        def emit_quant(em, arena, qdst, ntile, T, bcmark, psc=None,
                       qams=None):
            psc = psc if psc is not None else ps_bc
            qams = qams if qams is not None else (am1, am2)
            em.w("vector", "T", bcmark)
            em.w("vector", "V")
            clm = {}
            for c0 in range(0, ntile, 2):
                pair = [c0] + ([c0 + 1] if c0 + 1 < ntile else [])
                mm_ = {}
                rr_ = {}
                for c in pair:
                    if c - 2 in clm:
                        em.w("vector", "V", clm[c - 2])
                    mm_[c] = em.e(
                        "vector", lambda e, c=c, T=T, psc=psc,
                        q=qams[c0 % 2 if False else (c % 2)]:
                        e.tensor_tensor(
                            q[:, 0:T], arena[:, c * T:(c + 1) * T],
                            psc[:, 0:T], ALU.mult))
                for c in pair:
                    em.w("vector", "V", mm_[c])
                    rr_[c] = em.e(
                        "vector", lambda e, c=c, T=T, q=qams[c % 2]:
                        e.tensor_scalar(q[:, 0:T], q[:, 0:T], RC, RC,
                                        ALU.add, ALU.subtract))
                for c in pair:
                    em.w("vector", "V", rr_[c])
                    clm[c] = em.e(
                        "vector", lambda e, c=c, T=T, q=qams[c % 2]:
                        e.tensor_scalar(qdst[:, c * T:(c + 1) * T],
                                        q[:, 0:T], QB, -QB, ALU.min,
                                        ALU.max))
            return em.mark("V")

        def pa(em):
            T = TCA
            AXa = SLA
            ANa = SLB
            R2a = SLR
            R2v = SLR.bitcast(BF16)
            Qa = SLQ.bitcast(BF16)
            Q2a = R2v
            Ga = SLG.bitcast(BF16)
            wupb = SLW.bitcast(BF16)
            nb = T // 128
            dn = SLR[:, 8192:8192 + T]
            dng = SLR[:, 8192 + T:8192 + 2 * T]
            bbeta = SLR[:, 8192 + 2 * T:8192 + 3 * T]
            st = {"qm": 0, "qm2": 0, "lnap": 0, "bbm": 0, "gplast": 0,
                  "sillast": 0, "dncp": [0, 0], "rel2": 0, "ldm": 0,
                  "bcmt": 0}
            last_wup = [0, 0, 0, 0]
            last_wdn = [0, 0]
            x0 = 0 if multi else (n_half * L_own)

            seq = []
            for g in range(4):
                seq += [g, g + 16, g + 8, g + 24, g + 4, g + 20, g + 12,
                        g + 28]
            S_desc = (16, 8, 4, 2)
            GPLVL = 0
            lvl = {}
            sched = []
            for f in seq:
                lvl[f] = 0
                ops = []
                prog = True
                while prog:
                    prog = False
                    for li, s in enumerate(S_desc):
                        for c0 in range(FG):
                            c1 = c0 + s
                            if (c0 & s) == 0 and c1 < FG and \
                                    lvl.get(c0) == li and lvl.get(c1) == li:
                                ops.append((c0, c1, li))
                                lvl[c0] = li + 1
                                lvl[c1] = li + 1
                                prog = True
                sched.append(ops)
            assert all(v == len(S_desc) for v in lvl.values())

            def issue_ldx(em, k):
                t0 = k * T
                em.w("sync", "V", st["lnap"])
                if multi:
                    em.cnt["ldx"] += 16
                    if em.active == "sync":
                        for arm in em.engines["sync"].switch_core_id(n=128):
                            h = arm.logical % 2
                            o0 = h * L_own
                            em.engines["sync"].dma_start(
                                out=AXa[:, 0:DG * T].rearrange(
                                    "p (g t) -> p g t", g=DG),
                                in_=x_fm[:, o0 + t0:o0 + t0 + T].rearrange(
                                    "(g p) t -> p g t", p=128)).then_inc(
                                em.sems["ldx"], 16)
                    st["ldm"] = em.mark("ldx")
                else:
                    st["ldm"] = fm_load(AXa, x_fm[:, x0 + t0:x0 + t0 + T],
                                        DG, T, "ldx")

            def a_stats(em, k):
                st["bcmt"] = emit_ln_rows(em, AXa, T, st["ldm"], "ldx")

            def a_apply(em, k):
                em.w("vector", "T", st["bcmt"])
                for dg in range(DG):
                    em.e("vector", lambda e, dg=dg: e.tensor_tensor(
                        ANa[:, dg * T:(dg + 1) * T],
                        AXa[:, dg * T:(dg + 1) * T],
                        ps_bc[:, 0:T], ALU.mult))
                st["lnap"] = em.mark("V")
                em.w("vector", "V")
                for dg in range(DG):
                    em.e("vector", lambda e, dg=dg: e.tensor_tensor(
                        ANa[:, dg * T:(dg + 1) * T],
                        ANa[:, dg * T:(dg + 1) * T],
                        ps_bc[:, 512:512 + T], ALU.add))
                if "xn" in dbg and k == 0:
                    em.w("sync", "V")
                    fm_store(dbg["xn"], ANa, DG, T, "std")
                    em.barrier()

            def a_bfly(em, k):
                emit_bfly(em, ANa, DG, T)

            def a_h128(em, k):
                st["sm_a"] = emit_h128tree(em, ANa, DG, T, ps_t, ps_x,
                                           am3, am2)
                if "y2" in dbg and k == 0:
                    em.w("sync", "S")
                    fm_store(dbg["y2"], ANa, DG, T, "std")
                    em.barrier()

            def a_tail(em, k):
                em.w("vector", "G")
                bcm = emit_absmax_tail(em, T, EPS_Q * SQ_D, ps_t, ps_bc,
                                       0, 0, am3, am2)
                em.w("vector", "V")
                em.e("vector", lambda e: e.tensor_scalar(
                    acol1[:, 0:nb], mca[:, 0:nb], supc[:],
                    1.0 / (QB * SQ_D), ALU.mult, ALU.mult))
                st["qm"] = emit_quant(em, ANa, Qa, DG, T, bcm,
                                      qams=(mxb, am3))
                if "q1" in dbg and k == 0:
                    em.w("sync", "V")
                    fm_store(dbg["q1"], Qa, DG, T, "std")
                    em.barrier()

            def part_up(em, k):
                nxt = k + 1 < nch_a
                if nxt:
                    issue_ldx(em, k + 1)
                rel = []
                tl = {}
                tupv = tupQ.rearrange("p (g x) -> p g x", g=FG)
                for pos in range(FG):
                    fb = seq[pos]
                    j = pos % 4
                    wsl = wupb[:, j * 2048:(j + 1) * 2048]
                    if pos % 2 == 0:
                        assert seq[pos + 1] == fb + 16
                        q = (pos // 2) % 2
                        wsem = ("lwa", "lwb")[q]
                        wm = max(last_wup[2 * q], last_wup[2 * q + 1])
                        if q == 1:
                            wm = max(wm, last_wdn[0])
                        em.w("sync", "T", wm)
                        lw = em.dma(
                            "sync",
                            wupb[:, 2 * q * 2048:(2 * q + 2) * 2048]
                            .rearrange("p (g x) -> p g x", g=2),
                            tupv[:, fb:fb + 17:16, :], wsem)
                        em.w("tensor", wsem, lw)
                    if pos == 0:
                        em.w("tensor", "V", st["qm"])
                        em.w("tensor", "S", st["dncp"][0])
                    if pos == 1:
                        em.w("tensor", "S", st["dncp"][1])
                    if pos >= 2:
                        em.w("tensor", "S", rel[pos - 2][0])
                    pp = ps_mm if pos % 2 == 0 else ps_mm2
                    for dgk in range(DG):
                        mt = em.e("tensor",
                                  lambda e, dgk=dgk, pp=pp, w=wsl, T=T:
                                  e.matmul(
                                      pp[:, 0:T],
                                      w[:, dgk * 128:(dgk + 1) * 128],
                                      Qa[:, dgk * T:(dgk + 1) * T],
                                      start=(dgk == 0),
                                      stop=(dgk == DG - 1)))
                    last_wup[j] = mt
                    em.w("scalar", "T", mt)
                    if pos == 0:
                        em.w("scalar", "S")
                        em.w("scalar", "V", st["qm2"])
                        em.w("scalar", "G", st["gplast"])
                    amx = am1 if pos % 2 == 0 else am2
                    m1 = em.e("scalar", lambda e, pp=pp, amx=amx, T=T:
                              e.activation(amx[:, 0:T], pp[:, 0:T], AF.Relu))
                    em.w("scalar", "S", m1)
                    m2 = em.e("scalar", lambda e, fb=fb, amx=amx, T=T:
                              e.activation(R2a[:, fb * T:(fb + 1) * T],
                                           amx[:, 0:T], AF.Square))
                    rel.append((m1, m2))
                    for c0, c1, li in sched[pos]:
                        if li < GPLVL:
                            em.w("gpsimd", "S", m2)
                            em.w("gpsimd", "G")
                            em.e("gpsimd", lambda e, c0=c0, c1=c1, T=T:
                                 e.tensor_tensor(
                                     mxb[:, 0:T],
                                     R2a[:, c0 * T:(c0 + 1) * T],
                                     R2a[:, c1 * T:(c1 + 1) * T],
                                     ALU.subtract))
                            em.w("gpsimd", "G")
                            em.e("gpsimd", lambda e, c0=c0, c1=c1, T=T:
                                 e.tensor_tensor(
                                     R2a[:, c0 * T:(c0 + 1) * T],
                                     R2a[:, c0 * T:(c0 + 1) * T],
                                     R2a[:, c1 * T:(c1 + 1) * T], ALU.add))
                            em.w("gpsimd", "G")
                            gm = em.e("gpsimd",
                                      lambda e, c1=c1, T=T:
                                      e.tensor_copy(
                                          R2a[:, c1 * T:(c1 + 1) * T],
                                          mxb[:, 0:T]))
                            tl[c0] = tl[c1] = gm
                        else:
                            em.w("vector", "S", m2)
                            for t in (c0, c1):
                                if t in tl:
                                    em.w("vector", "G", tl.pop(t))
                            em.w("vector", "V")
                            em.e("vector", lambda e, c0=c0, c1=c1, T=T:
                                 e.tensor_tensor(
                                     R2a[:, c0 * T:(c0 + 1) * T],
                                     R2a[:, c0 * T:(c0 + 1) * T],
                                     R2a[:, c1 * T:(c1 + 1) * T], ALU.add))
                            em.w("vector", "V")
                            em.e("vector", lambda e, c0=c0, c1=c1, T=T:
                                 e.scalar_tensor_tensor(
                                     R2a[:, c1 * T:(c1 + 1) * T],
                                     R2a[:, c1 * T:(c1 + 1) * T], -2.0,
                                     R2a[:, c0 * T:(c0 + 1) * T], ALU.mult,
                                     ALU.add))
                    if nxt and pos == 7:
                        a_stats(em, k + 1)
                    if nxt and pos == 15:
                        a_apply(em, k + 1)
                st["rel2"] = rel[-1][1]

            def part_b(em, k):
                if "r2" in dbg and k == 0:
                    em.w("sync", "S", st["rel2"])
                    fm_store(dbg["r2"], R2a, FG, T, "std")
                    em.barrier()
                em.w("vector", "S", st["rel2"])
                em.w("vector", "G")
                emit_bfly(em, R2a, FG, T, steps=[1])
                emit_h128tree(em, R2a, FG, T, ps_y, ps_y2, mxa, mxb)
                bcm = emit_absmax_tail(em, T, 1e-30, ps_y, ps_y2, 4, 512,
                                       mxa, mxb)
                em.w("vector", "V")
                em.e("vector", lambda e: e.tensor_tensor(
                    bcol1[:, 0:nb], acol1[:, 0:nb], acol1[:, 0:nb],
                    ALU.mult))
                em.e("vector", lambda e: e.tensor_scalar(
                    coltmp[:, 0:1], sdnc[:], EPS_Q / QB, None, ALU.mult))
                em.w("vector", "V")
                em.e("vector", lambda e: e.tensor_tensor(
                    bcol1[:, 0:nb], bcol1[:, 0:nb], mca[:, 4:4 + nb],
                    ALU.mult))
                em.w("vector", "V")
                em.e("vector", lambda e: e.tensor_scalar(
                    bcol1[:, 0:nb], bcol1[:, 0:nb], sdnc[:],
                    1.0 / (QB * 64.0), ALU.mult, ALU.mult))
                em.w("vector", "V")
                for b in range(nb):
                    em.e("vector", lambda e, b=b: e.tensor_tensor(
                        bcol1[:, b:b + 1], bcol1[:, b:b + 1], coltmp[:, 0:1],
                        ALU.max))
                st["qm2"] = emit_quant(em, R2a, Q2a, FG, T, bcm,
                                       psc=ps_y2, qams=(am1, mxa))
                if "q2" in dbg and k == 0:
                    em.w("sync", "V")
                    fm_store(dbg["q2"], Q2a, FG, T, "std")
                    em.barrier()
                bmt = emit_colbcast(em, bcol1, 0, nb, em.mark("V"),
                                    pst=ps_y, psc=ps_y2, rq0=512)
                em.w("scalar", "T", bmt)
                em.w("scalar", "V", st["qm2"])
                em.w("scalar", "S")
                em.w("scalar", "G")
                st["bbm"] = em.e("scalar", lambda e: e.copy(
                    bbeta, ps_y2[:, 0:T]))

            def part_dn(em, k, mids=None):
                t0 = k * T
                cpy = [0] * DG
                gpm = [0] * DG
                sil = [0] * DG
                for eb in range(DG):
                    if mids and eb in mids:
                        mids[eb]()
                    wsl = SLW.bitcast(BF16)[:, 4096 + (eb % 2) * 4096:
                                            8192 + (eb % 2) * 4096]
                    wsem = "lwc" if eb % 2 == 0 else "lwd"
                    wm = last_wdn[eb % 2]
                    if eb == 0:
                        wm = max(wm, last_wup[2], last_wup[3])
                    em.w("sync", "T", wm)
                    lw = em.dma("sync", wsl,
                                tdownQ[:, eb * FFDIM:(eb + 1) * FFDIM],
                                wsem)
                    em.w("tensor", wsem, lw)
                    if eb == 0:
                        em.w("tensor", "V", st["qm2"])
                    if eb >= 2:
                        em.w("tensor", "S", cpy[eb - 2])
                    pp = ps_mm if eb % 2 == 0 else ps_mm2
                    for fgk in range(FG):
                        mt = em.e("tensor",
                                  lambda e, fgk=fgk, pp=pp, w=wsl, T=T:
                                  e.matmul(
                                      pp[:, 0:T],
                                      w[:, fgk * 128:(fgk + 1) * 128],
                                      Q2a[:, fgk * T:(fgk + 1) * T],
                                      start=(fgk == 0),
                                      stop=(fgk == FG - 1)))
                    last_wdn[eb % 2] = mt
                    em.w("scalar", "T", mt)
                    em.w("scalar", "S", st["bbm"])
                    if eb > 0:
                        em.w("scalar", "G", gpm[eb - 1])
                    else:
                        em.w("scalar", "G")
                        em.w("gpsimd", "V", st["qm2"])
                    cpy[eb] = em.e("scalar", lambda e, pp=pp, T=T: e.copy(
                        dn, pp[:, 0:T]))
                    em.w("gpsimd", "S", cpy[eb])
                    em.w("gpsimd", "G")
                    gpm[eb] = em.e("gpsimd", lambda e: e.tensor_tensor(
                        dng, dn, bbeta, ALU.mult))
                    em.w("scalar", "G", gpm[eb])
                    hslot = eb % 8
                    if hslot == 0:
                        em.w("scalar", "stg")
                    sil[eb] = em.e("scalar",
                                   lambda e, hslot=hslot, T=T: e.activation(
                                       Ga[:, hslot * T:(hslot + 1) * T],
                                       dng, AF.Silu))
                    if hslot == 7:
                        em.w("sync", "S", sil[eb])
                        half = eb // 8
                        em.dma("sync",
                               g_all[half * 1024:(half + 1) * 1024,
                                     t0:t0 + T].rearrange(
                                   "(g p) t -> p g t", p=128),
                               Ga[:, 0:8 * T].rearrange("p (g t) -> p g t",
                                                        g=8), "stg")
                st["gplast"] = gpm[-1]
                st["sillast"] = sil[-1]
                st["dncp"] = [cpy[DG - 2], cpy[DG - 1]]

            issue_ldx(em, 0)
            a_stats(em, 0)
            a_apply(em, 0)
            a_bfly(em, 0)
            a_h128(em, 0)
            a_tail(em, 0)
            for k in range(nch_a):
                part_up(em, k)
                part_b(em, k)
                kk = k + 1
                if kk < nch_a:
                    part_dn(em, k, mids={
                        2: (lambda kk=kk: a_bfly(em, kk)),
                        8: (lambda kk=kk: a_h128(em, kk)),
                        12: (lambda kk=kk: a_tail(em, kk))})
                else:
                    part_dn(em, k)
            em.barrier()

        def xnb_store(em, t0, T, XB, xbm):
            em.w("sync", "V", xbm)
            em.w("sync", "stx")
            if multi:
                em.cnt["stx"] += 16
                if em.active == "sync":
                    for arm in em.engines["sync"].switch_core_id(n=128):
                        h = arm.logical % 2
                        own0 = h * L_own
                        if own0 <= t0 < own0 + L_own:
                            dst = xnb_own[:, t0 - own0:t0 - own0 + T]
                        else:
                            dst = dsink[:, 0:T]
                        em.engines["sync"].dma_start(
                            out=dst.rearrange("(g p) t -> p g t", p=128),
                            in_=XB[:, 0:DG * T].rearrange(
                                "p (g t) -> p g t", g=DG)).then_inc(
                            em.sems["stx"], 16)
            else:
                own0 = n_half * L_own
                if own0 <= t0 < own0 + L_own:
                    fm_store(xnb_own[:, t0 - own0:t0 - own0 + T], XB, DG, T,
                             "stx")

        def pb(em):
            T = TCB
            AXa = SLA
            XB = SLQ.bitcast(BF16)
            SLRb = SLR.bitcast(BF16)
            SLBb = SLB.bitcast(BF16)
            DLT = [SLRb[:, 0:DG * T], SLRb[:, DG * 512:DG * 512 + DG * T]]
            UU = [SLRb[:, 2 * DG * 512:2 * DG * 512 + DG * T],
                  SLRb[:, 3 * DG * 512:3 * DG * 512 + DG * T]]
            BCB = [SLBb[:, 0:16 * T], SLBb[:, 16 * 512:16 * 512 + 16 * T]]
            DW = SLG.bitcast(BF16)
            sww = SLW.bitcast(BF16)
            dtl = sww[:, 0:512]
            yev = [sww[:, 512:1024], sww[:, 1024:1536]]
            scr = [sww[:, 1536:2048], sww[:, 2048:2560]]
            wbg = [sww[:, 2560 + i * 512:3072 + i * 512] for i in range(3)]
            bcrow1 = SLA.bitcast(BF16)
            em.w("vector", "V")
            em.e("vector", lambda e: e.memset(statec[:], 0.0))
            prep_marks = {}

            def prep(k):
                q = k % 2
                t0 = k * T
                em.w("sync", "V")
                em.w("sync", "T")
                em.w("sync", "S")
                ldm = fm_load(AXa, x_fm[:, t0:t0 + T], DG, T, "ldx")
                emit_ln_rows(em, AXa, T, ldm, "ldx")
                em.w("vector", "stx")
                mb0 = em.e("vector", lambda e, T=T: e.tensor_copy(
                    scr[0][:, 0:T], ps_bc[:, 0:T]))
                mb1 = em.e("vector", lambda e, T=T: e.tensor_copy(
                    scr[1][:, 0:T], ps_bc[:, 512:512 + T]))
                for dg in range(DG):
                    em.w("vector", "V")
                    em.e("vector", lambda e, dg=dg, T=T: e.tensor_copy(
                        XB[:, dg * T:(dg + 1) * T],
                        AXa[:, dg * T:(dg + 1) * T]))
                    em.w("vector", "V")
                    em.e("vector", lambda e, dg=dg, T=T: e.tensor_tensor(
                        XB[:, dg * T:(dg + 1) * T],
                        XB[:, dg * T:(dg + 1) * T], scr[0][:, 0:T],
                        ALU.mult))
                    em.w("vector", "V")
                    em.e("vector", lambda e, dg=dg, T=T: e.tensor_tensor(
                        XB[:, dg * T:(dg + 1) * T],
                        XB[:, dg * T:(dg + 1) * T], scr[1][:, 0:T],
                        ALU.add))
                xbm = em.mark("V")
                xnb_store(em, t0, T, XB, xbm)
                em.w("tensor", "V", xbm)
                for dg in range(DG):
                    em.e("tensor", lambda e, dg=dg, T=T: e.matmul(
                        ps_mm[:, 0:T], xprojTb[:, dg * 144:dg * 144 + 128],
                        XB[:, dg * T:(dg + 1) * T],
                        start=(dg == 0), stop=(dg == DG - 1)))
                for dg in range(DG):
                    mt = em.e("tensor", lambda e, dg=dg, T=T: e.matmul(
                        ps_t[0:16, 0:T],
                        xprojTb[:, dg * 144 + 128:dg * 144 + 144],
                        XB[:, dg * T:(dg + 1) * T],
                        start=(dg == 0), stop=(dg == DG - 1)))
                em.w("vector", "T", mt)
                em.e("vector", lambda e, T=T: e.tensor_copy(dtl[:, 0:T],
                                                            ps_mm[:, 0:T]))
                mv = em.e("vector", lambda e, T=T: e.tensor_copy(
                    bcs16[:, 0:T], ps_t[0:16, 0:T]))
                em.w("sync", "V", mv)
                em.w("sync", "ldb")
                stbm = em.dma("sync",
                              bcscr.rearrange("p (j t) -> (p j) t", j=16)[
                                  :, 0:T],
                              bcs16[:, 0:T], "stb")
                em.w("sync", "stb", stbm)
                ldbm = em.dma("sync",
                              bcrow1[0:1, 0:16 * T].rearrange(
                                  "p (j t) -> p j t", j=16),
                              bcscr.rearrange("p (j t) -> p j t", j=16)[
                                  0:1, :, 0:T], "ldb")
                em.w("tensor", "ldb", ldbm)
                em.w("scalar", "V")
                em.w("scalar", "S")
                em.w("scalar", "G")
                bcm = []
                for j in range(16):
                    if j >= 2:
                        em.w("tensor", "S", bcm[j - 2])
                    mt = em.e("tensor", lambda e, j=j, T=T: e.matmul(
                        ps_bc[:, (j % 2) * 512:(j % 2) * 512 + T],
                        onesb_r[0:1, :], bcrow1[0:1, j * T:(j + 1) * T],
                        start=True, stop=True))
                    em.w("scalar", "T", mt)
                    bcm.append(em.e("scalar", lambda e, j=j, q=q, T=T: e.copy(
                        BCB[q][:, j * T:(j + 1) * T],
                        ps_bc[:, (j % 2) * 512:(j % 2) * 512 + T])))
                dtm = []
                for dg in range(DG):
                    pp = ps_mm2 if dg % 2 == 0 else ps_mm
                    if dg >= 2:
                        em.w("tensor", "S", dtm[dg - 2])
                    mt = em.e("tensor", lambda e, dg=dg, pp=pp, T=T:
                              e.matmul(pp[:, 0:T],
                                       dtprojTb[:, dg * 128:(dg + 1) * 128],
                                       dtl[:, 0:T], start=True, stop=True))
                    em.w("scalar", "T", mt)
                    amx = am1 if dg % 2 == 0 else am2
                    m1 = em.e("scalar", lambda e, dg=dg, pp=pp, amx=amx, T=T:
                              e.activation(amx[:, 0:T], pp[:, 0:T], AF.Exp,
                                           bias=dtbias_s[:, dg:dg + 1]))
                    em.w("scalar", "S", m1)
                    dtm.append(em.e("scalar",
                                    lambda e, dg=dg, amx=amx, q=q, T=T:
                                    e.activation(
                                        DLT[q][:, dg * T:(dg + 1) * T],
                                        amx[:, 0:T], AF.Ln,
                                        bias=cst[:, 0:1])))
                em.w("vector", "S", dtm[-1])
                em.w("vector", "G")
                for dg in range(DG):
                    em.e("vector", lambda e, dg=dg, q=q, T=T: e.tensor_tensor(
                        UU[q][:, dg * T:(dg + 1) * T],
                        DLT[q][:, dg * T:(dg + 1) * T],
                        XB[:, dg * T:(dg + 1) * T], ALU.mult))
                prep_marks[k] = (em.mark("V"), bcm[-1], dtm[-1])

            def scan(k):
                q = k % 2
                t0 = k * T
                um, bce, dte = prep_marks[k]
                em.w("vector", "V", um)
                em.w("vector", "S", bce)
                em.w("gpsimd", "V")
                em.w("gpsimd", "S", bce)
                em.w("tensor", "S")
                S = DG * NH
                tts_m = {}
                mmv_m = {}
                w_m = {}
                weng = {}
                da_m = {}
                imm_m = {}
                ev_marks = {}
                ydma = {}

                def buf(base, s):
                    par = s % 2
                    return DW[:, base + par * 512:base + par * 512 + 512]

                def emit_da(s):
                    dg, j = divmod(s, NH)
                    dab = DW[:, (s % 2) * 1024:(s % 2) * 1024 + 512]
                    if s >= 2:
                        em.w("scalar", "V", tts_m[s - 2])
                    da_m[s] = em.e("scalar", lambda e, dg=dg, j=j, dab=dab,
                                   q=q, T=T:
                                   e.activation(
                                       dab[:, 0:T],
                                       DLT[q][:, dg * T:(dg + 1) * T],
                                       AF.Exp,
                                       scale=A_s[:, dg * NH + j:
                                                 dg * NH + j + 1]))

                def emit_w(s, eng):
                    dg, j = divmod(s, NH)
                    if j >= 5:
                        wb_ = wbg[j - 5]
                        if dg >= 1:
                            em.w(eng, "V", tts_m[(dg - 1) * NH + j])
                    else:
                        wb_ = DW[:, (s % 2) * 1024 + 512:
                                 (s % 2) * 1024 + 1024]
                        if s >= 2:
                            em.w(eng, "V", tts_m[s - 2])
                    weng[s] = eng
                    w_m[s] = em.e(eng, lambda e, dg=dg, j=j, wb_=wb_, q=q,
                                  T=T:
                                  e.tensor_tensor(
                                      wb_[:, 0:T],
                                      UU[q][:, dg * T:(dg + 1) * T],
                                      BCB[q][:, j * T:(j + 1) * T],
                                      ALU.mult))

                def emit_tts(s):
                    dg, j = divmod(s, NH)
                    dab = DW[:, (s % 2) * 1024:(s % 2) * 1024 + 512]
                    if j >= 5:
                        wb_ = wbg[j - 5]
                    else:
                        wb_ = DW[:, (s % 2) * 1024 + 512:
                                 (s % 2) * 1024 + 1024]
                    hb_ = DW[:, 2048 + (s % 2) * 512:
                             2048 + (s % 2) * 512 + 512]
                    em.w("vector", "S", da_m[s])
                    if s >= 2:
                        em.w("vector", "V", mmv_m[s - 2])
                    if weng[s] == "gpsimd":
                        em.w("vector", "G", w_m[s])
                    else:
                        em.w("vector", "V", w_m[s])
                    tts_m[s] = em.e("vector",
                                    lambda e, dab=dab, wb_=wb_, hb_=hb_,
                                    dg=dg, j=j, T=T:
                                    e.tensor_tensor_scan(
                                        hb_[:, 0:T], dab[:, 0:T],
                                        wb_[:, 0:T],
                                        statec[:, dg * NH + j:
                                               dg * NH + j + 1],
                                        ALU.mult, ALU.add))

                def emit_consume(s):
                    dg, j = divmod(s, NH)
                    hb_ = DW[:, 2048 + (s % 2) * 512:
                             2048 + (s % 2) * 512 + 512]
                    mb_ = DW[:, 3072 + (s % 2) * 512:
                             3072 + (s % 2) * 512 + 512]
                    em.w("vector", "V", tts_m[s])
                    em.e("vector", lambda e, hb_=hb_, dg=dg, j=j, T=T:
                         e.tensor_copy(
                             statec[:, dg * NH + j:dg * NH + j + 1],
                             hb_[:, T - 1:T]))
                    if s >= 2:
                        em.w("vector", "T", imm_m[s - 2])
                    mm = em.e("vector",
                              lambda e, hb_=hb_, mb_=mb_, j=j, q=q, T=T:
                              e.tensor_tensor(
                                  mb_[:, 0:T], hb_[:, 0:T],
                                  BCB[q][:, (8 + j) * T:(9 + j) * T],
                                  ALU.mult))
                    mmv_m[s] = mm
                    psy = ps_y if dg % 2 == 0 else ps_y2
                    em.w("tensor", "V", mm)
                    if j == 0 and dg >= 2:
                        em.w("tensor", "S", ev_marks[dg - 2])
                    imm_m[s] = em.e(
                        "tensor", lambda e, mb_=mb_, psy=psy, j=j, T=T:
                        e.matmul(psy[:, 0:T], identb[:], mb_[:, 0:T],
                                 start=(j == 0), stop=(j == NH - 1)))
                    if j == NH - 1:
                        ysem = "sty" if dg % 2 == 0 else "sty2"
                        em.w("scalar", "T", imm_m[s])
                        em.w("scalar", ysem)
                        ev_marks[dg] = em.e(
                            "scalar", lambda e, psy=psy, dg=dg, T=T:
                            e.copy(yev[dg % 2][:, 0:T], psy[:, 0:T]))
                        em.w("sync", "S", ev_marks[dg])
                        ydma[dg] = em.dma(
                            "sync",
                            y_all[dg * 128:(dg + 1) * 128, t0:t0 + T],
                            yev[dg % 2][:, 0:T], ysem)

                for s in range(S + 2):
                    dg, j = divmod(s, NH) if s < S else (None, None)
                    if s < S and j == 1:
                        em.w("gpsimd", "G")
                        for jj in range(5, NH):
                            emit_w(dg * NH + jj, "gpsimd")
                    if s < S:
                        emit_da(s)
                        if j < 5:
                            emit_w(s, "vector")
                    if 1 <= s < S + 1:
                        emit_tts(s - 1)
                    if s >= 2:
                        emit_consume(s - 2)

            prep(0)
            for k in range(nch_b):
                if k + 1 < nch_b:
                    prep(k + 1)
                scan(k)
            if multi:
                em.w("sync", "sty")
                em.w("sync", "sty2")
                em.cnt["styp"] += 16
                if em.active == "sync":
                    for arm in em.engines["sync"].switch_core_id(n=128):
                        h = arm.logical % 2
                        po = (1 - h) * L_own
                        em.engines["sync"].dma_start(
                            out=ysh[:, h * L_own:(h + 1) * L_own],
                            in_=y_all[:, po:po + L_own]).then_inc(
                            em.sems["styp"], 16)
                em.w("gpsimd", "styp")
                em.e("gpsimd", lambda e: e.collective_compute(
                    "AllReduce", ALU.add,
                    replica_groups=[[0, 1], [2, 3], [4, 5], [6, 7]],
                    ins=[bar_i], outs=[bar_o]))
            em.barrier()

        def pc(em):
            T = TCC
            AXa = SLA
            OUT = SLB
            YT = SLQ.bitcast(BF16)
            SLRb = SLR.bitcast(BF16)
            YP = SLRb[:, 0:DG * T]
            SS = SLRb[:, DG * 512:DG * 512 + DG * T]
            XBc = SLRb[:, 2 * DG * 512:2 * DG * 512 + DG * T]
            GG = SLRb[:, 3 * DG * 512:3 * DG * 512 + DG * T]
            wob = SLW.bitcast(BF16)
            last_wo = [0, 0, 0, 0]
            pst = {"om": 0, "ytm": 0, "mml": 0, "sttl": 0, "prodl": 0}
            wov = WoutTb.rearrange("p (g x) -> p g x", g=DG)
            for k in range(nch_c):
                t0 = k * T
                if k == 0:
                    em.w("sync", "V")
                    em.w("sync", "T")
                    em.w("sync", "S")
                em.w("sync", "V", pst["om"])
                if multi:
                    em.cnt["ldx"] += 16
                    if em.active == "sync":
                        for arm in em.engines["sync"].switch_core_id(n=128):
                            h = arm.logical % 2
                            o0 = h * L_own
                            em.engines["sync"].dma_start(
                                out=AXa[:, 0:DG * T].rearrange(
                                    "p (g t) -> p g t", g=DG),
                                in_=x_fm[:, o0 + t0:o0 + t0 + T].rearrange(
                                    "(g p) t -> p g t", p=128)).then_inc(
                                em.sems["ldx"], 16)
                    l1 = em.mark("ldx")
                else:
                    o0 = n_half * L_own
                    l1 = fm_load(AXa, x_fm[:, o0 + t0:o0 + t0 + T], DG, T,
                                 "ldx")
                em.w("sync", "V", pst["ytm"])
                em.w("sync", "T", pst["mml"])
                if multi:
                    em.cnt["ldx2"] += 32
                    if em.active == "sync":
                        for arm in em.engines["sync"].switch_core_id(n=128):
                            h = arm.logical % 2
                            o0 = h * L_own
                            em.engines["sync"].dma_start(
                                out=YP[:, 0:DG * T].rearrange(
                                    "p (g t) -> p g t", g=DG),
                                in_=ysh[:, (1 - h) * L_own + t0:
                                        (1 - h) * L_own + t0 + T].rearrange(
                                    "(g p) t -> p g t", p=128)).then_inc(
                                em.sems["ldx2"], 16)
                            em.engines["sync"].dma_start(
                                out=YT[:, 0:DG * T].rearrange(
                                    "p (g t) -> p g t", g=DG),
                                in_=y_all[:, o0 + t0:o0 + t0 + T].rearrange(
                                    "(g p) t -> p g t", p=128)).then_inc(
                                em.sems["ldx2"], 16)
                    l2 = em.mark("ldx2")
                else:
                    o0 = n_half * L_own
                    fm_load(YP, yprt[:, t0:t0 + T], DG, T, "ldx2")
                    l2 = fm_load(YT, y_all[:, o0 + t0:o0 + t0 + T], DG, T,
                                 "ldx2")
                em.w("sync", "V", pst["sttl"])
                fm_load(XBc, xnb_own[:, t0:t0 + T], DG, T, "ldc")
                em.w("sync", "V", pst["prodl"])
                l4 = fm_load(GG, g_all[:, t0:t0 + T], DG, T, "ldc")
                em.w("vector", "ldx2", l2)
                em.w("vector", "V")
                for dg in range(DG):
                    em.e("vector", lambda e, dg=dg, T=T: e.tensor_tensor(
                        YT[:, dg * T:(dg + 1) * T],
                        YT[:, dg * T:(dg + 1) * T],
                        YP[:, dg * T:(dg + 1) * T], ALU.add))
                ytm = em.mark("V")
                pst["ytm"] = ytm
                em.w("vector", "ldc", l4)
                sttm = 0
                for eb in range(DG):
                    wsl = wob[:, (eb % 4) * 2048:(eb % 4) * 2048 + 2048]
                    if eb % 2 == 0:
                        q = (eb // 2) % 2
                        wsem = "lwa" if q == 0 else "lwb"
                        em.w("sync", "T",
                             max(last_wo[2 * q], last_wo[2 * q + 1]))
                        lw = em.dma(
                            "sync",
                            wob[:, 2 * q * 2048:(2 * q + 2) * 2048]
                            .rearrange("p (g x) -> p g x", g=2),
                            wov[:, eb:eb + 2, :], wsem)
                        em.w("tensor", wsem, lw)
                    if eb == 0:
                        em.w("tensor", "V", ytm)
                    pp = ps_mm if eb % 2 == 0 else ps_mm2
                    for dgk in range(DG):
                        mt = em.e("tensor",
                                  lambda e, dgk=dgk, pp=pp, w=wsl, T=T:
                                  e.matmul(
                                      pp[:, 0:T],
                                      w[:, dgk * 128:(dgk + 1) * 128],
                                      YT[:, dgk * T:(dgk + 1) * T],
                                      start=(dgk == 0),
                                      stop=(dgk == DG - 1)))
                    last_wo[eb % 4] = mt
                    em.w("scalar", "T", mt)
                    amx = am1 if eb % 2 == 0 else am2
                    ms = em.e("scalar", lambda e, pp=pp, amx=amx, eb=eb, T=T:
                              e.activation(amx[:, 0:T], pp[:, 0:T], AF.Relu,
                                           bias=obias_s[:, eb:eb + 1]))
                    em.w("vector", "S", ms)
                    mv = em.e("vector", lambda e, eb=eb, amx=amx, T=T:
                              e.scalar_tensor_tensor(
                                  SS[:, eb * T:(eb + 1) * T],
                                  XBc[:, eb * T:(eb + 1) * T],
                                  D_s[:, eb:eb + 1], amx[:, 0:T],
                                  ALU.mult, ALU.add))
                    sttm = mv
                    em.w("tensor", "V", mv)
                    em.w("scalar", "V", mv)
                pst["mml"] = last_wo[(DG - 1) % 4]
                pst["sttl"] = sttm
                if "s2" in dbg and k == 0:
                    em.w("sync", "V")
                    fm_store(dbg["s2"], SS, DG, T, "std")
                    em.barrier()
                em.w("vector", "V")
                em.w("vector", "ldx", l1)
                em.w("vector", "sto")
                for dg in range(DG):
                    pm = em.e("vector", lambda e, dg=dg, T=T: e.tensor_tensor(
                        am3[:, 0:T], SS[:, dg * T:(dg + 1) * T],
                        GG[:, dg * T:(dg + 1) * T], ALU.mult))
                    if dg == DG - 1:
                        pst["prodl"] = pm
                    em.w("vector", "V")
                    em.e("vector", lambda e, dg=dg, T=T: e.tensor_tensor(
                        OUT[:, dg * T:(dg + 1) * T],
                        AXa[:, dg * T:(dg + 1) * T], am3[:, 0:T], ALU.add))
                    em.w("vector", "V")
                om = em.mark("V")
                pst["om"] = om
                em.w("sync", "V", om)
                em.w("sync", "sto")
                fm_store(out_fm[:, t0:t0 + T], OUT, DG, T, "sto")
            em.w("sync", "sto")
            em.w("sync", "std")
            em.barrier()

        def prog(em):
            p0(em)
            pa(em)
            pb(em)
            pc(em)

        block = ctx.enter_context(nc.Block())
        em.run(block, prog)

    return nc



def _hadamard128():
    H = np.array([[1.0]], dtype=np.float32)
    while H.shape[0] < 128:
        H = np.block([[H, H], [H, -H]])
    return np.ascontiguousarray(H.astype(np.float32))


def host_inputs(x_fm, xproj, dtproj, dtbias, A_log, Dv, Wout, obias, w_up,
                w_down, half):
    d = {}
    d["x_fm"] = np.ascontiguousarray(x_fm)
    WT = xproj.T
    d["xprojT"] = np.ascontiguousarray(np.concatenate(
        [WT[:, 0:128], WT[:, 128 + 8 * half:136 + 8 * half],
         WT[:, 144 + 8 * half:152 + 8 * half]], axis=1))
    d["dtprojT"] = np.ascontiguousarray(dtproj.T)
    d["dtbias_c"] = np.ascontiguousarray(dtbias.reshape(DG, 128).T)
    d["A_cols"] = np.ascontiguousarray(
        A_log[:, 8 * half:8 * half + 8].reshape(DG, 128, NH)
        .transpose(1, 0, 2).reshape(128, DG * NH))
    d["D_cols"] = np.ascontiguousarray(Dv.reshape(DG, 128).T)
    d["WoutT"] = np.ascontiguousarray(Wout.T)
    d["obias_c"] = np.ascontiguousarray(obias.reshape(DG, 128).T)
    d["wupT"] = np.ascontiguousarray(w_up.T)
    d["wdownT"] = np.ascontiguousarray(w_down.T)
    d["H128"] = _hadamard128()
    d["ident"] = np.ascontiguousarray(np.eye(128, dtype=np.float32))
    d["ones_r"] = np.ones((1, 128), np.float32)
    return d


_nc_cache = {}


def kernel(L0_attn_hidden_states, L0_attn_x_proj_weight,
           L0_attn_dt_proj_weight, L0_attn_dt_proj_bias, L0_attn_A_log,
           L0_attn_D, L0_attn_out_proj_weight, L0_attn_out_proj_bias,
           L0_ffn_w_up, L0_ffn_w_down):
    x = np.asarray(L0_attn_hidden_states)
    B, L, _ = x.shape
    L_own = L // 2
    nc = _nc_cache.get((L, L_own))
    if nc is None:
        nc = build_program(L_all=L, L_own=L_own, TCA=512, TCB=512, TCC=512,
                           n_half=None)
        _nc_cache[(L, L_own)] = nc
    in_maps = []
    for c in range(8):
        b, h = c // 2, c % 2
        in_maps.append(host_inputs(
            x[b].T, np.asarray(L0_attn_x_proj_weight),
            np.asarray(L0_attn_dt_proj_weight),
            np.asarray(L0_attn_dt_proj_bias),
            np.asarray(L0_attn_A_log), np.asarray(L0_attn_D),
            np.asarray(L0_attn_out_proj_weight),
            np.asarray(L0_attn_out_proj_bias),
            np.asarray(L0_ffn_w_up), np.asarray(L0_ffn_w_down), half=h))
    res = run_bass_kernel_spmd(nc, in_maps, core_ids=list(range(8)))
    out = np.empty((B, L, D), np.float32)
    for c in range(8):
        b, h = c // 2, c % 2
        out[b, h * L_own:(h + 1) * L_own, :] = res.results[c]["out_fm"].T
    return out
